# version 06
# baseline (speedup 1.0000x reference)
import numpy as np
import ml_dtypes
import concourse.bass as bass
import concourse.mybir as mybir
from concourse.bass_utils import run_bass_kernel_spmd

F32 = mybir.dt.float32
BF16 = mybir.dt.bfloat16
AF = mybir.ActivationFunctionType
ALU = mybir.AluOpType
AX = mybir.AxisListType
bf16 = ml_dtypes.bfloat16

NB = 2
T = 2048
D = 1024
KD = D // 128
CW = 512
NC_PER_B = T // CW
EPS = 1e-5

ENGINES = ("sync", "tensor", "vector", "scalar", "gpsimd")


DMA_SLOTS = 4


class Ctx:
    def __init__(self, nc):
        self.nc = nc
        self.code = {e: [] for e in ENGINES}
        self._sems = {}
        self.ctr = {}
        self.consts = {}
        self.dma_ctr = {}
        self.hwm = {}
        self._uid = 0

    def uid(self):
        self._uid += 1
        return self._uid

    def sem(self, name):
        if name not in self._sems:
            self._sems[name] = self.nc.alloc_semaphore(name="sem_" + name)
            self.ctr[name] = 0
        return self._sems[name]

    def wait(self, eng, name, val=None):
        if isinstance(val, tuple):
            name, val = val
        elif isinstance(name, tuple):
            name, val = name
        if val is None or val <= 0:
            return
        hw = self.hwm.setdefault(eng, {})
        if hw.get(name, -1) >= val:
            return
        hw[name] = val
        h = self.sem(name)
        self.code[eng].append(lambda e, h=h, v=val: e.wait_ge(h, v))

    def join(self, eng, sems):
        for s in sems:
            variants = [v for v in (s, s + "g") if v in self.dma_ctr]
            if variants:
                for v in variants:
                    for i in range(DMA_SLOTS):
                        self.wait(eng, f"{v}{i}", self.ctr.get(f"{v}{i}", 0))
            else:
                self.wait(eng, s, self.ctr.get(s, 0))

    def barrier(self, engs=("tensor", "vector", "scalar"),
                sems=("T", "V", "S")):
        for e in engs:
            self.join(e, [s for s in sems])

    def op(self, eng, fn, inc=None, n=1, lp=False):
        tgt = None
        nc = self.nc
        if lp:
            fn0 = fn

            def fn(e, fn0=fn0):
                with nc.allow_low_precision(reason="bf16 matmul-operand cast"):
                    return fn0(e)
        if inc is not None:
            h = self.sem(inc)
            self.ctr[inc] += n
            tgt = self.ctr[inc]

            def go(e, fn=fn, h=h, n=n):
                fn(e).then_inc(h, n)
        else:
            def go(e, fn=fn):
                fn(e)
        self.code[eng].append(go)
        return tgt

    def fop(self, eng, fn, lp=False):
        sem = {"vector": "V", "scalar": "S", "gpsimd": "G", "tensor": "T"}[eng]
        v = self.op(eng, fn, sem, lp=lp)
        self.wait(eng, sem, v)
        return v

    def dma(self, q, dst, src, stream):
        stream = f"{stream}{'g' if q == 'gpsimd' else ''}"
        i = self.dma_ctr.get(stream, 0)
        self.dma_ctr[stream] = i + 1
        sub = f"{stream}{i % DMA_SLOTS}"
        self.wait(q, sub, self.ctr.get(sub, 0))
        val = self.op(q, lambda e: e.dma_start(dst, src), sub, 16)
        return (sub, val)

    def sb(self, name, shape, dtype):
        if getattr(self, "_es", None) is not None:
            return self._es.enter_context(
                self.nc.sbuf_tensor(f"{name}_{self.uid()}", shape, dtype))
        return self.nc.alloc_sbuf_tensor(f"{name}_{self.uid()}", shape, dtype)

    def ps(self, name, shape):
        if getattr(self, "_es", None) is not None:
            return self._es.enter_context(
                self.nc.psum_tensor(f"{name}_{self.uid()}", shape, F32))
        return self.nc.alloc_psum_tensor(f"{name}_{self.uid()}", shape, F32)

    def full_barrier(self):
        sems = list(self.ctr.keys())
        for eng in ENGINES:
            for s in sems:
                self.wait(eng, s, self.ctr.get(s, 0))

    def fconst(self, val):
        key = f"f_{val!r}"
        if key not in self.consts:
            es = getattr(self, "_es", None)
            self._es = None
            t = self.sb("fc", [128, 1], F32)
            self._es = es
            arr = np.full((128, 1), val, dtype=np.float32)
            self.dma("gpsimd", t[:, :], self.const(key, arr), "W")
            self.consts[key] = ("sb", t)
        return self.consts[key][1]

    def const(self, name, arr):
        key = "c_" + name
        if key not in self.consts:
            self.consts[key] = self.nc.inline_tensor(arr, name=key).ap()
        return self.consts[key]

    def finalize(self):
        with self.nc.Block() as block:
            @block.sync
            def _(e):
                for fn in self.code["sync"]:
                    fn(e)

            @block.tensor
            def _(e):
                for fn in self.code["tensor"]:
                    fn(e)

            @block.vector
            def _(e):
                for fn in self.code["vector"]:
                    fn(e)

            @block.scalar
            def _(e):
                for fn in self.code["scalar"]:
                    fn(e)

            @block.gpsimd
            def _(e):
                for fn in self.code["gpsimd"]:
                    fn(e)
        return self.nc



def load_common(ctx):
    c = {}
    c["ident"] = ctx.sb("ident", [128, 128], BF16)
    c["ones_row"] = ctx.sb("ones_row", [128, 128], BF16)
    c["ones_row_f"] = ctx.sb("ones_row_f", [1, 128], F32)
    c["ones_col"] = ctx.sb("ones_col", [128, 1], BF16)
    c["ident_f"] = ctx.sb("ident_f", [128, 128], F32)
    ctx.dma("gpsimd", c["ident_f"][:, :], ctx.const("identf", np.eye(128, dtype=np.float32)), "W")
    ctx.dma("gpsimd", c["ident"][:, :], ctx.const("ident", np.eye(128).astype(bf16)), "W")
    ctx.dma("gpsimd", c["ones_row"][:, :], ctx.const("onesrow", np.ones((128, 128), dtype=bf16)), "W")
    c["shift0"] = ctx.sb("shift0", [64, 128], BF16)
    c["shift1"] = ctx.sb("shift1", [64, 128], BF16)
    sh0 = np.zeros((64, 128), dtype=bf16); sh0[np.arange(64), np.arange(64)] = 1
    sh1 = np.zeros((64, 128), dtype=bf16); sh1[np.arange(64), 64 + np.arange(64)] = 1
    ctx.dma("gpsimd", c["shift0"][:, :], ctx.const("shift0", sh0), "W")
    ctx.dma("gpsimd", c["shift1"][:, :], ctx.const("shift1", sh1), "W")
    ctx.dma("gpsimd", c["ones_row_f"][:, :], ctx.const("onesrowf", np.ones((1, 128), dtype=np.float32)), "W")
    c["w_done"] = ctx.dma("gpsimd", c["ones_col"][:, :], ctx.const("onescol", np.ones((128, 1), dtype=bf16)), "W")
    c["ln_xb"] = ctx.sb("ln_xb", [128, KD // 2, CW], BF16)
    c["ln_xq"] = ctx.sb("ln_xq", [128, KD // 2, CW], BF16)
    c["ln_nm"] = ctx.sb("ln_nm", [1, CW], F32)
    c["ln_var"] = ctx.sb("ln_var", [1, CW], F32)
    c["ln_t2"] = ctx.sb("ln_t2", [1, CW], F32)
    c["ln_rstd"] = ctx.sb("ln_rstd", [1, CW], F32)
    c["ln_tmp"] = ctx.sb("ln_tmp", [128, CW], F32)
    c["one_c"] = ctx.fconst(1.0)
    ctx.fconst(EPS)
    ctx.fconst(1e-24)
    for eng in ("tensor", "vector", "scalar", "sync"):
        ctx.join(eng, ["W"])
    return c


def emit_ln_chunk(ctx, com, xf, xln, psB1, psB2, width=CW, gain=None,
                  xf_ready=None, src_view=None, xf_guard=("S", "V"),
                  out_f32=None):
    if src_view is not None:
        for s in xf_guard:
            ctx.wait("sync", s, ctx.ctr.get(s, 0))
        xf_ready = ctx.dma("sync", xf[:, :, :], src_view, "A")
    if xf_ready is not None:
        ctx.wait("scalar", "A", xf_ready)
        ctx.wait("vector", "A", xf_ready)
    xb, xq = com["ln_xb"], com["ln_xq"]
    ctx.join("tensor", ["V"])
    H = KD // 2
    for r in range(2):
        if r == 1:
            ctx.wait("scalar", "T", ctx.ctr["T"])
        ctx.op("scalar", lambda e, r=r: e.activation(
            xb[:, :, 0:width], xf[:, r * H:(r + 1) * H, :], AF.Identity))
        s_done = ctx.op("scalar", lambda e, r=r: e.activation(
            xq[:, :, 0:width], xf[:, r * H:(r + 1) * H, :], AF.Square), "S")
        ctx.wait("tensor", "S", s_done)
        for k in range(H):
            ctx.op("tensor", lambda e, k=k, r=r: e.matmul(
                psB1[0:1, 0:width], com["ones_col"][:, 0:1], xb[:, k, 0:width],
                start=(r == 0 and k == 0), stop=(r == 1 and k == H - 1)))
        for k in range(H):
            t_done = ctx.op("tensor", lambda e, k=k, r=r: e.matmul(
                psB2[0:1, 0:width], com["ones_col"][:, 0:1], xq[:, k, 0:width],
                start=(r == 0 and k == 0), stop=(r == 1 and k == H - 1)), "T")
    ctx.wait("vector", "T", t_done)
    nm, var, msq = com["ln_nm"], com["ln_var"], com["ln_t2"]
    lnv, rstd, tmp = com["ln_t2"], com["ln_rstd"], com["ln_tmp"]
    ctx.op("vector", lambda e: e.tensor_scalar(nm[:, 0:width], psB1[0:1, 0:width],
                                               -1.0 / D, None, op0=ALU.mult))
    ctx.op("vector", lambda e: e.tensor_mul(msq[:, 0:width], nm[:, 0:width],
                                            nm[:, 0:width]))
    v_done = ctx.op("vector", lambda e: e.scalar_tensor_tensor(
        var[:, 0:width], psB2[0:1, 0:width], 1.0 / D, msq[:, 0:width],
        op0=ALU.mult, op1=ALU.subtract), "V")
    ctx.wait("scalar", "V", v_done)
    epsc = ctx.fconst(EPS)
    ctx.op("scalar", lambda e: e.activation(lnv[:, 0:width], var[:, 0:width],
                                            AF.Ln, bias=epsc[0:1, 0:1]))
    s2 = ctx.op("scalar", lambda e: e.activation(rstd[:, 0:width],
                                                 lnv[:, 0:width], AF.Exp,
                                                 scale=-0.5), "S")
    ctx.wait("tensor", "S", s2)
    ctx.join("tensor", ["V"])
    for j in range((width + 511) // 512):
        w = min(512, width - j * 512)
        ctx.op("tensor", lambda e, j=j, w=w: e.matmul(
            psB1[:, j * 512:j * 512 + w], com["ones_row_f"][0:1, :],
            nm[0:1, j * 512:j * 512 + w], start=True, stop=True))
        t2 = ctx.op("tensor", lambda e, j=j, w=w: e.matmul(
            psB2[:, j * 512:j * 512 + w], com["ones_row_f"][0:1, :],
            rstd[0:1, j * 512:j * 512 + w], start=True, stop=True), "T")
    ctx.wait("vector", "T", t2)
    out_val, out_sem = None, "V"
    for k in range(KD):
        ctx.op("vector", lambda e, k=k: e.tensor_tensor(
            tmp[:, 0:width], xf[:, k, :], psB1[:, 0:width], op=ALU.add))
        if out_f32 is not None:
            out_val = ctx.op("vector", lambda e, k=k: e.tensor_tensor(
                out_f32[:, k, 0:width], tmp[:, 0:width], psB2[:, 0:width],
                op=ALU.mult), "V")
        elif gain is None:
            out_val = ctx.op("vector", lambda e, k=k: e.tensor_tensor(
                xln[:, k, 0:width], tmp[:, 0:width], psB2[:, 0:width],
                op=ALU.mult), "V")
        else:
            gcol, bcol = gain
            ctx.op("vector", lambda e, k=k: e.scalar_tensor_tensor(
                tmp[:, 0:width], tmp[:, 0:width], gcol(k), psB2[:, 0:width],
                op0=ALU.mult, op1=ALU.mult), "V")
            ctx.join("scalar", ["V"])
            out_val = ctx.op("scalar", lambda e, k=k: e.activation(
                xln[:, k, 0:width], tmp[:, 0:width], AF.Identity,
                bias=bcol(k)), "S")
            out_sem = "S"
    return out_val, out_sem


def xview(xdram, b, t0, width):
    return xdram.rearrange("b (k p) t -> b p k t", p=128)[b, :, :, t0:t0 + width]



def pass_l0aa(ctx, com, xin, xout, wd):
    nc = ctx.nc
    wqk = ctx.sb("wqk", [128, KD, 2048], BF16)
    wv = ctx.sb("wv", [128, KD, 1024], BF16)
    bqkc = ctx.sb("bqkc", [128, 16], F32)
    bqks = ctx.sb("bqks", [128, 16], F32)
    bvr = ctx.sb("bvr", [1, 1024], F32)
    bvrs = bvr
    bvb = ctx.sb("bvb", [128, 1024], F32)
    ctx.dma("gpsimd", wqk[:, :, :], wd["Wqk"].rearrange("(k p) f -> p k f", p=128), "W")
    ctx.dma("gpsimd", wv[:, :, :], wd["Wv"].rearrange("(k p) f -> p k f", p=128), "W")
    ctx.dma("gpsimd", bqkc[:, :], wd["bqk_c"], "W")
    ctx.dma("gpsimd", bvr[:, :], wd["bv_r"], "W")
    ctx.join("scalar", ["W"])
    ctx.join("tensor", ["W"])
    ctx.op("scalar", lambda e: e.activation(bqks[:, :], bqkc[:, :], AF.Identity,
                                            scale=1.0 / 256.0))
    s0 = ctx.op("scalar", lambda e: e.activation(bvrs[:, :], bvr[:, :],
                                                 AF.Identity, scale=1.0 / 32.0), "S")
    ctx.wait("tensor", "S", s0)

    psA = ctx.ps("psA", [128, 1024])
    psB = ctx.ps("psB", [128, 512])
    psC = ctx.ps("psC", [128, 512])
    psC2 = ctx.ps("psC2", [128, 512])
    psD = ctx.ps("psD", [128, 512])
    psE = ctx.ps("psE", [128, 512])

    for j in range(2):
        t0_ = ctx.op("tensor", lambda e, j=j: e.matmul(
            psA[:, j * 512:(j + 1) * 512], com["ones_row_f"][0:1, :],
            bvrs[0:1, j * 512:(j + 1) * 512], start=True, stop=True), "T")
    ctx.wait("vector", "T", t0_)
    ctx.op("vector", lambda e: e.tensor_copy(bvb[:, :], psA[:, 0:1024]), "V")

    qb = ctx.sb("qb", [128, 8, T], BF16)
    kb = ctx.sb("kb", [128, 8, T], BF16)
    vtm = ctx.sb("vtm", [128, 16, 16, 65], BF16)
    Etb = ctx.sb("Etb", [128, 2, 512], BF16)
    xf = ctx.sb("xf_aa", [128, KD, CW], F32)
    xln = ctx.sb("xln_aa", [128, KD, CW], BF16)
    negm = ctx.sb("negm", [128, 2, 4], F32)
    mx2 = ctx.sb("mx2", [128, 2], F32)
    negmc = ctx.sb("negmc", [128, 2], BF16)
    negrow = ctx.sb("negrow", [1, 2, 512], BF16)
    rrowb = ctx.sb("rrowb", [65, 512], BF16)
    attnh = ctx.sb("attnh", [64, 512], BF16)
    attv = ctx.sb("attv", [64, 512], BF16)
    psE2 = ctx.ps("psE2", [128, 512])

    for b in range(NB):
        ctx.barrier()
        ones_done = ctx.op("vector", lambda e: e.memset(vtm[:, :, :, 64:65], 1.0), "V")
        ctx.wait("tensor", "V", ones_done)
        qk_ev = {}
        for c in range(NC_PER_B):
            t0c = c * CW
            lnv, lne = emit_ln_chunk(ctx, com, xf, xln, psA[:, 0:512],
                                     psA[:, 512:1024],
                                     src_view=xview(xin, b, t0c, CW))
            ctx.wait("tensor", lne, lnv)
            for ft in range(16):
                ps = psB if ft % 2 == 0 else psC
                if ft >= 2:
                    ctx.wait("tensor", "S", qk_ev[ft - 2])
                for k in range(KD):
                    mm = ctx.op("tensor", lambda e, ft=ft, k=k, ps=ps: e.matmul(
                        ps[:, :], wqk[:, k, ft * 128:(ft + 1) * 128],
                        xln[:, k, :], start=(k == 0), stop=(k == KD - 1)),
                        "T" if k == KD - 1 else None)
                ctx.wait("scalar", "T", mm)
                dstb = qb if ft < 8 else kb
                dstf = ft if ft < 8 else ft - 8
                qk_ev[ft] = ctx.op("scalar", lambda e, dstb=dstb, dstf=dstf,
                                   t0c=t0c, ft=ft, ps=ps:
                                   e.activation(dstb[:, dstf, t0c:t0c + CW],
                                                ps[:, :], AF.Identity,
                                                scale=1.0 / 256.0,
                                                bias=bqks[:, ft:ft + 1]), "S")
            v_ev = None
            for tt in range(4):
                gtt = c * 4 + tt
                for half in range(2):
                    if v_ev is not None:
                        ctx.wait("tensor", "V", v_ev)
                    for k in range(KD):
                        mm = ctx.op("tensor",
                                    lambda e, tt=tt, half=half, k=k: e.matmul(
                                        psD[:, :],
                                        xln[:, k, tt * 128:(tt + 1) * 128],
                                        wv[:, k, half * 512:(half + 1) * 512],
                                        start=(k == 0), stop=(k == KD - 1)),
                                    "T" if k == KD - 1 else None)
                    ctx.wait("vector", "T", mm)
                    v_ev = ctx.op("vector", lambda e, gtt=gtt, half=half:
                                  e.scalar_tensor_tensor(
                                      vtm[:, gtt, half * 8:(half + 1) * 8, 0:64],
                                      psD[:, :].rearrange("p (h d) -> p h d", d=64),
                                      1.0 / 32.0,
                                      bvb[:, half * 512:(half + 1) * 512]
                                      .rearrange("p (h d) -> p h d", d=64),
                                      op0=ALU.mult, op1=ALU.add), "V", lp=True)
        ctx.barrier()
        for c in range(NC_PER_B):
            t0c = c * CW
            ctx.join("sync", ["V", "O"])
            a_done = ctx.dma("sync", xf[:, :, :], xview(xin, b, t0c, CW), "A")
            ctx.wait("vector", "A", a_done)
            se_tok = {}

            def P1(h):
                j, p0 = h // 2, (h % 2) * 64
                par = h % 2
                psm = psB if par == 0 else psE
                red_done = None
                for qt in range(4):
                    qoff = t0c + qt * 128
                    for kc in range(4):
                        ps1 = psA[:, (kc % 2) * 512:(kc % 2) * 512 + 512]
                        if red_done is not None:
                            ctx.wait("tensor", "V", red_done - 1)
                        mm = ctx.op("tensor", lambda e, j=j, p0=p0, qoff=qoff,
                                    kc=kc, ps1=ps1: e.matmul(
                                        ps1, qb[p0:p0 + 64, j, qoff:qoff + 128],
                                        kb[p0:p0 + 64, j, kc * 512:(kc + 1) * 512],
                                        start=True, stop=True), "T")
                        ctx.wait("vector", "T", mm)
                        if kc == 0:
                            red_done = ctx.fop("vector", lambda e, qt=qt, par=par,
                                               ps1=ps1: e.reduce_max(
                                                   negm[:, par, qt:qt + 1], ps1,
                                                   axis=AX.X, negate=True))
                        else:
                            ctx.fop("vector", lambda e, par=par, ps1=ps1:
                                    e.reduce_max(mx2[:, par:par + 1], ps1,
                                                 axis=AX.X, negate=True))
                            red_done = ctx.op("vector", lambda e, qt=qt, par=par:
                                              e.tensor_tensor(
                                                  negm[:, par, qt:qt + 1],
                                                  negm[:, par, qt:qt + 1],
                                                  mx2[:, par:par + 1],
                                                  op=ALU.min), "V")
                    vv = ctx.op("vector", lambda e, qt=qt, par=par: e.tensor_copy(
                        negmc[:, par:par + 1], negm[:, par, qt:qt + 1]), "V")
                    ctx.wait("tensor", "V", vv)
                    tt_ = ctx.op("tensor", lambda e, qt=qt, par=par, psm=psm:
                                 e.matmul(
                                     psm[0:1, qt * 128:(qt + 1) * 128],
                                     negmc[:, par:par + 1],
                                     com["ident"][:, :], start=True, stop=True), "T")
                ctx.wait("scalar", "T", tt_)
                se_tok[h] = ctx.op("scalar", lambda e, par=par, psm=psm: e.copy(
                    negrow[0:1, par, :], psm[0:1, 0:512]), "S")

            def P2(h):
                j, p0 = h // 2, (h % 2) * 64
                par = h % 2
                ctx.wait("tensor", "S", se_tok[h])
                ev = {}
                av = {}
                for kt in range(16):
                    ps2 = psC if kt % 2 == 0 else psC2
                    if kt >= 2:
                        ctx.wait("tensor", "S", ev[kt - 2])
                    if kt >= 1:
                        ctx.wait("tensor", "S", ev[kt - 1])
                        av[kt - 1] = ctx.op("tensor", lambda e, h=h, kt=kt: e.matmul(
                            psD[0:65, :], vtm[:, kt - 1, h, :], Etb[:, (kt - 1) % 2, :],
                            start=(kt - 1 == 0), stop=False), "T")
                    ctx.op("tensor", lambda e, ps2=ps2, par=par: e.matmul(
                        ps2[:, :], com["ones_row"][0:1, :], negrow[0:1, par, :],
                        start=True, stop=False))
                    mm = ctx.op("tensor", lambda e, j=j, p0=p0, kt=kt, ps2=ps2,
                                t0c=t0c: e.matmul(
                                    ps2[:, :], kb[p0:p0 + 64, j, kt * 128:(kt + 1) * 128],
                                    qb[p0:p0 + 64, j, t0c:t0c + CW],
                                    start=False, stop=True), "T")
                    ctx.wait("scalar", "T", mm)
                    ev[kt] = ctx.op("scalar", lambda e, kt=kt, ps2=ps2: e.activation(
                        Etb[:, kt % 2, :], ps2[:, :], AF.Exp), "S")
                ctx.wait("tensor", "S", ev[15])
                av[15] = ctx.op("tensor", lambda e, h=h: e.matmul(
                    psD[0:65, :], vtm[:, 15, h, :], Etb[:, 15 % 2, :],
                    start=False, stop=True), "T")
                ctx.wait("vector", "T", av[15])
                rv = ctx.op("vector", lambda e: e.reciprocal(rrowb[64:65, :],
                                                             psD[64:65, :]), "V",
                            lp=True)
                ctx.wait("tensor", "V", rv)
                tb = ctx.op("tensor", lambda e: e.matmul(
                    psC[0:64, :], com["ones_row"][64:65, 0:64], rrowb[64:65, :],
                    start=True, stop=True), "T")
                ctx.wait("scalar", "T", tb)
                sv = ctx.op("scalar", lambda e: e.copy(attnh[0:64, :],
                                                       psD[0:64, :]), "S")
                ctx.wait("vector", "S", sv)
                vv = ctx.op("vector", lambda e: e.tensor_tensor(
                    attv[0:64, :], attnh[0:64, :], psC[0:64, :], op=ALU.mult), "V")
                ctx.wait("tensor", "V", vv)
                sh = com["shift0"] if h % 2 == 0 else com["shift1"]
                pe2 = ctx.op("tensor", lambda e, sh=sh, h=h: e.matmul(
                    psE2[:, :], sh[0:64, :], attv[0:64, :],
                    start=(h % 2 == 0), stop=(h % 2 == 1)), "T")
                if h % 2 == 1:
                    ctx.wait("vector", "T", pe2)
                    rr_ = ctx.op("vector", lambda e, j=j: e.tensor_tensor(
                        xf[:, j, :], xf[:, j, :], psE2[:, :], op=ALU.add), "V")
                    ctx.wait("tensor", "V", rr_)
                ctx.wait("scalar", "V", ctx.ctr["V"])

            P1(0)
            for h in range(16):
                if h + 1 < 16:
                    P1(h + 1)
                P2(h)
            ctx.join("sync", ["V"])
            st = ctx.dma("sync", xview(xout, b, t0c, CW), xf[:, :, :], "O")
    return None





def pass_l1aa(ctx, com, xin, xout, wd):
    DC = 256
    wdq = ctx.sb("wdq", [128, KD, DC], BF16)
    wdkv = ctx.sb("wdkv", [128, KD, DC], BF16)
    wuq = ctx.sb("wuq", [128, 2, 1024], BF16)
    wuk = ctx.sb("wuk", [128, 2, 1024], BF16)
    wuv = ctx.sb("wuv", [128, 2, 1024], BF16)
    wo = ctx.sb("wo_mla", [128, KD, 1024], BF16)
    ctx.dma("gpsimd", wdq[:, :, :], wd["W_dq"].rearrange("(k p) f -> p k f", p=128), "W")
    ctx.dma("gpsimd", wdkv[:, :, :], wd["W_dkv"].rearrange("(k p) f -> p k f", p=128), "W")
    ctx.dma("gpsimd", wuq[:, :, :], wd["W_uq"].rearrange("(k p) f -> p k f", p=128), "W")
    ctx.dma("gpsimd", wuk[:, :, :], wd["W_uk"].rearrange("(k p) f -> p k f", p=128), "W")
    ctx.dma("gpsimd", wuv[:, :, :], wd["W_uv"].rearrange("(k p) f -> p k f", p=128), "W")
    ctx.dma("gpsimd", wo[:, :, :], wd["W_o"].rearrange("(k p) f -> p k f", p=128), "W")
    ctx.join("tensor", ["W"])

    NEG = -1e30
    m1 = np.zeros((4, 128, 512), dtype=np.float32)
    for qt in range(4):
        for qi in range(128):
            m1[qt, qi, :] = NEG
            m1[qt, qi, :qt * 128 + qi + 1] = 0.0
    m2 = np.zeros((4, 128, 512), dtype=np.float32)
    for r in range(4):
        for ki in range(128):
            key = r * 128 + ki
            m2[r, ki, :] = NEG
            m2[r, ki, key:] = 0.0
    mb1 = ctx.sb("mb1", [128, 4, 512], BF16)
    mb2 = ctx.sb("mb2", [128, 4, 512], BF16)
    ctx.dma("gpsimd", mb1[:, :, :],
            ctx.const("mla_m1", np.ascontiguousarray(
                m1.transpose(1, 0, 2)).astype(bf16)), "W")
    ctx.dma("gpsimd", mb2[:, :, :],
            ctx.const("mla_m2", np.ascontiguousarray(
                m2.transpose(1, 0, 2)).astype(bf16)), "W")
    ctx.join("tensor", ["W"])

    psA = ctx.ps("psA2", [128, 1024])
    psB = ctx.ps("psB2", [128, 512])
    psC = ctx.ps("psC2a", [128, 512])
    psC2 = ctx.ps("psC2b", [128, 512])
    psD = ctx.ps("psD2", [128, 512])
    psE = ctx.ps("psE2a", [128, 512])
    psE2 = ctx.ps("psE2b", [128, 512])

    cqb = ctx.sb("cqb", [128, 2, CW], BF16)
    ckvb = ctx.sb("ckvb", [128, 2, CW], BF16)
    Qb = ctx.sb("Qb", [128, 8, T], BF16)
    Kb = ctx.sb("Kb", [128, 8, T], BF16)
    vtm = ctx.sb("vtm2", [128, 16, 16, 65], BF16)
    Etb = ctx.sb("Etb2", [128, 2, 512], BF16)
    xf = ctx.sb("xf_mla", [128, KD, CW], F32)
    xln = ctx.sb("xln_mla", [128, KD, CW], BF16)
    negm = ctx.sb("negm2", [128, 4], F32)
    mx2 = ctx.sb("mx22", [128, 1], F32)
    negmc = ctx.sb("negmc2", [128, 1], BF16)
    negrow = ctx.sb("negrow2", [1, 512], BF16)
    rrowb = ctx.sb("rrowb2", [65, 512], BF16)
    attnh = ctx.sb("attnh2", [64, 512], BF16)
    attv = ctx.sb("attv2", [64, 512], BF16)
    aout = ctx.sb("aout2", [128, 8, 512], BF16)

    for b in range(NB):
        ctx.barrier()
        ones_done = ctx.op("vector", lambda e: e.memset(vtm[:, :, :, 64:65], 1.0), "V")
        ctx.wait("tensor", "V", ones_done)
        ev_prev = {}
        for c in range(NC_PER_B):
            t0c = c * CW
            lnv, lne = emit_ln_chunk(ctx, com, xf, xln, psA[:, 0:512],
                                     psA[:, 512:1024],
                                     src_view=xview(xin, b, t0c, CW))
            ctx.wait("tensor", lne, lnv)
            ps_ev = {}
            for (wt, dstb, tag) in ((wdq, cqb, "cq"), (wdkv, ckvb, "ckv")):
                for ft in range(2):
                    ps = psB if ft % 2 == 0 else psC
                    if id(ps) in ps_ev:
                        ctx.wait("tensor", "S", ps_ev[id(ps)])
                    for k in range(KD):
                        mm = ctx.op("tensor", lambda e, wt=wt, ft=ft, k=k, ps=ps:
                                    e.matmul(ps[:, :], wt[:, k, ft * 128:(ft + 1) * 128],
                                             xln[:, k, :], start=(k == 0),
                                             stop=(k == KD - 1)),
                                    "T" if k == KD - 1 else None)
                    ctx.wait("scalar", "T", mm)
                    ev = ctx.op("scalar", lambda e, dstb=dstb, ft=ft, ps=ps:
                                e.activation(dstb[:, ft, 0:CW], ps[:, :],
                                             AF.Identity), "S")
                    ps_ev[id(ps)] = ev
            ctx.wait("tensor", "S", ev)
            for (srcb, wt, dstb, scl) in ((cqb, wuq, Qb, 0.125), (ckvb, wuk, Kb, 1.0)):
                for ft in range(8):
                    ps = psB if ft % 2 == 0 else psC
                    if id(ps) in ps_ev:
                        ctx.wait("tensor", "S", ps_ev[id(ps)])
                    for k in range(2):
                        mm = ctx.op("tensor", lambda e, srcb=srcb, wt=wt, ft=ft,
                                    k=k, ps=ps: e.matmul(
                                        ps[:, :], wt[:, k, ft * 128:(ft + 1) * 128],
                                        srcb[:, k, 0:CW], start=(k == 0),
                                        stop=(k == 1)), "T" if k == 1 else None)
                    ctx.wait("scalar", "T", mm)
                    ev = ctx.op("scalar", lambda e, dstb=dstb, ft=ft, t0c=t0c,
                                ps=ps, scl=scl: e.activation(
                                    dstb[:, ft, t0c:t0c + CW], ps[:, :],
                                    AF.Identity, scale=scl), "S")
                    ps_ev[id(ps)] = ev
            ctx.wait("tensor", "S", ev)
            v_ev = None
            for tt in range(4):
                gtt = c * 4 + tt
                for half in range(2):
                    if v_ev is not None:
                        ctx.wait("tensor", "V", v_ev)
                    for k in range(2):
                        mm = ctx.op("tensor", lambda e, tt=tt, half=half, k=k:
                                    e.matmul(
                                        psD[:, :],
                                        ckvb[:, k, tt * 128:(tt + 1) * 128],
                                        wuv[:, k, half * 512:(half + 1) * 512],
                                        start=(k == 0), stop=(k == 1)),
                                    "T" if k == 1 else None)
                    ctx.wait("vector", "T", mm)
                    v_ev = ctx.op("vector", lambda e, gtt=gtt, half=half:
                                  e.tensor_copy(
                                      vtm[:, gtt, half * 8:(half + 1) * 8, 0:64],
                                      psD[:, :].rearrange("p (h d) -> p h d", d=64)),
                                  "V", lp=True)
        ctx.barrier()
        for c in range(NC_PER_B):
            t0c = c * CW
            ctx.join("sync", ["V", "O"])
            a_done = ctx.dma("sync", xf[:, :, :], xview(xin, b, t0c, CW), "A")
            ctx.wait("vector", "A", a_done)
            ctx.join("tensor", ["V", "S"])
            for h in range(16):
                j, p0 = h // 2, (h % 2) * 64
                if h % 2 == 0 and h > 0:
                    ctx.wait("tensor", "S", ctx.ctr["S"])
                red_done = None
                for qt in range(4):
                    qoff = t0c + qt * 128
                    nkc = c + 1
                    for kc in range(nkc):
                        ps1 = psA[:, (kc % 2) * 512:(kc % 2) * 512 + 512]
                        if red_done is not None:
                            ctx.wait("tensor", "V", red_done - 1)
                        diag = (kc == c)
                        if diag:
                            ctx.op("tensor", lambda e, qt=qt, ps1=ps1: e.matmul(
                                ps1, com["ident"][:, :], mb1[:, qt, :],
                                start=True, stop=False))
                        mm = ctx.op("tensor", lambda e, j=j, p0=p0, qoff=qoff,
                                    kc=kc, ps1=ps1, diag=diag: e.matmul(
                                        ps1, Qb[p0:p0 + 64, j, qoff:qoff + 128],
                                        Kb[p0:p0 + 64, j, kc * 512:(kc + 1) * 512],
                                        start=not diag, stop=True), "T")
                        ctx.wait("vector", "T", mm)
                        if kc == 0:
                            red_done = ctx.fop("vector", lambda e, qt=qt, ps1=ps1:
                                               e.reduce_max(negm[:, qt:qt + 1], ps1,
                                                            axis=AX.X, negate=True))
                        else:
                            ctx.fop("vector", lambda e, ps1=ps1: e.reduce_max(
                                mx2[:, :], ps1, axis=AX.X, negate=True))
                            red_done = ctx.op("vector", lambda e, qt=qt:
                                              e.tensor_tensor(
                                                  negm[:, qt:qt + 1],
                                                  negm[:, qt:qt + 1], mx2[:, :],
                                                  op=ALU.min), "V")
                    vv = ctx.op("vector", lambda e, qt=qt: e.tensor_copy(
                        negmc[:, 0:1], negm[:, qt:qt + 1]), "V")
                    ctx.wait("tensor", "V", vv)
                    tt_ = ctx.op("tensor", lambda e, qt=qt: e.matmul(
                        psB[0:1, qt * 128:(qt + 1) * 128], negmc[:, 0:1],
                        com["ident"][:, :], start=True, stop=True), "T")
                ctx.wait("scalar", "T", tt_)
                se = ctx.op("scalar", lambda e: e.copy(negrow[0:1, :],
                                                       psB[0:1, 0:512]), "S")
                ctx.wait("tensor", "S", se)
                nkt = 4 * (c + 1)
                ev = {}
                av = {}
                for kt in range(nkt):
                    ps2 = psC if kt % 2 == 0 else psC2
                    if kt >= 2:
                        ctx.wait("tensor", "S", ev[kt - 2])
                    if kt >= 1:
                        ctx.wait("tensor", "S", ev[kt - 1])
                        av[kt - 1] = ctx.op("tensor", lambda e, h=h, kt=kt: e.matmul(
                            psD[0:65, :], vtm[:, kt - 1, h, :], Etb[:, (kt - 1) % 2, :],
                            start=(kt - 1 == 0), stop=False), "T")
                    ctx.op("tensor", lambda e, ps2=ps2: e.matmul(
                        ps2[:, :], com["ones_row"][0:1, :], negrow[0:1, :],
                        start=True, stop=False))
                    if kt >= 4 * c:
                        ctx.op("tensor", lambda e, kt=kt, c=c, ps2=ps2: e.matmul(
                            ps2[:, :], com["ident"][:, :], mb2[:, kt - 4 * c, :],
                            start=False, stop=False))
                    mm = ctx.op("tensor", lambda e, j=j, p0=p0, kt=kt, ps2=ps2,
                                t0c=t0c: e.matmul(
                                    ps2[:, :], Kb[p0:p0 + 64, j, kt * 128:(kt + 1) * 128],
                                    Qb[p0:p0 + 64, j, t0c:t0c + CW],
                                    start=False, stop=True), "T")
                    ctx.wait("scalar", "T", mm)
                    ev[kt] = ctx.op("scalar", lambda e, kt=kt, ps2=ps2: e.activation(
                        Etb[:, kt % 2, :], ps2[:, :], AF.Exp), "S")
                ctx.wait("tensor", "S", ev[nkt - 1])
                av[nkt - 1] = ctx.op("tensor", lambda e, h=h, nkt=nkt: e.matmul(
                    psD[0:65, :], vtm[:, nkt - 1, h, :], Etb[:, (nkt - 1) % 2, :],
                    start=(nkt == 1), stop=True), "T")
                ctx.wait("vector", "T", av[nkt - 1])
                rv = ctx.op("vector", lambda e: e.reciprocal(rrowb[64:65, :],
                                                             psD[64:65, :]), "V",
                            lp=True)
                ctx.wait("tensor", "V", rv)
                tb = ctx.op("tensor", lambda e: e.matmul(
                    psE[0:64, :], com["ones_row"][64:65, 0:64], rrowb[64:65, :],
                    start=True, stop=True), "T")
                ctx.wait("scalar", "T", tb)
                sv = ctx.op("scalar", lambda e: e.copy(attnh[0:64, :],
                                                       psD[0:64, :]), "S")
                ctx.wait("vector", "S", sv)
                vv = ctx.op("vector", lambda e: e.tensor_tensor(
                    attv[0:64, :], attnh[0:64, :], psE[0:64, :], op=ALU.mult), "V")
                ctx.wait("tensor", "V", vv)
                sh = com["shift0"] if h % 2 == 0 else com["shift1"]
                pe2 = ctx.op("tensor", lambda e, sh=sh, h=h: e.matmul(
                    psE2[:, :], sh[0:64, :], attv[0:64, :],
                    start=(h % 2 == 0), stop=(h % 2 == 1)), "T")
                if h % 2 == 1:
                    ctx.wait("scalar", "T", pe2)
                    sv2 = ctx.op("scalar", lambda e, j=j: e.copy(
                        aout[:, j, :], psE2[:, :]), "S")
                    ctx.wait("tensor", "S", sv2)
                ctx.wait("scalar", "V", ctx.ctr["V"])
            ctx.join("tensor", ["S"])
            for ft in range(KD):
                ps = psB if ft % 2 == 0 else psC
                if ft >= 2:
                    ctx.wait("tensor", "V", wo_ev[ft - 2])
                for k in range(KD):
                    mm = ctx.op("tensor", lambda e, ft=ft, k=k, ps=ps: e.matmul(
                        ps[:, :], wo[:, k, ft * 128:(ft + 1) * 128],
                        aout[:, k, :], start=(k == 0), stop=(k == KD - 1)),
                        "T" if k == KD - 1 else None)
                ctx.wait("vector", "T", mm)
                if ft == 0:
                    wo_ev = {}
                wo_ev[ft] = ctx.op("vector", lambda e, ft=ft, ps=ps: e.tensor_tensor(
                    xf[:, ft, :], xf[:, ft, :], ps[:, :], op=ALU.add), "V")
            ctx.join("sync", ["V"])
            st = ctx.dma("sync", xview(xout, b, t0c, CW), xf[:, :, :], "O")
    return None




def pass_l1moe(ctx, com, xin, xout, wd):
    E, HID = 8, 2048
    gwt = ctx.sb("gwt", [128, KD, 8], F32)
    w_done = ctx.dma("gpsimd", gwt[:, :, :],
                     wd["gate_wT"].rearrange("(k p) e -> p k e", p=128), "W")
    ctx.wait("tensor", "W", w_done)
    sel8 = ctx.sb("sel8", [8, 8, 128], BF16)
    sel = np.zeros((8, 8, 128), dtype=bf16)
    for e in range(8):
        sel[e, e, :] = 1
    sel = np.ascontiguousarray(sel.transpose(1, 0, 2))
    wsel = ctx.dma("gpsimd", sel8[:, :, :], ctx.const("sel8", sel), "W")
    ctx.wait("tensor", "W", wsel)
    ones_col_f = ctx.fconst(1.0)
    ctx.join("tensor", ["W"])

    psH1a = ctx.ps("psH1a", [128, 512])
    psH1b = ctx.ps("psH1b", [128, 512])
    psH2a = ctx.ps("psH2a", [128, 512])
    psH2b = ctx.ps("psH2b", [128, 512])
    psY = ctx.ps("psY", [128, 512])
    psY2 = ctx.ps("psY2", [128, 512])
    if getattr(ctx, "_es", None) is not None:
        psT = ctx._es.enter_context(
            ctx.nc.psum_tensor(f"psT_{ctx.uid()}", [128, 136], BF16))
        psTf = ctx._es.enter_context(
            ctx.nc.psum_tensor(f"psTf_{ctx.uid()}", [128, 16], F32))
    else:
        psT = ctx.nc.alloc_psum_tensor(f"psT_{ctx.uid()}", [128, 136], BF16)
        psTf = ctx.nc.alloc_psum_tensor(f"psTf_{ctx.uid()}", [128, 16], F32)

    TB = 1024
    NCB = TB // CW
    xlnb = ctx.sb("xlnb", [128, KD, TB], BF16)
    xf = ctx.sb("xf_moe", [128, KD, CW], F32)
    gsc = ctx.sb("gsc", [8, TB], F32)
    gtm = ctx.sb("gtm", [128, 8, 8], F32)
    gsum = ctx.sb("gsum", [1, 8], F32)
    rsv = ctx.sb("rsv", [1, TB], F32)
    rtm = ctx.sb("rtm", [128, 1], F32)
    m1n = ctx.sb("m1n", [128, 1], F32)
    m2 = ctx.sb("m2", [128, 1], F32)
    eq = ctx.sb("eq", [128, 8], F32)
    msk = ctx.sb("msk", [128, 8], F32)
    ge = ctx.sb("ge", [128, 8], F32)
    expw = ctx.sb("expw", [128, 8], F32)
    d2 = ctx.sb("d2", [128, 1], F32)
    rr = ctx.sb("rr_moe", [128, 1], F32)
    dd = ctx.sb("dd_moe", [128, 1], F32)
    wexp = ctx.sb("wexp", [128, 8], F32)
    ptb = ctx.sb("ptb", [128, 8], BF16)
    pfm = ctx.sb("pfm", [8, TB], BF16)
    pbcs = ctx.sb("pbcs", [128, 2, 512], BF16)
    wsl = ctx.sb("wsl", [128, 2, KD, 2, 128], BF16)
    w3sb = ctx.sb("w3sb", [128, 16, 1024], BF16)
    sh1 = ctx.sb("sh1", [128, 512], BF16)
    th = ctx.sb("th", [128, 512], BF16)
    hidb = ctx.sb("hidb", [128, 16, TB], BF16)
    yac = ctx.sb("yac", [128, KD, TB], F32)

    for k in range(KD):
        gs_mm = ctx.op("tensor", lambda e, k=k: e.matmul(
            psTf[0:1, 8:16], ones_col_f[:, 0:1], gwt[:, k, :],
            start=(k == 0), stop=(k == KD - 1)), "T" if k == KD - 1 else None)
    ctx.wait("vector", "T", gs_mm)
    gsv = ctx.op("vector", lambda e: e.tensor_copy(gsum[0:1, :],
                                                   psTf[0:1, 8:16]), "V")
    ctx.wait("tensor", "V", gsv)

    for blk in range(NB * 2):
        b, toff = blk // 2, (blk % 2) * TB
        ctx.barrier()
        for c in range(NCB):
            lnv, lne = emit_ln_chunk(ctx, com, xf, xlnb[:, :, c * CW:(c + 1) * CW],
                                     psH1a[:, :], psH1b[:, :],
                                     src_view=xview(xin, b, toff + c * CW, CW),
                                     xf_guard=("S", "V", "T"))
            if c > 0:
                ctx.wait("tensor", "S", gv)
            for k in range(KD):
                ctx.op("tensor", lambda e, k=k: e.matmul(
                    psY[0:8, 0:512], gwt[:, k, :], xf[:, k, :],
                    start=(k == 0), stop=False))
            mm = ctx.op("tensor", lambda e: e.matmul(
                psY[0:8, 0:512], gsum[0:1, :], com["ln_nm"][0:1, 0:512],
                start=False, stop=True), "T")
            ctx.wait("scalar", "T", mm)
            gv = ctx.op("scalar", lambda e, c=c: e.copy(
                gsc[0:8, c * 512:(c + 1) * 512], psY[0:8, 0:512]), "S")
            rv_ = ctx.op("vector", lambda e, c=c: e.tensor_copy(
                rsv[0:1, c * 512:(c + 1) * 512], com["ln_rstd"][0:1, 0:512]), "V")
        ctx.wait("tensor", "S", gv)
        ctx.wait("tensor", "V", rv_)
        for tt in range(TB // 128):
            tr = ctx.op("tensor", lambda e, tt=tt: e.transpose(
                psTf[0:128, 0:8], gsc[0:8, tt * 128:(tt + 1) * 128],
                com["ident_f"][0:8, 0:8]), "T")
            tr = ctx.op("tensor", lambda e, tt=tt: e.transpose(
                psTf[0:128, 8:9], rsv[0:1, tt * 128:(tt + 1) * 128],
                com["ident_f"][0:1, 0:1]), "T")
            ctx.wait("vector", "T", tr)
            ctx.fop("vector", lambda e: e.tensor_copy(rtm[:, 0:1],
                                                      psTf[0:128, 8:9]))
            ctx.fop("vector", lambda e, tt=tt: e.tensor_scalar(
                gtm[:, tt, :], psTf[0:128, 0:8], rtm[:, 0:1], None,
                op0=ALU.mult))
            g = gtm[:, tt, :]
            ctx.fop("vector", lambda e, g=g: e.reduce_max(m1n[:, :], g, axis=AX.X,
                                                          negate=True))
            ctx.fop("vector", lambda e, g=g: e.tensor_scalar(
                eq[:, :], g, m1n[:, 0:1], 0.0, op0=ALU.add, op1=ALU.is_ge))
            ctx.fop("vector", lambda e, g=g: e.scalar_tensor_tensor(
                msk[:, :], eq[:, :], -1e30, g, op0=ALU.mult, op1=ALU.add))
            vv = ctx.fop("vector", lambda e: e.reduce_max(m2[:, :], msk[:, :],
                                                          axis=AX.X))
            ctx.fop("vector", lambda e, g=g: e.tensor_scalar(
                ge[:, :], g, m2[:, 0:1], None, op0=ALU.is_ge))
            ctx.wait("scalar", "V", ctx.ctr["V"])
            s1 = ctx.op("scalar", lambda e, g=g: e.activation(
                expw[:, :], g, AF.Exp, bias=m1n[:, 0:1]), "S")
            s2 = ctx.op("scalar", lambda e: e.activation(
                d2[:, 0:1], m2[:, 0:1], AF.Exp, bias=m1n[:, 0:1]), "S")
            ctx.wait("vector", "S", s2)
            ctx.fop("vector", lambda e: e.tensor_scalar(
                dd[:, 0:1], d2[:, 0:1], 1.0, None, op0=ALU.add))
            ctx.fop("vector", lambda e: e.reciprocal(rr[:, 0:1], dd[:, 0:1]))
            ctx.fop("vector", lambda e: e.tensor_mul(wexp[:, :], expw[:, :],
                                                     ge[:, :]))
            pv = ctx.fop("vector", lambda e: e.tensor_scalar(
                ptb[:, :], wexp[:, :], rr[:, 0:1], None, op0=ALU.mult))
            ctx.wait("tensor", "V", pv)
            tr2 = ctx.op("tensor", lambda e: e.transpose(
                psT[0:8, 8:136], ptb[:, 0:8], com["ident"][:, :]), "T")
            ctx.wait("scalar", "T", tr2)
            sp = ctx.op("scalar", lambda e, tt=tt: e.copy(
                pfm[0:8, tt * 128:(tt + 1) * 128], psT[0:8, 8:136]), "S")
            ctx.wait("tensor", "S", sp)
            ctx.wait("vector", "S", sp)
        for e_i in range(E):
            ctx.join("tensor", ["V", "S"])
            for ch in range(NCB):
                pm = ctx.op("tensor", lambda e, e_i=e_i, ch=ch: e.matmul(
                    psY2[:, :], sel8[:, e_i, :], pfm[0:8, ch * 512:(ch + 1) * 512],
                    start=True, stop=True), "T")
                ctx.wait("scalar", "T", pm)
                pe_ = ctx.op("scalar", lambda e, ch=ch: e.copy(
                    pbcs[:, ch, :], psY2[:, :]), "S")
                ctx.wait("tensor", "S", pe_)
            ctx.join("gpsimd", ["T"])
            w3_done = ctx.dma("gpsimd", w3sb[:, :, :],
                              wd["w3T"].rearrange("e (k p) f -> e p k f", p=128)[e_i],
                              "W")
            hvs = {}
            for ht in range(16):
                sl = ht % 2
                if ht >= 2:
                    ctx.wait("gpsimd", "T", hl_done[ht - 2])
                wsrc = wd["w12T"].rearrange(
                    "e (k p) (g c) -> e p k g c", p=128, g=2)
                w0 = ctx.dma("gpsimd", wsl[:, sl, :, 0, :],
                             wsrc[e_i, :, :, 0, ht * 128:(ht + 1) * 128], "W")
                w1 = ctx.dma("gpsimd", wsl[:, sl, :, 1, :],
                             wsrc[e_i, :, :, 1, ht * 128:(ht + 1) * 128], "W")
                ctx.wait("tensor", w0)
                ctx.wait("tensor", w1)
                if ht == 0:
                    hl_done = {}
                for ch in range(NCB):
                    m = ht * NCB + ch
                    if m >= 2:
                        ctx.wait("tensor", "V", hvs[m - 2])
                    elif e_i > 0 and m == 0:
                        ctx.join("tensor", ["V"])
                    p1 = psH1a if m % 2 == 0 else psH1b
                    p2 = psH2a if m % 2 == 0 else psH2b
                    for k in range(KD):
                        ctx.op("tensor", lambda e, sl=sl, k=k, ch=ch, p1=p1:
                               e.matmul(
                                   p1[:, :], wsl[:, sl, k, 0, :],
                                   xlnb[:, k, ch * 512:(ch + 1) * 512],
                                   start=(k == 0), stop=(k == KD - 1)))
                    for k in range(KD):
                        mm2 = ctx.op("tensor", lambda e, sl=sl, k=k, ch=ch, p2=p2:
                                     e.matmul(
                                         p2[:, :], wsl[:, sl, k, 1, :],
                                         xlnb[:, k, ch * 512:(ch + 1) * 512],
                                         start=(k == 0), stop=(k == KD - 1)),
                                     "T" if k == KD - 1 else None)
                    hl_done[ht] = mm2
                    ctx.wait("scalar", "T", mm2)
                    if m >= 1:
                        ctx.wait("scalar", "V", hvs[m - 1])
                    sv = ctx.op("scalar", lambda e, p1=p1: e.activation(
                        sh1[:, :], p1[:, :], AF.Silu), "S")
                    ctx.wait("vector", "S", sv)
                    ctx.op("vector", lambda e, p2=p2: e.tensor_tensor(
                        th[:, :], sh1[:, :], p2[:, :],
                        op=ALU.mult), lp=True)
                    hv = ctx.op("vector", lambda e, ht=ht, ch=ch: e.tensor_tensor(
                        hidb[:, ht, ch * 512:(ch + 1) * 512], th[:, :],
                        pbcs[:, ch, :], op=ALU.mult), "V", lp=True)
                    hvs[m] = hv
            ctx.wait("tensor", "W", w3_done)
            ctx.join("tensor", ["V"])
            yv = {}
            for ch in range(NCB):
                for ft in range(KD):
                    ps = psY if ft % 2 == 0 else psY2
                    if (ch * KD + ft) >= 2:
                        prev = (ch * KD + ft) - 2
                        ctx.wait("tensor", "V", yv[prev])
                    for ht in range(16):
                        mm = ctx.op("tensor", lambda e, ht=ht, ft=ft, ch=ch, ps=ps:
                                    e.matmul(
                                        ps[:, :], w3sb[:, ht, ft * 128:(ft + 1) * 128],
                                        hidb[:, ht, ch * 512:(ch + 1) * 512],
                                        start=(ht == 0), stop=(ht == 15)),
                                    "T" if ht == 15 else None)
                    ctx.wait("vector", "T", mm)
                    if e_i == 0:
                        yv[ch * KD + ft] = ctx.op("vector",
                                                  lambda e, ft=ft, ch=ch, ps=ps:
                                                  e.tensor_copy(
                                                      yac[:, ft, ch * 512:(ch + 1) * 512],
                                                      ps[:, :]), "V")
                    else:
                        yv[ch * KD + ft] = ctx.op("vector",
                                                  lambda e, ft=ft, ch=ch, ps=ps:
                                                  e.tensor_tensor(
                                                      yac[:, ft, ch * 512:(ch + 1) * 512],
                                                      yac[:, ft, ch * 512:(ch + 1) * 512],
                                                      ps[:, :], op=ALU.add), "V")
        for c in range(NCB):
            ctx.join("sync", ["V", "O"])
            a_done = ctx.dma("sync", xf[:, :, :], xview(xin, b, toff + c * CW, CW), "A")
            ctx.wait("vector", "A", a_done)
            for k in range(KD):
                fv = ctx.op("vector", lambda e, k=k, c=c: e.tensor_tensor(
                    xf[:, k, :], xf[:, k, :], yac[:, k, c * 512:(c + 1) * 512],
                    op=ALU.add), "V")
            ctx.join("sync", ["V"])
            st = ctx.dma("sync", xview(xout, b, toff + c * CW, CW), xf[:, :, :], "O")

    return None




def pass_l0ab(ctx, com, xin, xout, wd):
    DI, NS = 2048, 16
    KI = DI // 128

    wxt = ctx.sb("wxt", [128, KI, 32], BF16)
    ctx.dma("gpsimd", wxt[:, :, :], wd["W_xT"].rearrange("(k p) f -> p k f", p=128), "W")
    wout = ctx.sb("wout_ab", [128, KI, 1024], BF16)
    ctx.dma("gpsimd", wout[:, :, :], wd["W_outT"].rearrange("(k p) f -> p k f", p=128), "W")
    convw = ctx.sb("convw", [128, 16, 4], F32)
    ctx.dma("gpsimd", convw[:, :, :], wd["conv_w_c"], "W")
    convb = ctx.sb("convb", [128, 16], F32)
    ctx.dma("gpsimd", convb[:, :], wd["conv_b_c"], "W")
    bdt = ctx.sb("bdt", [128, 16], F32)
    ctx.dma("gpsimd", bdt[:, :], wd["b_dt_c"], "W")
    alog = ctx.sb("alog", [128, 16, 16], F32)
    ctx.dma("gpsimd", alog[:, :, :], wd["A_log_c"], "W")
    dcol = ctx.sb("dcol", [128, 16], F32)
    ctx.dma("gpsimd", dcol[:, :], wd["D_c"], "W")
    selc = ctx.sb("selc", [32, 32, 128], BF16)
    sel = np.zeros((32, 32, 128), dtype=bf16)
    for j in range(32):
        sel[j, j, :] = 1
    ctx.dma("gpsimd", selc[:, :, :], ctx.const("sel32", sel), "W")
    for eng in ("tensor", "vector", "scalar"):
        ctx.join(eng, ["W"])
    Ac = ctx.sb("Ac", [128, 16, 16], F32)
    s0 = ctx.op("scalar", lambda e: e.activation(Ac[:, :, :], alog[:, :, :],
                                                 AF.Exp), "S")
    ctx.wait("vector", "S", s0)
    v0 = ctx.fop("vector", lambda e: e.tensor_scalar(
        Ac[:, :, :], Ac[:, :, :], -1.0, None, op0=ALU.mult))
    ctx.wait("scalar", "V", v0)

    psA = ctx.ps("psA_ab", [128, 1024])
    psB = ctx.ps("psB_ab", [128, 512])
    psC = ctx.ps("psC_ab", [128, 512])
    psBC = ctx.ps("psBC_ab", [128, 512])
    psY = ctx.ps("psY_ab", [128, 512])
    psY2 = ctx.ps("psY2_ab", [128, 512])

    xf = ctx.sb("xf_ab", [128, KD, CW], F32)
    xln = ctx.sb("xln_ab", [128, KD, CW], BF16)
    winsl = ctx.sb("winsl", [128, 2, KD, 128], BF16)
    wdtsl = ctx.sb("wdtsl", [128, 2, KI, 128], BF16)
    xib = ctx.sb("xib", [128, 16, 515], BF16)
    xtail = ctx.sb("xtail", [128, 16, 3], BF16)
    zsb = ctx.sb("zsb", [128, 16, 512], BF16)
    cacc = ctx.sb("cacc", [128, 512], F32)
    bcs = ctx.sb("bcs", [32, 512], BF16)
    Bbcs = ctx.sb("Bbcs", [128, 16, 512], BF16)
    Cbcs = ctx.sb("Cbcs", [128, 16, 512], BF16)
    dtf = ctx.sb("dtf", [128, 512], F32)
    spl = ctx.sb("spl", [128, 512], F32)
    dtu = ctx.sb("dtu", [128, 512], F32)
    dAt = ctx.sb("dAt", [128, 2, 512], F32)
    dBt = ctx.sb("dBt", [128, 512], F32)
    stt_ = ctx.sb("st_ab", [128, 512], F32)
    sCt = ctx.sb("sCt", [128, 2, 512], BF16)
    carry = ctx.sb("carry", [128, 256], F32)
    yb = ctx.sb("yb", [128, 16, 512], BF16)
    yt1 = ctx.sb("yt1", [128, 512], F32)

    for b in range(NB):
        ctx.barrier()
        cz = ctx.op("vector", lambda e: e.memset(carry[:, :], 0.0), "V")
        ht = ctx.op("vector", lambda e: e.memset(xtail[:, :, :], 0.0), "V")
        ctx.wait("vector", "V", ht)
        ctx.wait("scalar", "V", ht)
        for c in range(NC_PER_B):
            t0c = c * CW
            lnv, lne = emit_ln_chunk(ctx, com, xf, xln, psA[:, 0:512],
                                     psA[:, 512:1024],
                                     src_view=xview(xin, b, t0c, CW))
            ctx.wait("tensor", lne, lnv)
            hv = ctx.fop("vector", lambda e: e.tensor_copy(xib[:, :, 0:3],
                                                           xtail[:, :, :]))
            ctx.wait("scalar", "V", hv)
            ev = {}
            for sb_i in range(32):
                sl = sb_i % 2
                if sb_i >= 2:
                    ctx.wait("gpsimd", "T", ev[sb_i - 2])
                wtok = ctx.dma("gpsimd", winsl[:, sl, :, :],
                               wd["W_inT"].rearrange("(k p) f -> p k f", p=128)
                               [:, :, sb_i * 128:(sb_i + 1) * 128], "W")
                ctx.wait("tensor", wtok)
                for f4 in range(1):
                    ft = sb_i
                    ps = psB if ft % 2 == 0 else psC
                    if ft >= 2:
                        ctx.wait("tensor", "S", sev[ft - 2])
                    for k in range(KD):
                        mm = ctx.op("tensor", lambda e, sl=sl, k=k, ps=ps:
                                    e.matmul(ps[:, :],
                                             winsl[:, sl, k, :],
                                             xln[:, k, :], start=(k == 0),
                                             stop=(k == KD - 1)),
                                    "T" if k == KD - 1 else None)
                    if ft == 0:
                        sev = {}
                    ev[sb_i] = mm
                    ctx.wait("scalar", "T", mm)
                    if ft < 16:
                        sev[ft] = ctx.op("scalar", lambda e, ft=ft, ps=ps:
                                         e.copy(xib[:, ft, 3:515], ps[:, :]), "S")
                    else:
                        sev[ft] = ctx.op("scalar", lambda e, ft=ft, ps=ps:
                                         e.activation(zsb[:, ft - 16, :], ps[:, :],
                                                      AF.Silu), "S")
            ctx.wait("vector", "S", sev[15])
            tl = ctx.fop("vector", lambda e: e.tensor_copy(
                xtail[:, :, :], xib[:, :, 512:515]))
            for ft in range(16):
                if ft > 0:
                    ctx.wait("vector", "S", uv)
                ctx.fop("vector", lambda e, ft=ft: e.tensor_scalar(
                    cacc[:, :], xib[:, ft, 0:512], convw[:, ft, 0:1], None,
                    op0=ALU.mult))
                for j in range(1, 4):
                    ctx.fop("vector", lambda e, ft=ft, j=j: e.scalar_tensor_tensor(
                        cacc[:, :], xib[:, ft, j:512 + j], convw[:, ft, j:j + 1],
                        cacc[:, :], op0=ALU.mult, op1=ALU.add))
                ctx.wait("scalar", "V", ctx.ctr["V"])
                uv = ctx.op("scalar", lambda e, ft=ft: e.activation(
                    xib[:, ft, 3:515], cacc[:, :], AF.Silu,
                    bias=convb[:, ft:ft + 1]), "S")
            ctx.wait("tensor", "S", uv)
            for k in range(KI):
                mm = ctx.op("tensor", lambda e, k=k: e.matmul(
                    psBC[0:32, 0:512], wxt[:, k, :], xib[:, k, 3:515],
                    start=(k == 0), stop=(k == KI - 1)), "T" if k == KI - 1 else None)
            ctx.wait("scalar", "T", mm)
            bv = ctx.op("scalar", lambda e: e.copy(bcs[0:32, :], psBC[0:32, 0:512]),
                        "S")
            ctx.wait("tensor", "S", bv)
            for n in range(NS):
                for (srow, dstb) in ((n, Bbcs), (16 + n, Cbcs)):
                    mm = ctx.op("tensor", lambda e, srow=srow: e.matmul(
                        psBC[:, 0:512], selc[:, srow, :], bcs[0:32, :],
                        start=True, stop=True), "T")
                    ctx.wait("scalar", "T", mm)
                    se = ctx.op("scalar", lambda e, n=n, dstb=dstb: e.copy(
                        dstb[:, n, :], psBC[:, 0:512]), "S")
                    ctx.wait("tensor", "S", se)
            ctx.wait("vector", "S", se)
            for ft in range(KI):
                sl = ft % 2
                if ft >= 2:
                    ctx.wait("gpsimd", "T", dtev[ft - 2])
                wtok = ctx.dma("gpsimd", wdtsl[:, sl, :, :],
                               wd["W_dtT"].rearrange("(k p) f -> p k f", p=128)
                               [:, :, ft * 128:(ft + 1) * 128], "W")
                ctx.wait("tensor", wtok)
                ps = psB if ft % 2 == 0 else psC
                for k in range(KI):
                    mm = ctx.op("tensor", lambda e, sl=sl, k=k, ps=ps: e.matmul(
                        ps[:, :], wdtsl[:, sl, k, :], xib[:, k, 3:515],
                        start=(k == 0), stop=(k == KI - 1)),
                        "T" if k == KI - 1 else None)
                if ft == 0:
                    dtev = {}
                dtev[ft] = mm
                ctx.wait("scalar", "T", mm)
                s1 = ctx.op("scalar", lambda e, ft=ft, ps=ps: e.activation(
                    spl[:, :], ps[:, :], AF.Abs, bias=bdt[:, ft:ft + 1]), "S")
                s1 = ctx.op("scalar", lambda e: e.activation(
                    spl[:, :], spl[:, :], AF.Exp, scale=-1.0), "S")
                s1 = ctx.op("scalar", lambda e: e.activation(
                    spl[:, :], spl[:, :], AF.Ln, bias=com["one_c"][:, 0:1]), "S")
                ctx.wait("vector", "S", s1)
                ctx.fop("vector", lambda e, ft=ft, ps=ps: e.tensor_scalar(
                    dtf[:, :], ps[:, :], bdt[:, ft:ft + 1], 0.0,
                    op0=ALU.add, op1=ALU.max))
                dv = ctx.fop("vector", lambda e: e.tensor_add(
                    dtf[:, :], dtf[:, :], spl[:, :]))
                ctx.wait("tensor", "V", dv)
                dv = ctx.op("vector", lambda e, ft=ft: e.tensor_mul(
                    dtu[:, :], dtf[:, :], xib[:, ft, 3:515]), "V")
                ctx.wait("scalar", "V", dv)
                scv = {}
                ymm = {}
                for n in range(NS):
                    p = n % 2
                    if n >= 2:
                        ctx.wait("scalar", "V", scv[n - 1])
                    sa = ctx.op("scalar", lambda e, ft=ft, n=n, p=p: e.activation(
                        dAt[:, p, :], dtf[:, :], AF.Exp,
                        scale=Ac[:, ft, n:n + 1]), "S")
                    ctx.wait("vector", "S", sa)
                    ctx.op("vector", lambda e, n=n: e.tensor_mul(
                        dBt[:, :], dtu[:, :], Bbcs[:, n, :]))
                    ci = ft * NS + n
                    sc_ = ctx.fop("vector", lambda e, ci=ci, p=p:
                                  e.tensor_tensor_scan(
                                      stt_[:, :], dAt[:, p, :], dBt[:, :],
                                      carry[:, ci:ci + 1],
                                      op0=ALU.mult, op1=ALU.add))
                    scv[n] = sc_
                    ctx.op("vector", lambda e, ci=ci: e.tensor_copy(
                        carry[:, ci:ci + 1], stt_[:, 511:512]), "V")
                    if n >= 2:
                        ctx.wait("vector", "T", ymm[n - 2])
                    mv = ctx.op("vector", lambda e, n=n, p=p: e.tensor_mul(
                        sCt[:, p, :], stt_[:, :], Cbcs[:, n, :]), "V")
                    ctx.wait("tensor", "V", mv)
                    ymm[n] = ctx.op("tensor", lambda e, ft=ft, n=n, p=p: e.matmul(
                        psY[:, :] if ft % 2 == 0 else psY2[:, :],
                        com["ident"][:, :], sCt[:, p, :],
                        start=(n == 0), stop=(n == NS - 1)), "T")
                yv = ymm[NS - 1]
                ctx.wait("vector", "T", yv)
                pY = psY if ft % 2 == 0 else psY2
                ctx.fop("vector", lambda e, ft=ft, pY=pY: e.scalar_tensor_tensor(
                    yt1[:, :], xib[:, ft, 3:515], dcol[:, ft:ft + 1], pY[:, :],
                    op0=ALU.mult, op1=ALU.add))
                ye = ctx.fop("vector", lambda e, ft=ft: e.tensor_mul(
                    yb[:, ft, :], yt1[:, :], zsb[:, ft, :]))
                ctx.wait("tensor", "V", ye)
            for oft in range(KD):
                pso = psB if oft % 2 == 0 else psC
                if oft >= 2:
                    ctx.wait("tensor", "V", oev[oft - 2])
                for k in range(KI):
                    mm = ctx.op("tensor", lambda e, oft=oft, k=k, pso=pso: e.matmul(
                        pso[:, :], wout[:, k, oft * 128:(oft + 1) * 128],
                        yb[:, k, :], start=(k == 0), stop=(k == KI - 1)),
                        "T" if k == KI - 1 else None)
                ctx.wait("vector", "T", mm)
                if oft == 0:
                    oev = {}
                oev[oft] = ctx.op("vector", lambda e, oft=oft, pso=pso: e.tensor_tensor(
                    xf[:, oft, :], xf[:, oft, :], pso[:, :], op=ALU.add), "V")
            ctx.join("sync", ["V"])
            st = ctx.dma("sync", xview(xout, b, t0c, CW), xf[:, :, :], "O")
    return None



def pass_l1ab(ctx, com, xin, xout, wd):
    NS = 16
    wslg = ctx.sb("wslg", [128, 2, KD, 128], BF16)
    w1b = ctx.sb("w1b", [128, 8], F32)
    v1b = ctx.sb("v1b", [128, 8], F32)
    w2b = ctx.sb("w2b", [128, 8], F32)
    ctx.dma("gpsimd", w1b[:, :], wd["w1_b_c"], "W")
    ctx.dma("gpsimd", v1b[:, :], wd["v1_b_c"], "W")
    ctx.dma("gpsimd", w2b[:, :], wd["w2_b_c"], "W")
    convw = ctx.sb("convw1", [128, 8, 3], F32)
    ctx.dma("gpsimd", convw[:, :, :], wd["conv_w_c"], "W")
    Araw = ctx.sb("Araw", [128, 8, 16], F32)
    ctx.dma("gpsimd", Araw[:, :, :], wd["A_c"], "W")
    Bc = ctx.sb("Bc", [128, 8, 16], F32)
    ctx.dma("gpsimd", Bc[:, :, :], wd["B_c"], "W")
    Cc = ctx.sb("Cc", [128, 8, 16], F32)
    ctx.dma("gpsimd", Cc[:, :, :], wd["C_c"], "W")
    Dc = ctx.sb("Dc", [128, 8], F32)
    ctx.dma("gpsimd", Dc[:, :], wd["D_c"], "W")
    for eng in ("tensor", "vector", "scalar"):
        ctx.join(eng, ["W"])
    Abar = ctx.sb("Abar", [128, 8, 16], F32)
    s0 = ctx.op("scalar", lambda e: e.activation(Abar[:, :, :], Araw[:, :, :],
                                                 AF.Sigmoid), "S")
    ctx.wait("vector", "S", s0)

    psA = ctx.ps("psA_gb", [128, 1024])
    psB = ctx.ps("psB_gb", [128, 512])
    psC = ctx.ps("psC_gb", [128, 512])
    psY = ctx.ps("psY_gb", [128, 2048])

    xf = ctx.sb("xf_gb", [128, KD, CW], F32)
    xln = ctx.sb("xln_gb", [128, KD, CW], BF16)
    x1b = ctx.sb("x1b", [128, 8, 2050], BF16)
    gout = x1b
    vb = ctx.sb("vb_gb", [128, 8, 2048], BF16)
    xcb = ctx.sb("xcb_gb", [128, 8, 2048], BF16)
    cacc = ctx.sb("cacc_gb", [128, 2048], F32)
    onest = ctx.sb("onest", [128, 2048], F32)
    abt = ctx.sb("abt", [128, 2048], F32)
    bt = ctx.sb("bt_gb", [128, 2048], F32)
    st2 = ctx.sb("st_gb", [128, 2048], F32)
    hCt = ctx.sb("hCt", [128, 2048], BF16)
    yt1 = ctx.sb("yt1_gb", [128, 2048], F32)

    oz = ctx.op("vector", lambda e: e.memset(onest[:, :], 1.0), "V")
    ctx.wait("vector", "V", oz)

    for b in range(NB):
        ctx.barrier()
        hz = ctx.op("vector", lambda e: e.memset(x1b[:, :, 0:1], 0.0), "V")
        hz = ctx.op("vector", lambda e: e.memset(x1b[:, :, 2049:2050], 0.0), "V")
        ctx.wait("vector", "V", hz)
        ctx.wait("scalar", "V", hz)
        for c in range(NC_PER_B):
            t0c = c * CW
            lnv, lne = emit_ln_chunk(ctx, com, xf, xln, psA[:, 0:512],
                                     psA[:, 512:1024],
                                     src_view=xview(xin, b, t0c, CW))
            ctx.wait("tensor", lne, lnv)
            ev = {}
            for (wname, bias, tag) in (("w1_wT", w1b, 0), ("v1_wT", v1b, 1)):
                for ft in range(KD):
                    ps = psB if (tag * KD + ft) % 2 == 0 else psC
                    key = tag * KD + ft
                    sl = key % 2
                    if key >= 2:
                        ctx.wait("tensor", "S", ev[key - 2])
                        ctx.wait("gpsimd", "T", tvv[key - 2])
                    wtok = ctx.dma("gpsimd", wslg[:, sl, :, :],
                                   wd[wname].rearrange("(k p) f -> p k f", p=128)
                                   [:, :, ft * 128:(ft + 1) * 128], "W")
                    ctx.wait("tensor", wtok)
                    for k in range(KD):
                        mm = ctx.op("tensor", lambda e, sl=sl, k=k, ps=ps:
                                    e.matmul(ps[:, :], wslg[:, sl, k, :],
                                             xln[:, k, :], start=(k == 0),
                                             stop=(k == KD - 1)),
                                    "T" if k == KD - 1 else None)
                    if key == 0:
                        tvv = {}
                    tvv[key] = mm
                    ctx.wait("scalar", "T", mm)
                    if tag == 0:
                        ev[key] = ctx.op("scalar", lambda e, ft=ft, t0c=t0c, ps=ps,
                                         bias=bias: e.activation(
                                             x1b[:, ft, 1 + t0c:1 + t0c + CW],
                                             ps[:, :], AF.Identity,
                                             bias=bias[:, ft:ft + 1]), "S")
                    else:
                        ev[key] = ctx.op("scalar", lambda e, ft=ft, t0c=t0c, ps=ps,
                                         bias=bias: e.activation(
                                             vb[:, ft, t0c:t0c + CW],
                                             ps[:, :], AF.Silu,
                                             bias=bias[:, ft:ft + 1]), "S")
        ctx.wait("vector", "S", ev[2 * KD - 1])
        for ft in range(KD):
            if ft > 0:
                ctx.wait("vector", "S", cv)
            ctx.fop("vector", lambda e, ft=ft: e.tensor_scalar(
                cacc[:, :], x1b[:, ft, 0:2048], convw[:, ft, 0:1], None,
                op0=ALU.mult))
            for j in range(1, 3):
                ctx.fop("vector", lambda e, ft=ft, j=j: e.scalar_tensor_tensor(
                    cacc[:, :], x1b[:, ft, j:2048 + j], convw[:, ft, j:j + 1],
                    cacc[:, :], op0=ALU.mult, op1=ALU.add))
            ctx.wait("scalar", "V", ctx.ctr["V"])
            cv = ctx.op("scalar", lambda e, ft=ft: e.activation(
                xcb[:, ft, :], cacc[:, :], AF.Silu), "S")
        ctx.wait("vector", "S", cv)
        for cb in range(KD):
            for s_ in range(NS):
                av = ctx.fop("vector", lambda e, cb=cb, s_=s_: e.tensor_scalar(
                    abt[:, :], onest[:, :], Abar[:, cb, s_:s_ + 1], None,
                    op0=ALU.mult))
                ctx.wait("scalar", "V", av)
                sb_ = ctx.op("scalar", lambda e, cb=cb, s_=s_: e.activation(
                    bt[:, :], xcb[:, cb, :], AF.Identity,
                    scale=Bc[:, cb, s_:s_ + 1]), "S")
                ctx.wait("vector", "S", sb_)
                sc_ = ctx.fop("vector", lambda e: e.tensor_tensor_scan(
                    st2[:, :], abt[:, :], bt[:, :], 0.0,
                    op0=ALU.mult, op1=ALU.add))
                ctx.wait("scalar", "V", sc_)
                hv = ctx.op("scalar", lambda e, cb=cb, s_=s_: e.activation(
                    hCt[:, :], st2[:, :], AF.Identity,
                    scale=Cc[:, cb, s_:s_ + 1]), "S")
                ctx.wait("tensor", "S", hv)
                for q in range(4):
                    mm = ctx.op("tensor", lambda e, q=q, s_=s_: e.matmul(
                        psY[:, q * 512:(q + 1) * 512], com["ident"][:, :],
                        hCt[:, q * 512:(q + 1) * 512],
                        start=(s_ == 0), stop=(s_ == NS - 1)),
                        "T" if (s_ == NS - 1 and q == 3) else None)
                if s_ < NS - 1:
                    ctx.wait("vector", "T", ctx.ctr["T"])
            ctx.wait("vector", "T", mm)
            ctx.fop("vector", lambda e, cb=cb: e.scalar_tensor_tensor(
                yt1[:, :], xcb[:, cb, :], Dc[:, cb:cb + 1], psY[:, 0:2048],
                op0=ALU.mult, op1=ALU.add))
            gv = ctx.fop("vector", lambda e, cb=cb: e.tensor_mul(
                gout[:, cb, 0:2048], yt1[:, :], vb[:, cb, :]))
            ctx.wait("tensor", "V", gv)
        for c in range(NC_PER_B):
            t0c = c * CW
            ctx.join("sync", ["V", "O"])
            a_done = ctx.dma("sync", xf[:, :, :], xview(xin, b, t0c, CW), "A")
            ctx.wait("vector", "A", a_done)
            for oft in range(KD):
                ps = psB if oft % 2 == 0 else psC
                sl = oft % 2
                if oft >= 2:
                    ctx.wait("tensor", "V", oev[oft - 2])
                    ctx.wait("gpsimd", "V", oev[oft - 2])
                wtok = ctx.dma("gpsimd", wslg[:, sl, :, :],
                               wd["w2_wT"].rearrange("(k p) f -> p k f", p=128)
                               [:, :, oft * 128:(oft + 1) * 128], "W")
                ctx.wait("tensor", wtok)
                for k in range(KD):
                    mm = ctx.op("tensor", lambda e, sl=sl, k=k, ps=ps, t0c=t0c:
                                e.matmul(ps[:, :], wslg[:, sl, k, :],
                                         gout[:, k, t0c:t0c + CW],
                                         start=(k == 0), stop=(k == KD - 1)),
                                "T" if k == KD - 1 else None)
                ctx.wait("vector", "T", mm)
                if oft == 0:
                    oev = {}
                oev[oft] = ctx.fop("vector", lambda e, oft=oft, ps=ps:
                                   e.scalar_tensor_tensor(
                                       xf[:, oft, :], ps[:, :],
                                       w2b[:, oft:oft + 1], xf[:, oft, :],
                                       op0=ALU.add, op1=ALU.add))
            ctx.join("sync", ["V"])
            st = ctx.dma("sync", xview(xout, b, t0c, CW), xf[:, :, :], "O")
    return None




def _l0ffn_consts(ctx):
    DH, WIN, N = 64, 512, 2048
    half = np.arange(0, DH, 2, dtype=np.float64)
    inv = 1.0 / 10000.0 ** (half / DH)
    t = np.arange(N, dtype=np.float64)
    fr = t[:, None] * inv[None, :]
    fr = np.concatenate([fr, fr], axis=1)
    base = (half + 0.4 * DH) / (1.4 * DH)
    power = (t - N // 2) / float(WIN // 2)
    sc = base[None, :] ** power[:, None]
    sc = np.concatenate([sc, sc], axis=1)
    cq = (np.cos(fr) * sc * 8.0).T
    sq = (np.sin(fr) * sc * 8.0).T
    ck = (np.cos(fr) / sc).T
    sk = (np.sin(fr) / sc).T
    def dup(a):
        return np.concatenate([a, a], axis=0).astype(bf16)
    c = {}
    c["cq"] = ctx.const("ro_cq", dup(cq))
    c["sq"] = ctx.const("ro_sq", dup(sq))
    c["ck"] = ctx.const("ro_ck", dup(ck))
    c["sk"] = ctx.const("ro_sk", dup(sk))
    rot = np.zeros((128, 128), dtype=bf16)
    for blk in range(2):
        o = blk * 64
        for m in range(32):
            rot[o + m + 32, o + m] = -1
            rot[o + m, o + m + 32] = 1
    c["rot"] = ctx.const("ro_rot", rot)
    NEG = -1e30
    mA = np.zeros((4, 128, 512), dtype=np.float32)
    mB = np.zeros((4, 128, 512), dtype=np.float32)
    for qt in range(4):
        for qi in range(128):
            mA[qt, qi, :] = NEG
            lo = 128 * qt + qi + 1
            if lo < 512:
                mA[qt, qi, lo:] = 0.0
            mB[qt, qi, :] = NEG
            mB[qt, qi, :128 * qt + qi + 1] = 0.0
    mT = np.zeros((8, 128, 512), dtype=np.float32)
    for rel in range(8):
        for ki in range(128):
            d0 = 128 * (rel - 4) + ki
            mT[rel, ki, :] = NEG
            lo = max(0, d0)
            hi = min(512, d0 + 512)
            if lo < hi:
                mT[rel, ki, lo:hi] = 0.0
    c["mA"] = ctx.const("ffn_mA", np.ascontiguousarray(
        mA.transpose(1, 0, 2)).astype(bf16))
    c["mB"] = ctx.const("ffn_mB", np.ascontiguousarray(
        mB.transpose(1, 0, 2)).astype(bf16))
    c["mT"] = ctx.const("ffn_mT", np.ascontiguousarray(
        mT.transpose(1, 0, 2)).astype(bf16))
    return c


FFN_DEBUG = None


def pass_l0ffn(ctx, com, xin, xout, wd):
    nc = ctx.nc
    H, DH, WIN = 8, 64, 512
    cst = _l0ffn_consts(ctx)
    rotc = ctx.sb("rotc", [128, 128], BF16)
    ctx.dma("gpsimd", rotc[:, :], cst["rot"], "W")
    mAb = ctx.sb("mAb", [128, 4, 512], BF16)
    mBb = ctx.sb("mBb", [128, 4, 512], BF16)
    mTb = ctx.sb("mTb", [128, 8, 512], BF16)
    ctx.dma("gpsimd", mAb[:, :, :], cst["mA"], "W")
    ctx.dma("gpsimd", mBb[:, :, :], cst["mB"], "W")
    ctx.dma("gpsimd", mTb[:, :, :], cst["mT"], "W")

    qsc = ctx.sb("qsc", [128, 2], F32)
    ksc = ctx.sb("ksc", [128, 2], F32)
    ctx.dma("gpsimd", qsc[:, :], wd["qs_c"], "W")
    ctx.dma("gpsimd", ksc[:, :], wd["ks_c"], "W")
    lgc = ctx.sb("lgc", [128, 2, 8], F32)
    lbc = ctx.sb("lbc", [128, 2, 8], F32)
    ctx.dma("gpsimd", lgc[:, :, :], wd["ln_g_c"], "W")
    ctx.dma("gpsimd", lbc[:, :, :], wd["ln_b_c"], "W")
    b1c = ctx.sb("b1c", [128, 2, 44], F32)
    ctx.dma("gpsimd", b1c[:, :, :], wd["b1p_c"], "W")
    b2c = ctx.sb("b2c", [128, 2, 8], F32)
    ctx.dma("gpsimd", b2c[:, :, :], wd["b2_c"], "W")
    wqslab = ctx.sb("wqslab", [128, 2, KD, 128], BF16)
    wvslab = ctx.sb("wvslab", [128, KD, 512], BF16)
    wo = ctx.sb("wo_ffn", [128, 4, 1024], BF16)
    for eng in ("tensor", "vector", "scalar"):
        ctx.join(eng, ["W"])

    tcq = ctx.sb("tcq", [128, T], BF16)
    tsq = ctx.sb("tsq", [128, T], BF16)
    tck = ctx.sb("tck", [128, T], BF16)
    tsk = ctx.sb("tsk", [128, T], BF16)
    ctx.dma("gpsimd", tcq[:, :], cst["cq"], "W")
    ctx.dma("gpsimd", tsq[:, :], cst["sq"], "W")
    ctx.dma("gpsimd", tck[:, :], cst["ck"], "W")
    ctx.dma("gpsimd", tsk[:, :], cst["sk"], "W")

    psA = ctx.ps("psA_f", [128, 1024])
    psB = ctx.ps("psB_f", [128, 512])
    psC = ctx.ps("psC_f", [128, 512])
    psC2 = ctx.ps("psC2_f", [128, 512])
    psD = ctx.ps("psD_f", [128, 512])
    psE = ctx.ps("psE_f", [128, 512])
    psE2 = ctx.ps("psE2_f", [128, 512])

    sA = nc.dram_tensor(f"sA_{ctx.uid()}", [NB, D, T], F32, kind="Internal").ap()
    sB = nc.dram_tensor(f"sB_{ctx.uid()}", [NB, D, T], F32, kind="Internal").ap()

    xf = ctx.sb("xf_f", [128, KD, CW], F32)
    xcb = ctx.sb("xcb_f", [128, KD, 513], BF16)
    qpre = ctx.sb("qpre", [128, 4, T], BF16)
    hlnb = qpre[:, 0:2, :].rearrange("p a (c d) -> p (a c) d", c=4, d=512)
    kpre = ctx.sb("kpre", [128, 4, T], BF16)
    vtm = ctx.sb("vtm_f", [128, 16, H, 65], BF16)
    sqb = ctx.sb("sqb_f", [128, 512], BF16)
    qhat = ctx.sb("qhat", [128, 512], BF16)
    tmpA = ctx.sb("tmpA_f", [128, 512], BF16)
    rrq = ctx.sb("rrq_f", [1, 512], BF16)
    Etb = ctx.sb("Etb_f", [128, 2, 512], BF16)
    negm = ctx.sb("negm_f", [128, 4], F32)
    mx2 = ctx.sb("mx2_f", [128, 1], F32)
    negmc = ctx.sb("negmc_f", [128, 1], BF16)
    negrow = ctx.sb("negrow_f", [1, 512], BF16)
    rrowb = ctx.sb("rrowb_f", [65, 512], BF16)
    attnh = ctx.sb("attnh_f", [64, 512], BF16)
    attv = ctx.sb("attv_f", [64, 512], BF16)
    aout = ctx.sb("aout_f", [128, 4, 512], BF16)
    sq2 = ctx.sb("sq2_f", [128, 2, 512], BF16)
    w1sl = ctx.sb("w1sl", [128, 2, KD, 256], BF16)
    w2sl = ctx.sb("w2sl", [128, 2, 22, 128], BF16)
    gact = ctx.sb("gact", [128, 512], BF16)
    tmpB = gact
    lnt2b = ctx.sb("lnt2b", [128, 512], F32)
    hidden = ctx.sb("hid_f", [128, 22, 512], BF16)
    eps24 = ctx.fconst(1e-24)
    ctx.join("scalar", ["W"])

    pc = ctx.dma("sync", xout, xin, "O")
    for b in range(NB):
        for c in range(NC_PER_B):
            t0c = c * CW
            lnv, lne = emit_ln_chunk(ctx, com, xf, hlnb, psA[:, 0:512],
                                     psA[:, 512:1024],
                                     src_view=xview(xin, b, t0c, CW),
                                     out_f32=xf)
            ctx.join("sync", ["V"])
            ctx.dma("sync", xview(sA, b, t0c, CW), xf[:, :, :], "O")

    for layer in range(2):
        if FFN_DEBUG == "LN0" and layer == 0:
            ctx.join("sync", ["O"])
            ctx.dma("sync", xout, sA, "O")
            return None
        if FFN_DEBUG == "B0" and layer == 1:
            ctx.join("sync", ["O"])
            ctx.dma("sync", xout, sA, "O")
            return None
        ctx.barrier()
        ctx.join("gpsimd", ["T", "V", "S"])
        wv2 = ctx.dma("gpsimd", wo[:, :, :],
                      wd["wo"].rearrange("l (k p) f -> l p k f", p=128)[layer],
                      "W")
        ctx.wait("tensor", wv2)
        for b in range(NB):
            ctx.barrier()
            od = ctx.op("vector", lambda e: e.memset(vtm[:, :, :, 64:65], 1.0), "V")
            ctx.wait("tensor", "V", od)
            for c in range(NC_PER_B):
                t0c = c * CW
                ctx.join("gpsimd", ["T", "S", "V"])
                if c == 0:
                    hz = ctx.op("vector", lambda e: e.memset(xcb[:, :, 0:1], 0.0), "V")
                    ctx.wait("tensor", "V", hz)
                    ctx.wait("scalar", "V", hz)
                    xt = ctx.dma("gpsimd", xcb[:, :, 1:513],
                                 xview(sA, b, 0, CW), "A")
                else:
                    xt = ctx.dma("gpsimd", xcb[:, :, 0:513],
                                 xview(sA, b, t0c - 1, CW + 1), "A")
                ctx.wait("tensor", xt)
                sev = {}
                qkmm = {}
                for ft in range(8):
                    ps = psB if ft % 2 == 0 else psC
                    sl = ft % 2
                    if ft >= 2:
                        ctx.wait("tensor", "S", sev[ft - 2])
                        ctx.wait("gpsimd", "T", qkmm[ft - 2])
                    wtok = ctx.dma("gpsimd", wqslab[:, sl, :, :],
                                   wd["wqkv"].rearrange("l (k p) f -> l p k f", p=128)
                                   [layer, :, :, ft * 128:(ft + 1) * 128], "W")
                    ctx.wait("tensor", wtok)
                    for k in range(KD):
                        off = 1 if k < 4 else 0
                        mm = ctx.op("tensor", lambda e, sl=sl, k=k, ps=ps, off=off:
                                    e.matmul(ps[:, :], wqslab[:, sl, k, :],
                                             xcb[:, k, off:off + 512],
                                             start=(k == 0), stop=(k == KD - 1)),
                                    "T" if k == KD - 1 else None)
                    qkmm[ft] = mm
                    ctx.wait("scalar", "T", mm)
                    dstb = qpre if ft < 4 else kpre
                    jt = ft % 4
                    sev[ft] = ctx.op("scalar", lambda e, dstb=dstb, jt=jt, ps=ps,
                                     t0c=t0c: e.copy(
                                         dstb[:, jt, t0c:t0c + CW], ps[:, :]), "S")
                    sq_ev = ctx.op("scalar", lambda e, ps=ps: e.activation(
                        sqb[:, :], ps[:, :], AF.Square), "S")
                    (tc_, ts_) = (tcq, tsq) if ft < 4 else (tck, tsk)
                    gsc_ = qsc if ft < 4 else ksc
                    ctx.wait("tensor", "S", sq_ev)
                    for hh in range(2):
                        p0 = hh * 64
                        smm = ctx.op("tensor", lambda e, p0=p0: e.matmul(
                            psA[0:1, 0:512], com["ones_col"][p0:p0 + 64, 0:1],
                            sqb[p0:p0 + 64, :], start=True, stop=True), "T")
                        ctx.wait("scalar", "T", smm)
                        s1 = ctx.op("scalar", lambda e: e.activation(
                            com["ln_tmp"][0:1, :], psA[0:1, 0:512], AF.Ln,
                            bias=eps24[0:1, 0:1]), "S")
                        s1 = ctx.op("scalar", lambda e: e.activation(
                            rrq[0:1, :], com["ln_tmp"][0:1, :], AF.Exp, scale=-0.5), "S")
                        ctx.wait("tensor", "S", s1)
                        bmm = ctx.op("tensor", lambda e: e.matmul(
                            psE[0:64, :], com["ones_row"][0:1, 0:64],
                            rrq[0:1, :], start=True, stop=True), "T")
                        ctx.wait("vector", "T", bmm)
                        qv = ctx.fop("vector", lambda e, dstb=dstb, jt=jt, p0=p0,
                                     t0c=t0c, gsc_=gsc_, layer=layer: e.scalar_tensor_tensor(
                                         qhat[p0:p0 + 64, :],
                                         dstb[p0:p0 + 64, jt, t0c:t0c + CW],
                                         gsc_[p0:p0 + 64, layer:layer + 1],
                                         psE[0:64, :], op0=ALU.mult,
                                         op1=ALU.mult), lp=True)
                        ctx.wait("tensor", "V", qv)
                    rmm = ctx.op("tensor", lambda e: e.matmul(
                        psE2[:, :], rotc[:, :], qhat[:, :],
                        start=True, stop=True), "T")
                    ctx.wait("vector", "T", rmm)
                    ctx.op("vector", lambda e, tc_=tc_, t0c=t0c: e.tensor_tensor(
                        tmpA[:, :], qhat[:, :], tc_[:, t0c:t0c + CW],
                        op=ALU.mult), lp=True)
                    ctx.op("vector", lambda e, ts_=ts_, t0c=t0c: e.tensor_tensor(
                        tmpB[:, :], psE2[:, :], ts_[:, t0c:t0c + CW],
                        op=ALU.mult), lp=True)
                    rv = ctx.op("vector", lambda e, dstb=dstb, jt=jt, t0c=t0c:
                                e.tensor_tensor(dstb[:, jt, t0c:t0c + CW],
                                                tmpA[:, :], tmpB[:, :],
                                                op=ALU.add), "V", lp=True)
                    ctx.wait("tensor", "V", rv)
                    ctx.wait("scalar", "V", rv)
                ctx.join("gpsimd", ["T"])
                wtok = ctx.dma("gpsimd", wvslab[:, :, :],
                               wd["wqkv"].rearrange("l (k p) f -> l p k f", p=128)
                               [layer, :, :, 1024:1536], "W")
                ctx.wait("tensor", wtok)
                v_ev = None
                for tt in range(4):
                    gtt = c * 4 + tt
                    if v_ev is not None:
                        ctx.wait("tensor", "V", v_ev)
                    for k in range(KD):
                        off = 1 if k < 4 else 0
                        mm = ctx.op("tensor", lambda e, tt=tt, k=k, off=off:
                                    e.matmul(psD[:, :],
                                             xcb[:, k, off + tt * 128:off + (tt + 1) * 128],
                                             wvslab[:, k, :],
                                             start=(k == 0), stop=(k == KD - 1)),
                                    "T" if k == KD - 1 else None)
                    ctx.wait("vector", "T", mm)
                    v_ev = ctx.op("vector", lambda e, gtt=gtt: e.tensor_copy(
                        vtm[:, gtt, :, 0:64],
                        psD[:, :].rearrange("p (h d) -> p h d", d=64)), "V",
                        lp=True)
            ctx.barrier()
            for w in range(4):
                t0c = w * CW
                for h in range(H):
                    jt, p0 = h // 2, (h % 2) * 64
                    red_done = None
                    for qt in range(4):
                        qoff = t0c + qt * 128
                        if red_done is not None:
                            ctx.wait("tensor", "V", red_done)
                        if w > 0:
                            ctx.op("tensor", lambda e, qt=qt: e.matmul(
                                psA[:, 0:512], com["ident"][:, :], mAb[:, qt, :],
                                start=True, stop=False))
                            ctx.op("tensor", lambda e, jt=jt, p0=p0, qoff=qoff,
                                   t0c=t0c: e.matmul(
                                       psA[:, 0:512],
                                       qpre[p0:p0 + 64, jt, qoff:qoff + 128],
                                       kpre[p0:p0 + 64, jt, t0c - 512:t0c],
                                       start=False, stop=True))
                        ctx.op("tensor", lambda e, qt=qt: e.matmul(
                            psA[:, 512:1024], com["ident"][:, :], mBb[:, qt, :],
                            start=True, stop=False))
                        mm = ctx.op("tensor", lambda e, jt=jt, p0=p0, qoff=qoff,
                                    t0c=t0c: e.matmul(
                                        psA[:, 512:1024],
                                        qpre[p0:p0 + 64, jt, qoff:qoff + 128],
                                        kpre[p0:p0 + 64, jt, t0c:t0c + 512],
                                        start=False, stop=True), "T")
                        ctx.wait("vector", "T", mm)
                        rng_ = (0, 1024) if w > 0 else (512, 1024)
                        red_done = ctx.fop("vector", lambda e, qt=qt, rng_=rng_:
                                           e.reduce_max(negm[:, qt:qt + 1],
                                                        psA[:, rng_[0]:rng_[1]],
                                                        axis=AX.X, negate=True))
                        vv = ctx.op("vector", lambda e, qt=qt: e.tensor_copy(
                            negmc[:, 0:1], negm[:, qt:qt + 1]), "V")
                        ctx.wait("tensor", "V", vv)
                        tt_ = ctx.op("tensor", lambda e, qt=qt: e.matmul(
                            psB[0:1, qt * 128:(qt + 1) * 128], negmc[:, 0:1],
                            com["ident"][:, :], start=True, stop=True), "T")
                        red_done = vv
                    ctx.wait("scalar", "T", tt_)
                    se = ctx.op("scalar", lambda e: e.copy(negrow[0:1, :],
                                                           psB[0:1, 0:512]), "S")
                    ctx.wait("tensor", "S", se)
                    kts = list(range(max(0, 4 * (w - 1)), 4 * w + 4))
                    ev, av = {}, {}
                    for i, kt in enumerate(kts):
                        ps2 = psC if i % 2 == 0 else psC2
                        if i >= 2:
                            ctx.wait("tensor", "S", ev[i - 2])
                        if i >= 1:
                            ctx.wait("tensor", "S", ev[i - 1])
                            av[i - 1] = ctx.op("tensor", lambda e, h=h,
                                               pk=kts[i - 1], i=i: e.matmul(
                                                   psD[0:65, :], vtm[:, pk, h, :],
                                                   Etb[:, (i - 1) % 2, :],
                                                   start=(i - 1 == 0),
                                                   stop=False), "T")
                        rel = kt - 4 * (w - 1)
                        ctx.op("tensor", lambda e, ps2=ps2: e.matmul(
                            ps2[:, :], com["ones_row"][0:1, :], negrow[0:1, :],
                            start=True, stop=False))
                        ctx.op("tensor", lambda e, ps2=ps2, rel=rel: e.matmul(
                            ps2[:, :], com["ident"][:, :], mTb[:, rel, :],
                            start=False, stop=False))
                        mm = ctx.op("tensor", lambda e, jt=jt, p0=p0, kt=kt,
                                    ps2=ps2, t0c=t0c: e.matmul(
                                        ps2[:, :],
                                        kpre[p0:p0 + 64, jt, kt * 128:(kt + 1) * 128],
                                        qpre[p0:p0 + 64, jt, t0c:t0c + CW],
                                        start=False, stop=True), "T")
                        ctx.wait("scalar", "T", mm)
                        ev[i] = ctx.op("scalar", lambda e, i=i, ps2=ps2:
                                       e.activation(Etb[:, i % 2, :], ps2[:, :],
                                                    AF.Exp), "S")
                    nk = len(kts)
                    ctx.wait("tensor", "S", ev[nk - 1])
                    av[nk - 1] = ctx.op("tensor", lambda e, h=h, pk=kts[nk - 1],
                                        nk=nk: e.matmul(
                                            psD[0:65, :], vtm[:, pk, h, :],
                                            Etb[:, (nk - 1) % 2, :],
                                            start=(nk == 1), stop=True), "T")
                    ctx.wait("vector", "T", av[nk - 1])
                    rv = ctx.op("vector", lambda e: e.reciprocal(
                        rrowb[64:65, :], psD[64:65, :]), "V", lp=True)
                    ctx.wait("tensor", "V", rv)
                    tb = ctx.op("tensor", lambda e: e.matmul(
                        psE[0:64, :], com["ones_row"][64:65, 0:64],
                        rrowb[64:65, :], start=True, stop=True), "T")
                    ctx.wait("scalar", "T", tb)
                    sv = ctx.op("scalar", lambda e: e.copy(attnh[0:64, :],
                                                           psD[0:64, :]), "S")
                    ctx.wait("vector", "S", sv)
                    vv = ctx.op("vector", lambda e: e.tensor_tensor(
                        attv[0:64, :], attnh[0:64, :], psE[0:64, :],
                        op=ALU.mult), "V")
                    ctx.wait("tensor", "V", vv)
                    sh = com["shift0"] if h % 2 == 0 else com["shift1"]
                    pe2 = ctx.op("tensor", lambda e, sh=sh, h=h: e.matmul(
                        psE2[:, :], sh[0:64, :], attv[0:64, :],
                        start=(h % 2 == 0), stop=(h % 2 == 1)), "T")
                    if h % 2 == 1:
                        ctx.wait("scalar", "T", pe2)
                        sv2 = ctx.op("scalar", lambda e, jt=jt: e.copy(
                            aout[:, jt, :], psE2[:, :]), "S")
                        ctx.wait("tensor", "S", sv2)
                    ctx.wait("scalar", "V", ctx.ctr["V"])
                ctx.join("sync", ["V", "O"])
                a_done = ctx.dma("sync", xf[:, :, :], xview(sA, b, t0c, CW), "A")
                ctx.wait("vector", "A", a_done)
                ctx.join("tensor", ["S"])
                for oft in range(KD):
                    pso = psB if oft % 2 == 0 else psC
                    if oft >= 2:
                        ctx.wait("tensor", "V", oev[oft - 2])
                    for k in range(4):
                        mm = ctx.op("tensor", lambda e, oft=oft, k=k, pso=pso:
                                    e.matmul(pso[:, :],
                                             wo[:, k, oft * 128:(oft + 1) * 128],
                                             aout[:, k, :], start=(k == 0),
                                             stop=(k == 3)),
                                    "T" if k == 3 else None)
                    ctx.wait("vector", "T", mm)
                    if oft == 0:
                        oev = {}
                    oev[oft] = ctx.op("vector", lambda e, oft=oft, pso=pso:
                                      e.tensor_tensor(xf[:, oft, :],
                                                      xf[:, oft, :], pso[:, :],
                                                      op=ALU.add), "V")
                ctx.join("sync", ["V"])
                ctx.dma("sync", xview(sB, b, t0c, CW), xf[:, :, :], "O")
        if FFN_DEBUG == "A0":
            ctx.join("sync", ["O"])
            ctx.dma("sync", xout, sB, "O")
            return None
        ctx.barrier()
        for b in range(NB):
            for c in range(NC_PER_B):
                t0c = c * CW
                ctx.join("gpsimd", ["T", "S", "V"])
                if c == 0:
                    hz = ctx.op("vector", lambda e: e.memset(xcb[:, :, 0:1], 0.0), "V")
                    ctx.wait("tensor", "V", hz)
                    ctx.wait("scalar", "V", hz)
                    xt = ctx.dma("gpsimd", xcb[:, :, 1:513],
                                 xview(sB, b, 0, CW), "A")
                else:
                    xt = ctx.dma("gpsimd", xcb[:, :, 0:513],
                                 xview(sB, b, t0c - 1, CW + 1), "A")
                ctx.join("sync", ["V", "O"])
                xm = ctx.dma("sync", xf[:, :, :], xview(sB, b, t0c, CW), "A")
                ctx.wait("scalar", xt)
                ctx.wait("tensor", xt)
                ctx.join("tensor", ["V"])
                sqmm = {}
                for k in range(KD):
                    off = 1 if k < 4 else 0
                    if k >= 2:
                        ctx.wait("scalar", "T", sqmm[k - 2])
                    sq_ev = ctx.op("scalar", lambda e, k=k, off=off:
                                   e.activation(sq2[:, k % 2, :],
                                                xcb[:, k, off:off + 512],
                                                AF.Square), "S")
                    ctx.wait("tensor", "S", sq_ev)
                    ctx.op("tensor", lambda e, k=k, off=off: e.matmul(
                        psA[0:1, 0:512], com["ones_col"][:, 0:1],
                        xcb[:, k, off:off + 512],
                        start=(k == 0), stop=(k == KD - 1)))
                    t_done = ctx.op("tensor", lambda e, k=k: e.matmul(
                        psA[0:1, 512:1024],
                        com["ones_col"][:, 0:1], sq2[:, k % 2, :],
                        start=(k == 0), stop=(k == KD - 1)), "T")
                    sqmm[k] = t_done
                ctx.wait("vector", "T", t_done)
                nm, var = com["ln_nm"], com["ln_var"]
                msq, rstd = com["ln_t2"], com["ln_rstd"]
                ctx.op("vector", lambda e: e.tensor_scalar(
                    nm[:, 0:512], psA[0:1, 0:512], -1.0 / D, None, op0=ALU.mult))
                ctx.op("vector", lambda e: e.tensor_mul(msq[:, 0:512],
                                                        nm[:, 0:512], nm[:, 0:512]))
                v_done = ctx.op("vector", lambda e: e.scalar_tensor_tensor(
                    var[:, 0:512], psA[0:1, 512:1024], 1.0 / D, msq[:, 0:512],
                    op0=ALU.mult, op1=ALU.subtract), "V")
                ctx.wait("scalar", "V", v_done)
                epsc = ctx.fconst(EPS)
                ctx.op("scalar", lambda e: e.activation(
                    com["ln_t2"][:, 0:512], var[:, 0:512], AF.Ln,
                    bias=epsc[0:1, 0:1]))
                s2_ = ctx.op("scalar", lambda e: e.activation(
                    rstd[:, 0:512], com["ln_t2"][:, 0:512], AF.Exp, scale=-0.5), "S")
                ctx.wait("tensor", "S", s2_)
                ctx.join("tensor", ["V"])
                ctx.op("tensor", lambda e: e.matmul(
                    psA[:, 0:512], com["ones_row_f"][0:1, :], nm[0:1, 0:512],
                    start=True, stop=True))
                t2_ = ctx.op("tensor", lambda e: e.matmul(
                    psA[:, 512:1024], com["ones_row_f"][0:1, :],
                    rstd[0:1, 0:512], start=True, stop=True), "T")
                ctx.wait("vector", "T", t2_)
                hv = None
                for k in range(KD):
                    off = 1 if k < 4 else 0
                    if hv is not None:
                        ctx.wait("vector", "S", hv)
                    ctx.fop("vector", lambda e, k=k, off=off: e.tensor_tensor(
                        com["ln_tmp"][:, :], xcb[:, k, off:off + 512], psA[:, 0:512],
                        op=ALU.add))
                    ctx.fop("vector", lambda e, k=k, layer=layer: e.scalar_tensor_tensor(
                        lnt2b[:, :], com["ln_tmp"][:, :], lgc[:, layer, k:k + 1],
                        psA[:, 512:1024], op0=ALU.mult, op1=ALU.mult))
                    ctx.join("scalar", ["V"])
                    hv = ctx.op("scalar", lambda e, k=k, layer=layer: e.activation(
                        hlnb[:, k, :], lnt2b[:, :], AF.Identity,
                        bias=lbc[:, layer, k:k + 1]), "S")
                ctx.wait("tensor", "S", hv)
                hev = {}
                hq = None
                for tpair in range(22):
                    sl = tpair % 2
                    if tpair >= 2:
                        ctx.wait("gpsimd", "T", hev[tpair - 2])
                    wtok = ctx.dma("gpsimd", w1sl[:, sl, :, :],
                                   wd["w1p"].rearrange("l (k p) f -> l p k f", p=128)
                                   [layer, :, :, tpair * 256:(tpair + 1) * 256], "W")
                    ctx.wait("tensor", wtok)
                    if hq is not None:
                        ctx.wait("tensor", "V", hq)
                        ctx.wait("scalar", "V", hq)
                    for k in range(KD):
                        ctx.op("tensor", lambda e, sl=sl, k=k: e.matmul(
                            psB[:, :], w1sl[:, sl, k, 0:128], hlnb[:, k, :],
                            start=(k == 0), stop=(k == KD - 1)))
                    for k in range(KD):
                        mm = ctx.op("tensor", lambda e, sl=sl, k=k: e.matmul(
                            psC[:, :], w1sl[:, sl, k, 128:256], hlnb[:, k, :],
                            start=(k == 0), stop=(k == KD - 1)),
                            "T" if k == KD - 1 else None)
                    hev[tpair] = mm
                    ctx.wait("scalar", "T", mm)
                    ga = ctx.op("scalar", lambda e, tpair=tpair, layer=layer: e.activation(
                        gact[:, :], psC[:, :], AF.Gelu,
                        bias=b1c[:, layer, 2 * tpair + 1:2 * tpair + 2]), "S")
                    ctx.wait("vector", "S", ga)
                    hq = ctx.fop("vector", lambda e, tpair=tpair, layer=layer: e.scalar_tensor_tensor(
                        hidden[:, tpair, :], psB[:, :],
                        b1c[:, layer, 2 * tpair:2 * tpair + 1], gact[:, :],
                        op0=ALU.add, op1=ALU.mult))
                    ctx.wait("tensor", "V", hq)
                ctx.join("tensor", ["V"])
                for oft in range(KD):
                    slo = oft % 2
                    pso = psB if oft % 2 == 0 else psC
                    if oft >= 2:
                        ctx.wait("tensor", "V", oev2[oft - 2])
                        ctx.wait("gpsimd", "T", w2mm[oft - 2])
                    wtok = ctx.dma("gpsimd", w2sl[:, slo, :, :],
                                   wd["w2p"].rearrange("l (k p) f -> l p k f", p=128)
                                   [layer, :, :, oft * 128:(oft + 1) * 128], "W")
                    ctx.wait("tensor", wtok)
                    for k in range(22):
                        mm = ctx.op("tensor", lambda e, slo=slo, k=k, pso=pso:
                                    e.matmul(pso[:, :], w2sl[:, slo, k, :],
                                             hidden[:, k, :], start=(k == 0),
                                             stop=(k == 21)),
                                    "T" if k == 21 else None)
                    ctx.wait("vector", "T", mm)
                    if oft == 0:
                        oev2 = {}
                        w2mm = {}
                    w2mm[oft] = mm
                    oev2[oft] = ctx.fop("vector", lambda e, oft=oft, pso=pso, layer=layer:
                                        e.scalar_tensor_tensor(
                                            xf[:, oft, :], pso[:, :],
                                            b2c[:, layer, oft:oft + 1],
                                            xf[:, oft, :],
                                            op0=ALU.add, op1=ALU.add))
                if layer == 1:
                    ctx.join("gpsimd", ["V"])
                    ctx.wait("gpsimd", pc)
                    stream = "Og"
                    i = ctx.dma_ctr.get(stream, 0)
                    ctx.dma_ctr[stream] = i + 1
                    sub = f"{stream}{i % DMA_SLOTS}"
                    ctx.wait("gpsimd", sub, ctx.ctr.get(sub, 0))
                    ctx.op("gpsimd", lambda e, b=b, t0c=t0c: e.dma_start(
                        xview(xout, b, t0c, CW), xf[:, :, :],
                        accum_op=ALU.add), sub, 16)
                else:
                    ctx.join("sync", ["V"])
                    ctx.dma("sync", xview(sA, b, t0c, CW), xf[:, :, :], "O")
    return None


def _prep_inputs(inp):
    d = {}
    x = np.ascontiguousarray(np.transpose(inp["L0_attn_a_x"], (0, 2, 1)))
    d["x"] = x
    perm = ([h * 128 + j for h in range(16) for j in range(64)]
            + [h * 128 + 64 + j for h in range(16) for j in range(64)])
    d["l0aa.Wqk"] = np.ascontiguousarray(inp["L0_attn_a_Wqk"][:, perm])
    d["l0aa.Wv"] = inp["L0_attn_a_Wv"]
    d["l0aa.bqk_c"] = np.ascontiguousarray(
        inp["L0_attn_a_bqk"][perm].reshape(16, 128).T)
    d["l0aa.bv_r"] = inp["L0_attn_a_bv"].reshape(1, 1024)
    for k in ["W_dq", "W_uq", "W_dkv", "W_uk", "W_uv", "W_o"]:
        d[f"l1aa.{k}"] = inp[f"L1_attn_a_{k}"]
    d["l1moe.w12T"] = np.ascontiguousarray(
        inp["L1_ffn_w12"].transpose(0, 2, 1))
    d["l1moe.w3T"] = np.ascontiguousarray(
        inp["L1_ffn_w3"].transpose(0, 2, 1))
    d["l1moe.gate_wT"] = np.ascontiguousarray(inp["L1_ffn_gate_w"].T)
    d["l0ab.W_inT"] = np.ascontiguousarray(inp["L0_attn_b_W_in"].T)
    d["l0ab.W_xT"] = np.ascontiguousarray(inp["L0_attn_b_W_x"].T)
    d["l0ab.W_dtT"] = np.ascontiguousarray(inp["L0_attn_b_W_dt"].T)
    d["l0ab.W_outT"] = np.ascontiguousarray(inp["L0_attn_b_W_out"].T)
    d["l0ab.conv_w_c"] = np.ascontiguousarray(
        inp["L0_attn_b_conv_w"][:, 0, :].reshape(16, 128, 4).transpose(1, 0, 2))
    d["l0ab.conv_b_c"] = np.ascontiguousarray(
        inp["L0_attn_b_conv_b"].reshape(16, 128).T)
    d["l0ab.b_dt_c"] = np.ascontiguousarray(
        inp["L0_attn_b_b_dt"].reshape(16, 128).T)
    d["l0ab.A_log_c"] = np.ascontiguousarray(
        inp["L0_attn_b_A_log"].reshape(16, 128, 16).transpose(1, 0, 2))
    d["l0ab.D_c"] = np.ascontiguousarray(
        inp["L0_attn_b_D"].reshape(16, 128).T)
    d["l1ab.w1_wT"] = np.ascontiguousarray(inp["L1_attn_b_w1_w"].T)
    d["l1ab.v1_wT"] = np.ascontiguousarray(inp["L1_attn_b_v1_w"].T)
    d["l1ab.w2_wT"] = np.ascontiguousarray(inp["L1_attn_b_w2_w"].T)
    d["l1ab.w1_b_c"] = np.ascontiguousarray(
        inp["L1_attn_b_w1_b"].reshape(8, 128).T)
    d["l1ab.v1_b_c"] = np.ascontiguousarray(
        inp["L1_attn_b_v1_b"].reshape(8, 128).T)
    d["l1ab.w2_b_c"] = np.ascontiguousarray(
        inp["L1_attn_b_w2_b"].reshape(8, 128).T)
    d["l1ab.conv_w_c"] = np.ascontiguousarray(
        inp["L1_attn_b_conv_w"][:, 0, :].reshape(8, 128, 3).transpose(1, 0, 2))
    d["l1ab.A_c"] = np.ascontiguousarray(
        inp["L1_attn_b_A"].reshape(8, 128, 16).transpose(1, 0, 2))
    d["l1ab.B_c"] = np.ascontiguousarray(
        inp["L1_attn_b_B_ssm"].reshape(8, 128, 16).transpose(1, 0, 2))
    d["l1ab.C_c"] = np.ascontiguousarray(
        inp["L1_attn_b_C_ssm"].reshape(8, 128, 16).transpose(1, 0, 2))
    d["l1ab.D_c"] = np.ascontiguousarray(
        inp["L1_attn_b_D_ssm"].reshape(8, 128).T)
    d["l0ffn.wqkv"] = inp["L0_ffn_wqkv"]
    d["l0ffn.wo"] = inp["L0_ffn_wo"]
    qs = inp["L0_ffn_q_scale"]
    ks = inp["L0_ffn_k_scale"]
    d["l0ffn.qs_c"] = np.ascontiguousarray(
        np.concatenate([qs.T, qs.T], axis=0))
    d["l0ffn.ks_c"] = np.ascontiguousarray(
        np.concatenate([ks.T, ks.T], axis=0))
    d["l0ffn.ln_g_c"] = np.ascontiguousarray(
        inp["L0_ffn_ln_g"].reshape(2, 8, 128).transpose(2, 0, 1))
    d["l0ffn.ln_b_c"] = np.ascontiguousarray(
        inp["L0_ffn_ln_b"].reshape(2, 8, 128).transpose(2, 0, 1))
    w1 = inp["L0_ffn_w1"]
    b1 = inp["L0_ffn_b1"]
    HID = 2730
    w1p = np.zeros((2, 1024, 5632), dtype=np.float32)
    b1p = np.zeros((128, 2, 44), dtype=np.float32)
    for t in range(22):
        lo, hi = 128 * t, min(128 * (t + 1), HID)
        n = hi - lo
        w1p[:, :, 256 * t:256 * t + n] = w1[:, :, lo:hi]
        w1p[:, :, 256 * t + 128:256 * t + 128 + n] = w1[:, :, HID + lo:HID + hi]
        b1p[:n, :, 2 * t] = b1[:, lo:hi].T
        b1p[:n, :, 2 * t + 1] = b1[:, HID + lo:HID + hi].T
    d["l0ffn.w1p"] = w1p
    d["l0ffn.b1p_c"] = np.ascontiguousarray(b1p)
    d["l0ffn.w2p"] = np.ascontiguousarray(
        np.pad(inp["L0_ffn_w2"], ((0, 0), (0, 86), (0, 0))))
    d["l0ffn.b2_c"] = np.ascontiguousarray(
        inp["L0_ffn_b2"].reshape(2, 8, 128).transpose(2, 0, 1))
    return d


PASS_BUILDERS = {}


def _register():
    PASS_BUILDERS["l0aa"] = (pass_l0aa,
                             ["Wqk", "Wv", "bqk_c", "bv_r"])
    PASS_BUILDERS["l1aa"] = (pass_l1aa,
                             ["W_dq", "W_uq", "W_dkv", "W_uk", "W_uv", "W_o"])
    PASS_BUILDERS["l1moe"] = (pass_l1moe, ["w12T", "w3T", "gate_wT"])
    PASS_BUILDERS["l0ab"] = (pass_l0ab,
                             ["W_inT", "W_xT", "W_dtT", "W_outT", "conv_w_c",
                              "conv_b_c", "b_dt_c", "A_log_c", "D_c"])
    PASS_BUILDERS["l0ffn"] = (pass_l0ffn,
                              ["wqkv", "wo", "qs_c", "ks_c", "ln_g_c", "ln_b_c",
                               "w1p", "b1p_c", "w2p", "b2_c"])
    PASS_BUILDERS["l1ab"] = (pass_l1ab,
                             ["w1_wT", "v1_wT", "w2_wT", "w1_b_c", "v1_b_c",
                              "w2_b_c", "conv_w_c", "A_c", "B_c", "C_c", "D_c"])


_register()


def build_single_pass(pass_name, d):
    nc = bass.Bass("TRN2", target_bir_lowering=False, debug=False)
    xin = nc.dram_tensor("x_in", [NB, D, T], F32, kind="ExternalInput").ap()
    xout = nc.dram_tensor("x_out", [NB, D, T], F32, kind="ExternalOutput").ap()
    ctx = Ctx(nc)
    com = load_common(ctx)
    fn, wkeys = PASS_BUILDERS[pass_name]
    wd = {}
    for k in wkeys:
        arr = d[f"{pass_name}.{k}"]
        wd[k] = nc.dram_tensor(f"{pass_name}_{k}", list(arr.shape), F32,
                               kind="ExternalInput").ap()
    fn(ctx, com, xin, xout, wd)
    for s in list(ctx.dma_ctr):
        for i in range(DMA_SLOTS):
            ctx.wait("sync", f"{s}{i}", ctx.ctr.get(f"{s}{i}", 0))
    ctx.finalize()
    return nc


def _in_map_for(pass_name, d, xcur):
    m = {"x_in": xcur}
    pref = pass_name + "."
    for k, v in d.items():
        if k.startswith(pref):
            m[pass_name + "_" + k[len(pref):]] = np.ascontiguousarray(
                v, dtype=np.float32)
    return m


PASS_ORDER = ["l0aa", "l0ffn", "l0ab", "l0ffn", "l1aa", "l1moe", "l1ab",
              "l1moe"]


def build_fused(d):
    import contextlib
    nc = bass.Bass("TRN2", target_bir_lowering=False, debug=False)
    xin = nc.dram_tensor("x_in", [NB, D, T], F32, kind="ExternalInput").ap()
    xout = nc.dram_tensor("x_out", [NB, D, T], F32, kind="ExternalOutput").ap()
    s0 = nc.dram_tensor("chain0", [NB, D, T], F32, kind="Internal").ap()
    s1 = nc.dram_tensor("chain1", [NB, D, T], F32, kind="Internal").ap()
    ctx = Ctx(nc)
    com = load_common(ctx)
    wds = {}
    for pname in set(PASS_ORDER):
        fn, wkeys = PASS_BUILDERS[pname]
        wd = {}
        for k in wkeys:
            arr = d[f"{pname}.{k}"]
            wd[k] = nc.dram_tensor(f"{pname}_{k}", list(arr.shape), F32,
                                   kind="ExternalInput").ap()
        wds[pname] = wd
    n = len(PASS_ORDER)
    for i, pname in enumerate(PASS_ORDER):
        src_ap = xin if i == 0 else (s0 if i % 2 == 1 else s1)
        dst_ap = xout if i == n - 1 else (s0 if i % 2 == 0 else s1)
        fn, _ = PASS_BUILDERS[pname]
        ctx._es = contextlib.ExitStack()
        with ctx._es:
            fn(ctx, com, src_ap, dst_ap, wds[pname])
            ctx.full_barrier()
        ctx._es = None
    for s in list(ctx.dma_ctr):
        for j in range(DMA_SLOTS):
            ctx.wait("sync", f"{s}{j}", ctx.ctr.get(f"{s}{j}", 0))
    ctx.finalize()
    return nc


def _in_map_fused(d, xcur):
    m = {"x_in": xcur}
    for pname in set(PASS_ORDER):
        _, wkeys = PASS_BUILDERS[pname]
        for k in wkeys:
            m[f"{pname}_{k}"] = np.ascontiguousarray(d[f"{pname}.{k}"],
                                                     dtype=np.float32)
    return m


def kernel(**inputs):
    d = _prep_inputs(inputs)
    x = d["x"]
    shards = [np.ascontiguousarray(x[2 * i:2 * i + 2]) for i in range(8)]
    nc = build_fused(d)
    in_maps = [_in_map_fused(d, s) for s in shards]
    res = run_bass_kernel_spmd(nc, in_maps, core_ids=list(range(8)))
    shards = [np.asarray(r["x_out"]) for r in res.results]
    y = np.concatenate(shards, axis=0)
    return np.ascontiguousarray(np.transpose(y, (0, 2, 1)))


# version 07
# speedup vs baseline: 1.0037x; 1.0037x over previous
import numpy as np
import ml_dtypes
import concourse.bass as bass
import concourse.mybir as mybir
from concourse.bass_utils import run_bass_kernel_spmd

F32 = mybir.dt.float32
BF16 = mybir.dt.bfloat16
AF = mybir.ActivationFunctionType
ALU = mybir.AluOpType
AX = mybir.AxisListType
bf16 = ml_dtypes.bfloat16

NB = 2
T = 2048
D = 1024
KD = D // 128
CW = 512
NC_PER_B = T // CW
EPS = 1e-5

ENGINES = ("sync", "tensor", "vector", "scalar", "gpsimd")


DMA_SLOTS = 4


class Ctx:
    def __init__(self, nc):
        self.nc = nc
        self.code = {e: [] for e in ENGINES}
        self._sems = {}
        self.ctr = {}
        self.consts = {}
        self.dma_ctr = {}
        self.hwm = {}
        self._uid = 0

    def uid(self):
        self._uid += 1
        return self._uid

    def sem(self, name):
        if name not in self._sems:
            self._sems[name] = self.nc.alloc_semaphore(name="sem_" + name)
            self.ctr[name] = 0
        return self._sems[name]

    def wait(self, eng, name, val=None):
        if isinstance(val, tuple):
            name, val = val
        elif isinstance(name, tuple):
            name, val = name
        if val is None or val <= 0:
            return
        hw = self.hwm.setdefault(eng, {})
        if hw.get(name, -1) >= val:
            return
        hw[name] = val
        h = self.sem(name)
        self.code[eng].append(lambda e, h=h, v=val: e.wait_ge(h, v))

    def join(self, eng, sems):
        for s in sems:
            variants = [v for v in (s, s + "g") if v in self.dma_ctr]
            if variants:
                for v in variants:
                    for i in range(DMA_SLOTS):
                        self.wait(eng, f"{v}{i}", self.ctr.get(f"{v}{i}", 0))
            else:
                self.wait(eng, s, self.ctr.get(s, 0))

    def barrier(self, engs=("tensor", "vector", "scalar"),
                sems=("T", "V", "S")):
        for e in engs:
            self.join(e, [s for s in sems])

    def op(self, eng, fn, inc=None, n=1, lp=False):
        tgt = None
        nc = self.nc
        if lp:
            fn0 = fn

            def fn(e, fn0=fn0):
                with nc.allow_low_precision(reason="bf16 matmul-operand cast"):
                    return fn0(e)
        if inc is not None:
            h = self.sem(inc)
            self.ctr[inc] += n
            tgt = self.ctr[inc]

            def go(e, fn=fn, h=h, n=n):
                fn(e).then_inc(h, n)
        else:
            def go(e, fn=fn):
                fn(e)
        self.code[eng].append(go)
        return tgt

    def fop(self, eng, fn, lp=False):
        sem = {"vector": "V", "scalar": "S", "gpsimd": "G", "tensor": "T"}[eng]
        v = self.op(eng, fn, sem, lp=lp)
        self.wait(eng, sem, v)
        return v

    def dma(self, q, dst, src, stream):
        stream = f"{stream}{'g' if q == 'gpsimd' else ''}"
        i = self.dma_ctr.get(stream, 0)
        self.dma_ctr[stream] = i + 1
        sub = f"{stream}{i % DMA_SLOTS}"
        self.wait(q, sub, self.ctr.get(sub, 0))
        val = self.op(q, lambda e: e.dma_start(dst, src), sub, 16)
        return (sub, val)

    def sb(self, name, shape, dtype):
        if getattr(self, "_es", None) is not None:
            return self._es.enter_context(
                self.nc.sbuf_tensor(f"{name}_{self.uid()}", shape, dtype))
        return self.nc.alloc_sbuf_tensor(f"{name}_{self.uid()}", shape, dtype)

    def ps(self, name, shape):
        if getattr(self, "_es", None) is not None:
            return self._es.enter_context(
                self.nc.psum_tensor(f"{name}_{self.uid()}", shape, F32))
        return self.nc.alloc_psum_tensor(f"{name}_{self.uid()}", shape, F32)

    def full_barrier(self):
        sems = list(self.ctr.keys())
        for eng in ENGINES:
            for s in sems:
                self.wait(eng, s, self.ctr.get(s, 0))

    def fconst(self, val):
        key = f"f_{val!r}"
        if key not in self.consts:
            es = getattr(self, "_es", None)
            self._es = None
            t = self.sb("fc", [128, 1], F32)
            self._es = es
            arr = np.full((128, 1), val, dtype=np.float32)
            self.dma("gpsimd", t[:, :], self.const(key, arr), "W")
            self.consts[key] = ("sb", t)
        return self.consts[key][1]

    def const(self, name, arr):
        key = "c_" + name
        if key not in self.consts:
            self.consts[key] = self.nc.inline_tensor(arr, name=key).ap()
        return self.consts[key]

    def finalize(self):
        with self.nc.Block() as block:
            @block.sync
            def _(e):
                for fn in self.code["sync"]:
                    fn(e)

            @block.tensor
            def _(e):
                for fn in self.code["tensor"]:
                    fn(e)

            @block.vector
            def _(e):
                for fn in self.code["vector"]:
                    fn(e)

            @block.scalar
            def _(e):
                for fn in self.code["scalar"]:
                    fn(e)

            @block.gpsimd
            def _(e):
                for fn in self.code["gpsimd"]:
                    fn(e)
        return self.nc



def load_common(ctx):
    c = {}
    c["ident"] = ctx.sb("ident", [128, 128], BF16)
    c["ones_row"] = ctx.sb("ones_row", [128, 128], BF16)
    c["ones_row_f"] = ctx.sb("ones_row_f", [1, 128], F32)
    c["ones_col"] = ctx.sb("ones_col", [128, 1], BF16)
    c["ident_f"] = ctx.sb("ident_f", [128, 128], F32)
    ctx.dma("gpsimd", c["ident_f"][:, :], ctx.const("identf", np.eye(128, dtype=np.float32)), "W")
    ctx.dma("gpsimd", c["ident"][:, :], ctx.const("ident", np.eye(128).astype(bf16)), "W")
    ctx.dma("gpsimd", c["ones_row"][:, :], ctx.const("onesrow", np.ones((128, 128), dtype=bf16)), "W")
    c["shift0"] = ctx.sb("shift0", [64, 128], BF16)
    c["shift1"] = ctx.sb("shift1", [64, 128], BF16)
    sh0 = np.zeros((64, 128), dtype=bf16); sh0[np.arange(64), np.arange(64)] = 1
    sh1 = np.zeros((64, 128), dtype=bf16); sh1[np.arange(64), 64 + np.arange(64)] = 1
    ctx.dma("gpsimd", c["shift0"][:, :], ctx.const("shift0", sh0), "W")
    ctx.dma("gpsimd", c["shift1"][:, :], ctx.const("shift1", sh1), "W")
    ctx.dma("gpsimd", c["ones_row_f"][:, :], ctx.const("onesrowf", np.ones((1, 128), dtype=np.float32)), "W")
    c["w_done"] = ctx.dma("gpsimd", c["ones_col"][:, :], ctx.const("onescol", np.ones((128, 1), dtype=bf16)), "W")
    c["ln_xb"] = ctx.sb("ln_xb", [128, KD // 2, CW], BF16)
    c["ln_xq"] = ctx.sb("ln_xq", [128, KD // 2, CW], BF16)
    c["ln_nm"] = ctx.sb("ln_nm", [1, CW], F32)
    c["ln_var"] = ctx.sb("ln_var", [1, CW], F32)
    c["ln_t2"] = ctx.sb("ln_t2", [1, CW], F32)
    c["ln_rstd"] = ctx.sb("ln_rstd", [1, CW], F32)
    c["ln_tmp"] = ctx.sb("ln_tmp", [128, CW], F32)
    c["one_c"] = ctx.fconst(1.0)
    ctx.fconst(EPS)
    ctx.fconst(1e-24)
    for eng in ("tensor", "vector", "scalar", "sync"):
        ctx.join(eng, ["W"])
    return c


def emit_ln_chunk(ctx, com, xf, xln, psB1, psB2, width=CW, gain=None,
                  xf_ready=None, src_view=None, xf_guard=("S", "V"),
                  out_f32=None):
    if src_view is not None:
        for s in xf_guard:
            ctx.wait("sync", s, ctx.ctr.get(s, 0))
        xf_ready = ctx.dma("sync", xf[:, :, :], src_view, "A")
    if xf_ready is not None:
        ctx.wait("scalar", "A", xf_ready)
        ctx.wait("vector", "A", xf_ready)
    xb, xq = com["ln_xb"], com["ln_xq"]
    ctx.join("tensor", ["V"])
    H = KD // 2
    for r in range(2):
        if r == 1:
            ctx.wait("scalar", "T", ctx.ctr["T"])
        ctx.op("scalar", lambda e, r=r: e.activation(
            xb[:, :, 0:width], xf[:, r * H:(r + 1) * H, :], AF.Identity))
        s_done = ctx.op("scalar", lambda e, r=r: e.activation(
            xq[:, :, 0:width], xf[:, r * H:(r + 1) * H, :], AF.Square), "S")
        ctx.wait("tensor", "S", s_done)
        for k in range(H):
            ctx.op("tensor", lambda e, k=k, r=r: e.matmul(
                psB1[0:1, 0:width], com["ones_col"][:, 0:1], xb[:, k, 0:width],
                start=(r == 0 and k == 0), stop=(r == 1 and k == H - 1)))
        for k in range(H):
            t_done = ctx.op("tensor", lambda e, k=k, r=r: e.matmul(
                psB2[0:1, 0:width], com["ones_col"][:, 0:1], xq[:, k, 0:width],
                start=(r == 0 and k == 0), stop=(r == 1 and k == H - 1)), "T")
    ctx.wait("vector", "T", t_done)
    nm, var, msq = com["ln_nm"], com["ln_var"], com["ln_t2"]
    lnv, rstd, tmp = com["ln_t2"], com["ln_rstd"], com["ln_tmp"]
    ctx.op("vector", lambda e: e.tensor_scalar(nm[:, 0:width], psB1[0:1, 0:width],
                                               -1.0 / D, None, op0=ALU.mult))
    ctx.op("vector", lambda e: e.tensor_mul(msq[:, 0:width], nm[:, 0:width],
                                            nm[:, 0:width]))
    v_done = ctx.op("vector", lambda e: e.scalar_tensor_tensor(
        var[:, 0:width], psB2[0:1, 0:width], 1.0 / D, msq[:, 0:width],
        op0=ALU.mult, op1=ALU.subtract), "V")
    ctx.wait("scalar", "V", v_done)
    epsc = ctx.fconst(EPS)
    ctx.op("scalar", lambda e: e.activation(lnv[:, 0:width], var[:, 0:width],
                                            AF.Ln, bias=epsc[0:1, 0:1]))
    s2 = ctx.op("scalar", lambda e: e.activation(rstd[:, 0:width],
                                                 lnv[:, 0:width], AF.Exp,
                                                 scale=-0.5), "S")
    ctx.wait("tensor", "S", s2)
    ctx.join("tensor", ["V"])
    for j in range((width + 511) // 512):
        w = min(512, width - j * 512)
        ctx.op("tensor", lambda e, j=j, w=w: e.matmul(
            psB1[:, j * 512:j * 512 + w], com["ones_row_f"][0:1, :],
            nm[0:1, j * 512:j * 512 + w], start=True, stop=True))
        t2 = ctx.op("tensor", lambda e, j=j, w=w: e.matmul(
            psB2[:, j * 512:j * 512 + w], com["ones_row_f"][0:1, :],
            rstd[0:1, j * 512:j * 512 + w], start=True, stop=True), "T")
    ctx.wait("vector", "T", t2)
    out_val, out_sem = None, "V"
    for k in range(KD):
        ctx.op("vector", lambda e, k=k: e.tensor_tensor(
            tmp[:, 0:width], xf[:, k, :], psB1[:, 0:width], op=ALU.add))
        if out_f32 is not None:
            out_val = ctx.op("vector", lambda e, k=k: e.tensor_tensor(
                out_f32[:, k, 0:width], tmp[:, 0:width], psB2[:, 0:width],
                op=ALU.mult), "V")
        elif gain is None:
            out_val = ctx.op("vector", lambda e, k=k: e.tensor_tensor(
                xln[:, k, 0:width], tmp[:, 0:width], psB2[:, 0:width],
                op=ALU.mult), "V")
        else:
            gcol, bcol = gain
            ctx.op("vector", lambda e, k=k: e.scalar_tensor_tensor(
                tmp[:, 0:width], tmp[:, 0:width], gcol(k), psB2[:, 0:width],
                op0=ALU.mult, op1=ALU.mult), "V")
            ctx.join("scalar", ["V"])
            out_val = ctx.op("scalar", lambda e, k=k: e.activation(
                xln[:, k, 0:width], tmp[:, 0:width], AF.Identity,
                bias=bcol(k)), "S")
            out_sem = "S"
    return out_val, out_sem


def xview(xdram, b, t0, width):
    return xdram.rearrange("b (k p) t -> b p k t", p=128)[b, :, :, t0:t0 + width]



def pass_l0aa(ctx, com, xin, xout, wd):
    nc = ctx.nc
    wqk = ctx.sb("wqk", [128, KD, 2048], BF16)
    wv = ctx.sb("wv", [128, KD, 1024], BF16)
    bqkc = ctx.sb("bqkc", [128, 16], F32)
    bqks = ctx.sb("bqks", [128, 16], F32)
    bvr = ctx.sb("bvr", [1, 1024], F32)
    bvrs = bvr
    bvb = ctx.sb("bvb", [128, 1024], F32)
    ctx.dma("gpsimd", wqk[:, :, :], wd["Wqk"].rearrange("(k p) f -> p k f", p=128), "W")
    ctx.dma("gpsimd", wv[:, :, :], wd["Wv"].rearrange("(k p) f -> p k f", p=128), "W")
    ctx.dma("gpsimd", bqkc[:, :], wd["bqk_c"], "W")
    ctx.dma("gpsimd", bvr[:, :], wd["bv_r"], "W")
    ctx.join("scalar", ["W"])
    ctx.join("tensor", ["W"])
    ctx.op("scalar", lambda e: e.activation(bqks[:, :], bqkc[:, :], AF.Identity,
                                            scale=1.0 / 256.0))
    s0 = ctx.op("scalar", lambda e: e.activation(bvrs[:, :], bvr[:, :],
                                                 AF.Identity, scale=1.0 / 32.0), "S")
    ctx.wait("tensor", "S", s0)

    psA = ctx.ps("psA", [128, 1024])
    psB = ctx.ps("psB", [128, 512])
    psC = ctx.ps("psC", [128, 512])
    psC2 = ctx.ps("psC2", [128, 512])
    psD = ctx.ps("psD", [128, 512])
    psE = ctx.ps("psE", [128, 512])

    for j in range(2):
        t0_ = ctx.op("tensor", lambda e, j=j: e.matmul(
            psA[:, j * 512:(j + 1) * 512], com["ones_row_f"][0:1, :],
            bvrs[0:1, j * 512:(j + 1) * 512], start=True, stop=True), "T")
    ctx.wait("vector", "T", t0_)
    ctx.op("vector", lambda e: e.tensor_copy(bvb[:, :], psA[:, 0:1024]), "V")

    qb = ctx.sb("qb", [128, 8, T], BF16)
    kb = ctx.sb("kb", [128, 8, T], BF16)
    vtm = ctx.sb("vtm", [128, 16, 16, 65], BF16)
    Etb = ctx.sb("Etb", [128, 2, 512], BF16)
    xf = ctx.sb("xf_aa", [128, KD, CW], F32)
    xln = ctx.sb("xln_aa", [128, KD, CW], BF16)
    negm = ctx.sb("negm", [128, 2, 4], F32)
    mx2 = ctx.sb("mx2", [128, 2], F32)
    negmc = ctx.sb("negmc", [128, 2], BF16)
    negrow = ctx.sb("negrow", [1, 2, 512], BF16)
    rrowb = ctx.sb("rrowb", [65, 512], BF16)
    attnh = ctx.sb("attnh", [64, 512], BF16)
    attv = ctx.sb("attv", [64, 512], BF16)
    psE2 = ctx.ps("psE2", [128, 512])

    for b in range(NB):
        ctx.barrier()
        ones_done = ctx.op("vector", lambda e: e.memset(vtm[:, :, :, 64:65], 1.0), "V")
        ctx.wait("tensor", "V", ones_done)
        qk_ev = {}
        for c in range(NC_PER_B):
            t0c = c * CW
            lnv, lne = emit_ln_chunk(ctx, com, xf, xln, psA[:, 0:512],
                                     psA[:, 512:1024],
                                     src_view=xview(xin, b, t0c, CW))
            ctx.wait("tensor", lne, lnv)
            for ft in range(16):
                ps = psB if ft % 2 == 0 else psC
                if ft >= 2:
                    ctx.wait("tensor", "S", qk_ev[ft - 2])
                for k in range(KD):
                    mm = ctx.op("tensor", lambda e, ft=ft, k=k, ps=ps: e.matmul(
                        ps[:, :], wqk[:, k, ft * 128:(ft + 1) * 128],
                        xln[:, k, :], start=(k == 0), stop=(k == KD - 1)),
                        "T" if k == KD - 1 else None)
                ctx.wait("scalar", "T", mm)
                dstb = qb if ft < 8 else kb
                dstf = ft if ft < 8 else ft - 8
                qk_ev[ft] = ctx.op("scalar", lambda e, dstb=dstb, dstf=dstf,
                                   t0c=t0c, ft=ft, ps=ps:
                                   e.activation(dstb[:, dstf, t0c:t0c + CW],
                                                ps[:, :], AF.Identity,
                                                scale=1.0 / 256.0,
                                                bias=bqks[:, ft:ft + 1]), "S")
            v_ev = None
            for tt in range(4):
                gtt = c * 4 + tt
                for half in range(2):
                    if v_ev is not None:
                        ctx.wait("tensor", "V", v_ev)
                    for k in range(KD):
                        mm = ctx.op("tensor",
                                    lambda e, tt=tt, half=half, k=k: e.matmul(
                                        psD[:, :],
                                        xln[:, k, tt * 128:(tt + 1) * 128],
                                        wv[:, k, half * 512:(half + 1) * 512],
                                        start=(k == 0), stop=(k == KD - 1)),
                                    "T" if k == KD - 1 else None)
                    ctx.wait("vector", "T", mm)
                    v_ev = ctx.op("vector", lambda e, gtt=gtt, half=half:
                                  e.scalar_tensor_tensor(
                                      vtm[:, gtt, half * 8:(half + 1) * 8, 0:64],
                                      psD[:, :].rearrange("p (h d) -> p h d", d=64),
                                      1.0 / 32.0,
                                      bvb[:, half * 512:(half + 1) * 512]
                                      .rearrange("p (h d) -> p h d", d=64),
                                      op0=ALU.mult, op1=ALU.add), "V", lp=True)
        ctx.barrier()
        for c in range(NC_PER_B):
            t0c = c * CW
            ctx.join("sync", ["V", "O"])
            a_done = ctx.dma("sync", xf[:, :, :], xview(xin, b, t0c, CW), "A")
            ctx.wait("vector", "A", a_done)
            se_tok = {}

            def P1(h):
                j, p0 = h // 2, (h % 2) * 64
                par = h % 2
                psm = psB if par == 0 else psE
                red_done = None
                for qt in range(4):
                    qoff = t0c + qt * 128
                    for kc in range(4):
                        ps1 = psA[:, (kc % 2) * 512:(kc % 2) * 512 + 512]
                        if red_done is not None:
                            ctx.wait("tensor", "V", red_done - 1)
                        mm = ctx.op("tensor", lambda e, j=j, p0=p0, qoff=qoff,
                                    kc=kc, ps1=ps1: e.matmul(
                                        ps1, qb[p0:p0 + 64, j, qoff:qoff + 128],
                                        kb[p0:p0 + 64, j, kc * 512:(kc + 1) * 512],
                                        start=True, stop=True), "T")
                        ctx.wait("vector", "T", mm)
                        if kc == 0:
                            red_done = ctx.fop("vector", lambda e, qt=qt, par=par,
                                               ps1=ps1: e.reduce_max(
                                                   negm[:, par, qt:qt + 1], ps1,
                                                   axis=AX.X, negate=True))
                        else:
                            ctx.fop("vector", lambda e, par=par, ps1=ps1:
                                    e.reduce_max(mx2[:, par:par + 1], ps1,
                                                 axis=AX.X, negate=True))
                            red_done = ctx.op("vector", lambda e, qt=qt, par=par:
                                              e.tensor_tensor(
                                                  negm[:, par, qt:qt + 1],
                                                  negm[:, par, qt:qt + 1],
                                                  mx2[:, par:par + 1],
                                                  op=ALU.min), "V")
                    vv = ctx.op("vector", lambda e, qt=qt, par=par: e.tensor_copy(
                        negmc[:, par:par + 1], negm[:, par, qt:qt + 1]), "V")
                    ctx.wait("tensor", "V", vv)
                    tt_ = ctx.op("tensor", lambda e, qt=qt, par=par, psm=psm:
                                 e.matmul(
                                     psm[0:1, qt * 128:(qt + 1) * 128],
                                     negmc[:, par:par + 1],
                                     com["ident"][:, :], start=True, stop=True), "T")
                ctx.wait("scalar", "T", tt_)
                se_tok[h] = ctx.op("scalar", lambda e, par=par, psm=psm: e.copy(
                    negrow[0:1, par, :], psm[0:1, 0:512]), "S")

            def P2(h):
                j, p0 = h // 2, (h % 2) * 64
                par = h % 2
                ctx.wait("tensor", "S", se_tok[h])
                ev = {}
                av = {}
                for kt in range(16):
                    ps2 = psC if kt % 2 == 0 else psC2
                    if kt >= 2:
                        ctx.wait("tensor", "S", ev[kt - 2])
                    if kt >= 1:
                        ctx.wait("tensor", "S", ev[kt - 1])
                        av[kt - 1] = ctx.op("tensor", lambda e, h=h, kt=kt: e.matmul(
                            psD[0:65, :], vtm[:, kt - 1, h, :], Etb[:, (kt - 1) % 2, :],
                            start=(kt - 1 == 0), stop=False), "T")
                    ctx.op("tensor", lambda e, ps2=ps2, par=par: e.matmul(
                        ps2[:, :], com["ones_row"][0:1, :], negrow[0:1, par, :],
                        start=True, stop=False))
                    mm = ctx.op("tensor", lambda e, j=j, p0=p0, kt=kt, ps2=ps2,
                                t0c=t0c: e.matmul(
                                    ps2[:, :], kb[p0:p0 + 64, j, kt * 128:(kt + 1) * 128],
                                    qb[p0:p0 + 64, j, t0c:t0c + CW],
                                    start=False, stop=True), "T")
                    ctx.wait("scalar", "T", mm)
                    ev[kt] = ctx.op("scalar", lambda e, kt=kt, ps2=ps2: e.activation(
                        Etb[:, kt % 2, :], ps2[:, :], AF.Exp), "S")
                ctx.wait("tensor", "S", ev[15])
                av[15] = ctx.op("tensor", lambda e, h=h: e.matmul(
                    psD[0:65, :], vtm[:, 15, h, :], Etb[:, 15 % 2, :],
                    start=False, stop=True), "T")
                ctx.wait("vector", "T", av[15])
                rv = ctx.op("vector", lambda e: e.reciprocal(rrowb[64:65, :],
                                                             psD[64:65, :]), "V",
                            lp=True)
                ctx.wait("tensor", "V", rv)
                tb = ctx.op("tensor", lambda e: e.matmul(
                    psC[0:64, :], com["ones_row"][64:65, 0:64], rrowb[64:65, :],
                    start=True, stop=True), "T")
                ctx.wait("scalar", "T", tb)
                sv = ctx.op("scalar", lambda e: e.copy(attnh[0:64, :],
                                                       psD[0:64, :]), "S")
                ctx.wait("vector", "S", sv)
                vv = ctx.op("vector", lambda e: e.tensor_tensor(
                    attv[0:64, :], attnh[0:64, :], psC[0:64, :], op=ALU.mult), "V")
                ctx.wait("tensor", "V", vv)
                sh = com["shift0"] if h % 2 == 0 else com["shift1"]
                pe2 = ctx.op("tensor", lambda e, sh=sh, h=h: e.matmul(
                    psE2[:, :], sh[0:64, :], attv[0:64, :],
                    start=(h % 2 == 0), stop=(h % 2 == 1)), "T")
                if h % 2 == 1:
                    ctx.wait("vector", "T", pe2)
                    rr_ = ctx.op("vector", lambda e, j=j: e.tensor_tensor(
                        xf[:, j, :], xf[:, j, :], psE2[:, :], op=ALU.add), "V")
                    ctx.wait("tensor", "V", rr_)
                ctx.wait("scalar", "V", ctx.ctr["V"])

            P1(0)
            for h in range(16):
                if h + 1 < 16:
                    P1(h + 1)
                P2(h)
            ctx.join("sync", ["V"])
            st = ctx.dma("sync", xview(xout, b, t0c, CW), xf[:, :, :], "O")
    return None





def pass_l1aa(ctx, com, xin, xout, wd):
    DC = 256
    wdq = ctx.sb("wdq", [128, KD, DC], BF16)
    wdkv = ctx.sb("wdkv", [128, KD, DC], BF16)
    wuq = ctx.sb("wuq", [128, 2, 1024], BF16)
    wuk = ctx.sb("wuk", [128, 2, 1024], BF16)
    wuv = ctx.sb("wuv", [128, 2, 1024], BF16)
    wo = ctx.sb("wo_mla", [128, KD, 1024], BF16)
    ctx.dma("gpsimd", wdq[:, :, :], wd["W_dq"].rearrange("(k p) f -> p k f", p=128), "W")
    ctx.dma("gpsimd", wdkv[:, :, :], wd["W_dkv"].rearrange("(k p) f -> p k f", p=128), "W")
    ctx.dma("gpsimd", wuq[:, :, :], wd["W_uq"].rearrange("(k p) f -> p k f", p=128), "W")
    ctx.dma("gpsimd", wuk[:, :, :], wd["W_uk"].rearrange("(k p) f -> p k f", p=128), "W")
    ctx.dma("gpsimd", wuv[:, :, :], wd["W_uv"].rearrange("(k p) f -> p k f", p=128), "W")
    ctx.dma("gpsimd", wo[:, :, :], wd["W_o"].rearrange("(k p) f -> p k f", p=128), "W")
    ctx.join("tensor", ["W"])

    NEG = -1e30
    m1 = np.zeros((4, 128, 512), dtype=np.float32)
    for qt in range(4):
        for qi in range(128):
            m1[qt, qi, :] = NEG
            m1[qt, qi, :qt * 128 + qi + 1] = 0.0
    m2 = np.zeros((4, 128, 512), dtype=np.float32)
    for r in range(4):
        for ki in range(128):
            key = r * 128 + ki
            m2[r, ki, :] = NEG
            m2[r, ki, key:] = 0.0
    mb1 = ctx.sb("mb1", [128, 4, 512], BF16)
    mb2 = ctx.sb("mb2", [128, 4, 512], BF16)
    ctx.dma("gpsimd", mb1[:, :, :],
            ctx.const("mla_m1", np.ascontiguousarray(
                m1.transpose(1, 0, 2)).astype(bf16)), "W")
    ctx.dma("gpsimd", mb2[:, :, :],
            ctx.const("mla_m2", np.ascontiguousarray(
                m2.transpose(1, 0, 2)).astype(bf16)), "W")
    ctx.join("tensor", ["W"])

    psA = ctx.ps("psA2", [128, 1024])
    psB = ctx.ps("psB2", [128, 512])
    psC = ctx.ps("psC2a", [128, 512])
    psC2 = ctx.ps("psC2b", [128, 512])
    psD = ctx.ps("psD2", [128, 512])
    psE = ctx.ps("psE2a", [128, 512])
    psE2 = ctx.ps("psE2b", [128, 512])

    cqb = ctx.sb("cqb", [128, 2, CW], BF16)
    ckvb = ctx.sb("ckvb", [128, 2, CW], BF16)
    Qb = ctx.sb("Qb", [128, 8, T], BF16)
    Kb = ctx.sb("Kb", [128, 8, T], BF16)
    vtm = ctx.sb("vtm2", [128, 16, 16, 65], BF16)
    Etb = ctx.sb("Etb2", [128, 2, 512], BF16)
    xf = ctx.sb("xf_mla", [128, KD, CW], F32)
    xln = ctx.sb("xln_mla", [128, KD, CW], BF16)
    negm = ctx.sb("negm2", [128, 2, 4], F32)
    mx2 = ctx.sb("mx22", [128, 2], F32)
    negmc = ctx.sb("negmc2", [128, 2], BF16)
    negrow = ctx.sb("negrow2", [1, 2, 512], BF16)
    rrowb = ctx.sb("rrowb2", [65, 512], BF16)
    attnh = ctx.sb("attnh2", [64, 512], BF16)
    attv = ctx.sb("attv2", [64, 512], BF16)
    aout = ctx.sb("aout2", [128, 8, 512], BF16)

    for b in range(NB):
        ctx.barrier()
        ones_done = ctx.op("vector", lambda e: e.memset(vtm[:, :, :, 64:65], 1.0), "V")
        ctx.wait("tensor", "V", ones_done)
        ev_prev = {}
        for c in range(NC_PER_B):
            t0c = c * CW
            lnv, lne = emit_ln_chunk(ctx, com, xf, xln, psA[:, 0:512],
                                     psA[:, 512:1024],
                                     src_view=xview(xin, b, t0c, CW))
            ctx.wait("tensor", lne, lnv)
            ps_ev = {}
            for (wt, dstb, tag) in ((wdq, cqb, "cq"), (wdkv, ckvb, "ckv")):
                for ft in range(2):
                    ps = psB if ft % 2 == 0 else psC
                    if id(ps) in ps_ev:
                        ctx.wait("tensor", "S", ps_ev[id(ps)])
                    for k in range(KD):
                        mm = ctx.op("tensor", lambda e, wt=wt, ft=ft, k=k, ps=ps:
                                    e.matmul(ps[:, :], wt[:, k, ft * 128:(ft + 1) * 128],
                                             xln[:, k, :], start=(k == 0),
                                             stop=(k == KD - 1)),
                                    "T" if k == KD - 1 else None)
                    ctx.wait("scalar", "T", mm)
                    ev = ctx.op("scalar", lambda e, dstb=dstb, ft=ft, ps=ps:
                                e.activation(dstb[:, ft, 0:CW], ps[:, :],
                                             AF.Identity), "S")
                    ps_ev[id(ps)] = ev
            ctx.wait("tensor", "S", ev)
            for (srcb, wt, dstb, scl) in ((cqb, wuq, Qb, 0.125), (ckvb, wuk, Kb, 1.0)):
                for ft in range(8):
                    ps = psB if ft % 2 == 0 else psC
                    if id(ps) in ps_ev:
                        ctx.wait("tensor", "S", ps_ev[id(ps)])
                    for k in range(2):
                        mm = ctx.op("tensor", lambda e, srcb=srcb, wt=wt, ft=ft,
                                    k=k, ps=ps: e.matmul(
                                        ps[:, :], wt[:, k, ft * 128:(ft + 1) * 128],
                                        srcb[:, k, 0:CW], start=(k == 0),
                                        stop=(k == 1)), "T" if k == 1 else None)
                    ctx.wait("scalar", "T", mm)
                    ev = ctx.op("scalar", lambda e, dstb=dstb, ft=ft, t0c=t0c,
                                ps=ps, scl=scl: e.activation(
                                    dstb[:, ft, t0c:t0c + CW], ps[:, :],
                                    AF.Identity, scale=scl), "S")
                    ps_ev[id(ps)] = ev
            ctx.wait("tensor", "S", ev)
            v_ev = None
            for tt in range(4):
                gtt = c * 4 + tt
                for half in range(2):
                    if v_ev is not None:
                        ctx.wait("tensor", "V", v_ev)
                    for k in range(2):
                        mm = ctx.op("tensor", lambda e, tt=tt, half=half, k=k:
                                    e.matmul(
                                        psD[:, :],
                                        ckvb[:, k, tt * 128:(tt + 1) * 128],
                                        wuv[:, k, half * 512:(half + 1) * 512],
                                        start=(k == 0), stop=(k == 1)),
                                    "T" if k == 1 else None)
                    ctx.wait("vector", "T", mm)
                    v_ev = ctx.op("vector", lambda e, gtt=gtt, half=half:
                                  e.tensor_copy(
                                      vtm[:, gtt, half * 8:(half + 1) * 8, 0:64],
                                      psD[:, :].rearrange("p (h d) -> p h d", d=64)),
                                  "V", lp=True)
        ctx.barrier()
        for c in range(NC_PER_B):
            t0c = c * CW
            ctx.join("sync", ["V", "O"])
            a_done = ctx.dma("sync", xf[:, :, :], xview(xin, b, t0c, CW), "A")
            ctx.wait("vector", "A", a_done)
            ctx.join("tensor", ["V", "S"])
            se_tok = {}

            def P1(h):
                j, p0 = h // 2, (h % 2) * 64
                par = h % 2
                psm = psB if par == 0 else psE
                red_done = None
                for qt in range(4):
                    qoff = t0c + qt * 128
                    nkc = c + 1
                    for kc in range(nkc):
                        ps1 = psA[:, (kc % 2) * 512:(kc % 2) * 512 + 512]
                        if red_done is not None:
                            ctx.wait("tensor", "V", red_done - 1)
                        diag = (kc == c)
                        if diag:
                            ctx.op("tensor", lambda e, qt=qt, ps1=ps1: e.matmul(
                                ps1, com["ident"][:, :], mb1[:, qt, :],
                                start=True, stop=False))
                        mm = ctx.op("tensor", lambda e, j=j, p0=p0, qoff=qoff,
                                    kc=kc, ps1=ps1, diag=diag: e.matmul(
                                        ps1, Qb[p0:p0 + 64, j, qoff:qoff + 128],
                                        Kb[p0:p0 + 64, j, kc * 512:(kc + 1) * 512],
                                        start=not diag, stop=True), "T")
                        ctx.wait("vector", "T", mm)
                        if kc == 0:
                            red_done = ctx.fop("vector", lambda e, qt=qt, par=par,
                                               ps1=ps1: e.reduce_max(
                                                   negm[:, par, qt:qt + 1], ps1,
                                                   axis=AX.X, negate=True))
                        else:
                            ctx.fop("vector", lambda e, par=par, ps1=ps1:
                                    e.reduce_max(mx2[:, par:par + 1], ps1,
                                                 axis=AX.X, negate=True))
                            red_done = ctx.op("vector", lambda e, qt=qt, par=par:
                                              e.tensor_tensor(
                                                  negm[:, par, qt:qt + 1],
                                                  negm[:, par, qt:qt + 1],
                                                  mx2[:, par:par + 1],
                                                  op=ALU.min), "V")
                    vv = ctx.op("vector", lambda e, qt=qt, par=par: e.tensor_copy(
                        negmc[:, par:par + 1], negm[:, par, qt:qt + 1]), "V")
                    ctx.wait("tensor", "V", vv)
                    tt_ = ctx.op("tensor", lambda e, qt=qt, par=par, psm=psm:
                                 e.matmul(
                                     psm[0:1, qt * 128:(qt + 1) * 128],
                                     negmc[:, par:par + 1],
                                     com["ident"][:, :], start=True, stop=True), "T")
                ctx.wait("scalar", "T", tt_)
                se_tok[h] = ctx.op("scalar", lambda e, par=par, psm=psm: e.copy(
                    negrow[0:1, par, :], psm[0:1, 0:512]), "S")

            def P2(h):
                j, p0 = h // 2, (h % 2) * 64
                par = h % 2
                ctx.wait("tensor", "S", se_tok[h])
                nkt = 4 * (c + 1)
                ev = {}
                av = {}
                for kt in range(nkt):
                    ps2 = psC if kt % 2 == 0 else psC2
                    if kt >= 2:
                        ctx.wait("tensor", "S", ev[kt - 2])
                    if kt >= 1:
                        ctx.wait("tensor", "S", ev[kt - 1])
                        av[kt - 1] = ctx.op("tensor", lambda e, h=h, kt=kt: e.matmul(
                            psD[0:65, :], vtm[:, kt - 1, h, :], Etb[:, (kt - 1) % 2, :],
                            start=(kt - 1 == 0), stop=False), "T")
                    ctx.op("tensor", lambda e, ps2=ps2, par=par: e.matmul(
                        ps2[:, :], com["ones_row"][0:1, :], negrow[0:1, par, :],
                        start=True, stop=False))
                    if kt >= 4 * c:
                        ctx.op("tensor", lambda e, kt=kt, c=c, ps2=ps2: e.matmul(
                            ps2[:, :], com["ident"][:, :], mb2[:, kt - 4 * c, :],
                            start=False, stop=False))
                    mm = ctx.op("tensor", lambda e, j=j, p0=p0, kt=kt, ps2=ps2,
                                t0c=t0c: e.matmul(
                                    ps2[:, :], Kb[p0:p0 + 64, j, kt * 128:(kt + 1) * 128],
                                    Qb[p0:p0 + 64, j, t0c:t0c + CW],
                                    start=False, stop=True), "T")
                    ctx.wait("scalar", "T", mm)
                    ev[kt] = ctx.op("scalar", lambda e, kt=kt, ps2=ps2: e.activation(
                        Etb[:, kt % 2, :], ps2[:, :], AF.Exp), "S")
                ctx.wait("tensor", "S", ev[nkt - 1])
                av[nkt - 1] = ctx.op("tensor", lambda e, h=h, nkt=nkt: e.matmul(
                    psD[0:65, :], vtm[:, nkt - 1, h, :], Etb[:, (nkt - 1) % 2, :],
                    start=(nkt == 1), stop=True), "T")
                ctx.wait("vector", "T", av[nkt - 1])
                rv = ctx.op("vector", lambda e: e.reciprocal(rrowb[64:65, :],
                                                             psD[64:65, :]), "V",
                            lp=True)
                ctx.wait("tensor", "V", rv)
                tb = ctx.op("tensor", lambda e: e.matmul(
                    psC[0:64, :], com["ones_row"][64:65, 0:64], rrowb[64:65, :],
                    start=True, stop=True), "T")
                ctx.wait("scalar", "T", tb)
                sv = ctx.op("scalar", lambda e: e.copy(attnh[0:64, :],
                                                       psD[0:64, :]), "S")
                ctx.wait("vector", "S", sv)
                vv = ctx.op("vector", lambda e: e.tensor_tensor(
                    attv[0:64, :], attnh[0:64, :], psC[0:64, :], op=ALU.mult), "V")
                ctx.wait("tensor", "V", vv)
                sh = com["shift0"] if h % 2 == 0 else com["shift1"]
                pe2 = ctx.op("tensor", lambda e, sh=sh, h=h: e.matmul(
                    psE2[:, :], sh[0:64, :], attv[0:64, :],
                    start=(h % 2 == 0), stop=(h % 2 == 1)), "T")
                if h % 2 == 1:
                    ctx.wait("scalar", "T", pe2)
                    sv2 = ctx.op("scalar", lambda e, j=j: e.copy(
                        aout[:, j, :], psE2[:, :]), "S")
                    ctx.wait("tensor", "S", sv2)
                ctx.wait("scalar", "V", ctx.ctr["V"])

            P1(0)
            for h in range(16):
                if h + 1 < 16:
                    P1(h + 1)
                P2(h)
            ctx.join("tensor", ["S"])
            for ft in range(KD):
                ps = psB if ft % 2 == 0 else psC
                if ft >= 2:
                    ctx.wait("tensor", "V", wo_ev[ft - 2])
                for k in range(KD):
                    mm = ctx.op("tensor", lambda e, ft=ft, k=k, ps=ps: e.matmul(
                        ps[:, :], wo[:, k, ft * 128:(ft + 1) * 128],
                        aout[:, k, :], start=(k == 0), stop=(k == KD - 1)),
                        "T" if k == KD - 1 else None)
                ctx.wait("vector", "T", mm)
                if ft == 0:
                    wo_ev = {}
                wo_ev[ft] = ctx.op("vector", lambda e, ft=ft, ps=ps: e.tensor_tensor(
                    xf[:, ft, :], xf[:, ft, :], ps[:, :], op=ALU.add), "V")
            ctx.join("sync", ["V"])
            st = ctx.dma("sync", xview(xout, b, t0c, CW), xf[:, :, :], "O")
    return None




def pass_l1moe(ctx, com, xin, xout, wd):
    E, HID = 8, 2048
    gwt = ctx.sb("gwt", [128, KD, 8], F32)
    w_done = ctx.dma("gpsimd", gwt[:, :, :],
                     wd["gate_wT"].rearrange("(k p) e -> p k e", p=128), "W")
    ctx.wait("tensor", "W", w_done)
    sel8 = ctx.sb("sel8", [8, 8, 128], BF16)
    sel = np.zeros((8, 8, 128), dtype=bf16)
    for e in range(8):
        sel[e, e, :] = 1
    sel = np.ascontiguousarray(sel.transpose(1, 0, 2))
    wsel = ctx.dma("gpsimd", sel8[:, :, :], ctx.const("sel8", sel), "W")
    ctx.wait("tensor", "W", wsel)
    ones_col_f = ctx.fconst(1.0)
    ctx.join("tensor", ["W"])

    psH1a = ctx.ps("psH1a", [128, 512])
    psH1b = ctx.ps("psH1b", [128, 512])
    psH2a = ctx.ps("psH2a", [128, 512])
    psH2b = ctx.ps("psH2b", [128, 512])
    psY = ctx.ps("psY", [128, 512])
    psY2 = ctx.ps("psY2", [128, 512])
    if getattr(ctx, "_es", None) is not None:
        psT = ctx._es.enter_context(
            ctx.nc.psum_tensor(f"psT_{ctx.uid()}", [128, 136], BF16))
        psTf = ctx._es.enter_context(
            ctx.nc.psum_tensor(f"psTf_{ctx.uid()}", [128, 16], F32))
    else:
        psT = ctx.nc.alloc_psum_tensor(f"psT_{ctx.uid()}", [128, 136], BF16)
        psTf = ctx.nc.alloc_psum_tensor(f"psTf_{ctx.uid()}", [128, 16], F32)

    TB = 1024
    NCB = TB // CW
    xlnb = ctx.sb("xlnb", [128, KD, TB], BF16)
    xf = ctx.sb("xf_moe", [128, KD, CW], F32)
    gsc = ctx.sb("gsc", [8, TB], F32)
    gtm = ctx.sb("gtm", [128, 8, 8], F32)
    gsum = ctx.sb("gsum", [1, 8], F32)
    rsv = ctx.sb("rsv", [1, TB], F32)
    rtm = ctx.sb("rtm", [128, 1], F32)
    m1n = ctx.sb("m1n", [128, 1], F32)
    m2 = ctx.sb("m2", [128, 1], F32)
    eq = ctx.sb("eq", [128, 8], F32)
    msk = ctx.sb("msk", [128, 8], F32)
    ge = ctx.sb("ge", [128, 8], F32)
    expw = ctx.sb("expw", [128, 8], F32)
    d2 = ctx.sb("d2", [128, 1], F32)
    rr = ctx.sb("rr_moe", [128, 1], F32)
    dd = ctx.sb("dd_moe", [128, 1], F32)
    wexp = ctx.sb("wexp", [128, 8], F32)
    ptb = ctx.sb("ptb", [128, 8], BF16)
    pfm = ctx.sb("pfm", [8, TB], BF16)
    pbcs = ctx.sb("pbcs", [128, 2, 512], BF16)
    wsl = ctx.sb("wsl", [128, 2, KD, 2, 128], BF16)
    w3sb = ctx.sb("w3sb", [128, 16, 1024], BF16)
    sh1 = ctx.sb("sh1", [128, 512], BF16)
    th = ctx.sb("th", [128, 512], BF16)
    hidb = ctx.sb("hidb", [128, 16, TB], BF16)
    yac = ctx.sb("yac", [128, KD, TB], F32)

    for k in range(KD):
        gs_mm = ctx.op("tensor", lambda e, k=k: e.matmul(
            psTf[0:1, 8:16], ones_col_f[:, 0:1], gwt[:, k, :],
            start=(k == 0), stop=(k == KD - 1)), "T" if k == KD - 1 else None)
    ctx.wait("vector", "T", gs_mm)
    gsv = ctx.op("vector", lambda e: e.tensor_copy(gsum[0:1, :],
                                                   psTf[0:1, 8:16]), "V")
    ctx.wait("tensor", "V", gsv)

    for blk in range(NB * 2):
        b, toff = blk // 2, (blk % 2) * TB
        ctx.barrier()
        for c in range(NCB):
            lnv, lne = emit_ln_chunk(ctx, com, xf, xlnb[:, :, c * CW:(c + 1) * CW],
                                     psH1a[:, :], psH1b[:, :],
                                     src_view=xview(xin, b, toff + c * CW, CW),
                                     xf_guard=("S", "V", "T"))
            if c > 0:
                ctx.wait("tensor", "S", gv)
            for k in range(KD):
                ctx.op("tensor", lambda e, k=k: e.matmul(
                    psY[0:8, 0:512], gwt[:, k, :], xf[:, k, :],
                    start=(k == 0), stop=False))
            mm = ctx.op("tensor", lambda e: e.matmul(
                psY[0:8, 0:512], gsum[0:1, :], com["ln_nm"][0:1, 0:512],
                start=False, stop=True), "T")
            ctx.wait("scalar", "T", mm)
            gv = ctx.op("scalar", lambda e, c=c: e.copy(
                gsc[0:8, c * 512:(c + 1) * 512], psY[0:8, 0:512]), "S")
            rv_ = ctx.op("vector", lambda e, c=c: e.tensor_copy(
                rsv[0:1, c * 512:(c + 1) * 512], com["ln_rstd"][0:1, 0:512]), "V")
        ctx.wait("tensor", "S", gv)
        ctx.wait("tensor", "V", rv_)
        for tt in range(TB // 128):
            tr = ctx.op("tensor", lambda e, tt=tt: e.transpose(
                psTf[0:128, 0:8], gsc[0:8, tt * 128:(tt + 1) * 128],
                com["ident_f"][0:8, 0:8]), "T")
            tr = ctx.op("tensor", lambda e, tt=tt: e.transpose(
                psTf[0:128, 8:9], rsv[0:1, tt * 128:(tt + 1) * 128],
                com["ident_f"][0:1, 0:1]), "T")
            ctx.wait("vector", "T", tr)
            ctx.fop("vector", lambda e: e.tensor_copy(rtm[:, 0:1],
                                                      psTf[0:128, 8:9]))
            ctx.fop("vector", lambda e, tt=tt: e.tensor_scalar(
                gtm[:, tt, :], psTf[0:128, 0:8], rtm[:, 0:1], None,
                op0=ALU.mult))
            g = gtm[:, tt, :]
            ctx.fop("vector", lambda e, g=g: e.reduce_max(m1n[:, :], g, axis=AX.X,
                                                          negate=True))
            ctx.fop("vector", lambda e, g=g: e.tensor_scalar(
                eq[:, :], g, m1n[:, 0:1], 0.0, op0=ALU.add, op1=ALU.is_ge))
            ctx.fop("vector", lambda e, g=g: e.scalar_tensor_tensor(
                msk[:, :], eq[:, :], -1e30, g, op0=ALU.mult, op1=ALU.add))
            vv = ctx.fop("vector", lambda e: e.reduce_max(m2[:, :], msk[:, :],
                                                          axis=AX.X))
            ctx.fop("vector", lambda e, g=g: e.tensor_scalar(
                ge[:, :], g, m2[:, 0:1], None, op0=ALU.is_ge))
            ctx.wait("scalar", "V", ctx.ctr["V"])
            s1 = ctx.op("scalar", lambda e, g=g: e.activation(
                expw[:, :], g, AF.Exp, bias=m1n[:, 0:1]), "S")
            s2 = ctx.op("scalar", lambda e: e.activation(
                d2[:, 0:1], m2[:, 0:1], AF.Exp, bias=m1n[:, 0:1]), "S")
            ctx.wait("vector", "S", s2)
            ctx.fop("vector", lambda e: e.tensor_scalar(
                dd[:, 0:1], d2[:, 0:1], 1.0, None, op0=ALU.add))
            ctx.fop("vector", lambda e: e.reciprocal(rr[:, 0:1], dd[:, 0:1]))
            ctx.fop("vector", lambda e: e.tensor_mul(wexp[:, :], expw[:, :],
                                                     ge[:, :]))
            pv = ctx.fop("vector", lambda e: e.tensor_scalar(
                ptb[:, :], wexp[:, :], rr[:, 0:1], None, op0=ALU.mult))
            ctx.wait("tensor", "V", pv)
            tr2 = ctx.op("tensor", lambda e: e.transpose(
                psT[0:8, 8:136], ptb[:, 0:8], com["ident"][:, :]), "T")
            ctx.wait("scalar", "T", tr2)
            sp = ctx.op("scalar", lambda e, tt=tt: e.copy(
                pfm[0:8, tt * 128:(tt + 1) * 128], psT[0:8, 8:136]), "S")
            ctx.wait("tensor", "S", sp)
            ctx.wait("vector", "S", sp)
        for e_i in range(E):
            ctx.join("tensor", ["V", "S"])
            for ch in range(NCB):
                pm = ctx.op("tensor", lambda e, e_i=e_i, ch=ch: e.matmul(
                    psY2[:, :], sel8[:, e_i, :], pfm[0:8, ch * 512:(ch + 1) * 512],
                    start=True, stop=True), "T")
                ctx.wait("scalar", "T", pm)
                pe_ = ctx.op("scalar", lambda e, ch=ch: e.copy(
                    pbcs[:, ch, :], psY2[:, :]), "S")
                ctx.wait("tensor", "S", pe_)
            ctx.join("gpsimd", ["T"])
            w3_done = ctx.dma("gpsimd", w3sb[:, :, :],
                              wd["w3T"].rearrange("e (k p) f -> e p k f", p=128)[e_i],
                              "W")
            hvs = {}
            for ht in range(16):
                sl = ht % 2
                if ht >= 2:
                    ctx.wait("gpsimd", "T", hl_done[ht - 2])
                wsrc = wd["w12T"].rearrange(
                    "e (k p) (g c) -> e p k g c", p=128, g=2)
                w0 = ctx.dma("gpsimd", wsl[:, sl, :, 0, :],
                             wsrc[e_i, :, :, 0, ht * 128:(ht + 1) * 128], "W")
                w1 = ctx.dma("gpsimd", wsl[:, sl, :, 1, :],
                             wsrc[e_i, :, :, 1, ht * 128:(ht + 1) * 128], "W")
                ctx.wait("tensor", w0)
                ctx.wait("tensor", w1)
                if ht == 0:
                    hl_done = {}
                for ch in range(NCB):
                    m = ht * NCB + ch
                    if m >= 2:
                        ctx.wait("tensor", "V", hvs[m - 2])
                    elif e_i > 0 and m == 0:
                        ctx.join("tensor", ["V"])
                    p1 = psH1a if m % 2 == 0 else psH1b
                    p2 = psH2a if m % 2 == 0 else psH2b
                    for k in range(KD):
                        ctx.op("tensor", lambda e, sl=sl, k=k, ch=ch, p1=p1:
                               e.matmul(
                                   p1[:, :], wsl[:, sl, k, 0, :],
                                   xlnb[:, k, ch * 512:(ch + 1) * 512],
                                   start=(k == 0), stop=(k == KD - 1)))
                    for k in range(KD):
                        mm2 = ctx.op("tensor", lambda e, sl=sl, k=k, ch=ch, p2=p2:
                                     e.matmul(
                                         p2[:, :], wsl[:, sl, k, 1, :],
                                         xlnb[:, k, ch * 512:(ch + 1) * 512],
                                         start=(k == 0), stop=(k == KD - 1)),
                                     "T" if k == KD - 1 else None)
                    hl_done[ht] = mm2
                    ctx.wait("scalar", "T", mm2)
                    if m >= 1:
                        ctx.wait("scalar", "V", hvs[m - 1])
                    sv = ctx.op("scalar", lambda e, p1=p1: e.activation(
                        sh1[:, :], p1[:, :], AF.Silu), "S")
                    ctx.wait("vector", "S", sv)
                    ctx.op("vector", lambda e, p2=p2: e.tensor_tensor(
                        th[:, :], sh1[:, :], p2[:, :],
                        op=ALU.mult), lp=True)
                    hv = ctx.op("vector", lambda e, ht=ht, ch=ch: e.tensor_tensor(
                        hidb[:, ht, ch * 512:(ch + 1) * 512], th[:, :],
                        pbcs[:, ch, :], op=ALU.mult), "V", lp=True)
                    hvs[m] = hv
            ctx.wait("tensor", "W", w3_done)
            ctx.join("tensor", ["V"])
            yv = {}
            for ch in range(NCB):
                for ft in range(KD):
                    ps = psY if ft % 2 == 0 else psY2
                    if (ch * KD + ft) >= 2:
                        prev = (ch * KD + ft) - 2
                        ctx.wait("tensor", "V", yv[prev])
                    for ht in range(16):
                        mm = ctx.op("tensor", lambda e, ht=ht, ft=ft, ch=ch, ps=ps:
                                    e.matmul(
                                        ps[:, :], w3sb[:, ht, ft * 128:(ft + 1) * 128],
                                        hidb[:, ht, ch * 512:(ch + 1) * 512],
                                        start=(ht == 0), stop=(ht == 15)),
                                    "T" if ht == 15 else None)
                    ctx.wait("vector", "T", mm)
                    if e_i == 0:
                        yv[ch * KD + ft] = ctx.op("vector",
                                                  lambda e, ft=ft, ch=ch, ps=ps:
                                                  e.tensor_copy(
                                                      yac[:, ft, ch * 512:(ch + 1) * 512],
                                                      ps[:, :]), "V")
                    else:
                        yv[ch * KD + ft] = ctx.op("vector",
                                                  lambda e, ft=ft, ch=ch, ps=ps:
                                                  e.tensor_tensor(
                                                      yac[:, ft, ch * 512:(ch + 1) * 512],
                                                      yac[:, ft, ch * 512:(ch + 1) * 512],
                                                      ps[:, :], op=ALU.add), "V")
        for c in range(NCB):
            ctx.join("sync", ["V", "O"])
            a_done = ctx.dma("sync", xf[:, :, :], xview(xin, b, toff + c * CW, CW), "A")
            ctx.wait("vector", "A", a_done)
            for k in range(KD):
                fv = ctx.op("vector", lambda e, k=k, c=c: e.tensor_tensor(
                    xf[:, k, :], xf[:, k, :], yac[:, k, c * 512:(c + 1) * 512],
                    op=ALU.add), "V")
            ctx.join("sync", ["V"])
            st = ctx.dma("sync", xview(xout, b, toff + c * CW, CW), xf[:, :, :], "O")

    return None




def pass_l0ab(ctx, com, xin, xout, wd):
    DI, NS = 2048, 16
    KI = DI // 128

    wxt = ctx.sb("wxt", [128, KI, 32], BF16)
    ctx.dma("gpsimd", wxt[:, :, :], wd["W_xT"].rearrange("(k p) f -> p k f", p=128), "W")
    wout = ctx.sb("wout_ab", [128, KI, 1024], BF16)
    ctx.dma("gpsimd", wout[:, :, :], wd["W_outT"].rearrange("(k p) f -> p k f", p=128), "W")
    convw = ctx.sb("convw", [128, 16, 4], F32)
    ctx.dma("gpsimd", convw[:, :, :], wd["conv_w_c"], "W")
    convb = ctx.sb("convb", [128, 16], F32)
    ctx.dma("gpsimd", convb[:, :], wd["conv_b_c"], "W")
    bdt = ctx.sb("bdt", [128, 16], F32)
    ctx.dma("gpsimd", bdt[:, :], wd["b_dt_c"], "W")
    alog = ctx.sb("alog", [128, 16, 16], F32)
    ctx.dma("gpsimd", alog[:, :, :], wd["A_log_c"], "W")
    dcol = ctx.sb("dcol", [128, 16], F32)
    ctx.dma("gpsimd", dcol[:, :], wd["D_c"], "W")
    selc = ctx.sb("selc", [32, 32, 128], BF16)
    sel = np.zeros((32, 32, 128), dtype=bf16)
    for j in range(32):
        sel[j, j, :] = 1
    ctx.dma("gpsimd", selc[:, :, :], ctx.const("sel32", sel), "W")
    for eng in ("tensor", "vector", "scalar"):
        ctx.join(eng, ["W"])
    Ac = ctx.sb("Ac", [128, 16, 16], F32)
    s0 = ctx.op("scalar", lambda e: e.activation(Ac[:, :, :], alog[:, :, :],
                                                 AF.Exp), "S")
    ctx.wait("vector", "S", s0)
    v0 = ctx.fop("vector", lambda e: e.tensor_scalar(
        Ac[:, :, :], Ac[:, :, :], -1.0, None, op0=ALU.mult))
    ctx.wait("scalar", "V", v0)

    psA = ctx.ps("psA_ab", [128, 1024])
    psB = ctx.ps("psB_ab", [128, 512])
    psC = ctx.ps("psC_ab", [128, 512])
    psBC = ctx.ps("psBC_ab", [128, 512])
    psY = ctx.ps("psY_ab", [128, 512])
    psY2 = ctx.ps("psY2_ab", [128, 512])

    xf = ctx.sb("xf_ab", [128, KD, CW], F32)
    xln = ctx.sb("xln_ab", [128, KD, CW], BF16)
    winsl = ctx.sb("winsl", [128, 2, KD, 128], BF16)
    wdtsl = ctx.sb("wdtsl", [128, 2, KI, 128], BF16)
    xib = ctx.sb("xib", [128, 16, 515], BF16)
    xtail = ctx.sb("xtail", [128, 16, 3], BF16)
    zsb = ctx.sb("zsb", [128, 16, 512], BF16)
    cacc = ctx.sb("cacc", [128, 512], F32)
    bcs = ctx.sb("bcs", [32, 512], BF16)
    Bbcs = ctx.sb("Bbcs", [128, 16, 512], BF16)
    Cbcs = ctx.sb("Cbcs", [128, 16, 512], BF16)
    dtf = ctx.sb("dtf", [128, 512], F32)
    spl = ctx.sb("spl", [128, 512], F32)
    dtu = ctx.sb("dtu", [128, 512], F32)
    dAt = ctx.sb("dAt", [128, 2, 512], F32)
    dBt = ctx.sb("dBt", [128, 512], F32)
    stt_ = ctx.sb("st_ab", [128, 512], F32)
    sCt = ctx.sb("sCt", [128, 2, 512], BF16)
    carry = ctx.sb("carry", [128, 256], F32)
    yb = ctx.sb("yb", [128, 16, 512], BF16)
    yt1 = ctx.sb("yt1", [128, 512], F32)

    for b in range(NB):
        ctx.barrier()
        cz = ctx.op("vector", lambda e: e.memset(carry[:, :], 0.0), "V")
        ht = ctx.op("vector", lambda e: e.memset(xtail[:, :, :], 0.0), "V")
        ctx.wait("vector", "V", ht)
        ctx.wait("scalar", "V", ht)
        for c in range(NC_PER_B):
            t0c = c * CW
            lnv, lne = emit_ln_chunk(ctx, com, xf, xln, psA[:, 0:512],
                                     psA[:, 512:1024],
                                     src_view=xview(xin, b, t0c, CW))
            ctx.wait("tensor", lne, lnv)
            hv = ctx.fop("vector", lambda e: e.tensor_copy(xib[:, :, 0:3],
                                                           xtail[:, :, :]))
            ctx.wait("scalar", "V", hv)
            ev = {}
            for sb_i in range(32):
                sl = sb_i % 2
                if sb_i >= 2:
                    ctx.wait("gpsimd", "T", ev[sb_i - 2])
                wtok = ctx.dma("gpsimd", winsl[:, sl, :, :],
                               wd["W_inT"].rearrange("(k p) f -> p k f", p=128)
                               [:, :, sb_i * 128:(sb_i + 1) * 128], "W")
                ctx.wait("tensor", wtok)
                for f4 in range(1):
                    ft = sb_i
                    ps = psB if ft % 2 == 0 else psC
                    if ft >= 2:
                        ctx.wait("tensor", "S", sev[ft - 2])
                    for k in range(KD):
                        mm = ctx.op("tensor", lambda e, sl=sl, k=k, ps=ps:
                                    e.matmul(ps[:, :],
                                             winsl[:, sl, k, :],
                                             xln[:, k, :], start=(k == 0),
                                             stop=(k == KD - 1)),
                                    "T" if k == KD - 1 else None)
                    if ft == 0:
                        sev = {}
                    ev[sb_i] = mm
                    ctx.wait("scalar", "T", mm)
                    if ft < 16:
                        sev[ft] = ctx.op("scalar", lambda e, ft=ft, ps=ps:
                                         e.copy(xib[:, ft, 3:515], ps[:, :]), "S")
                    else:
                        sev[ft] = ctx.op("scalar", lambda e, ft=ft, ps=ps:
                                         e.activation(zsb[:, ft - 16, :], ps[:, :],
                                                      AF.Silu), "S")
            ctx.wait("vector", "S", sev[15])
            tl = ctx.fop("vector", lambda e: e.tensor_copy(
                xtail[:, :, :], xib[:, :, 512:515]))
            for ft in range(16):
                if ft > 0:
                    ctx.wait("vector", "S", uv)
                ctx.fop("vector", lambda e, ft=ft: e.tensor_scalar(
                    cacc[:, :], xib[:, ft, 0:512], convw[:, ft, 0:1], None,
                    op0=ALU.mult))
                for j in range(1, 4):
                    ctx.fop("vector", lambda e, ft=ft, j=j: e.scalar_tensor_tensor(
                        cacc[:, :], xib[:, ft, j:512 + j], convw[:, ft, j:j + 1],
                        cacc[:, :], op0=ALU.mult, op1=ALU.add))
                ctx.wait("scalar", "V", ctx.ctr["V"])
                uv = ctx.op("scalar", lambda e, ft=ft: e.activation(
                    xib[:, ft, 3:515], cacc[:, :], AF.Silu,
                    bias=convb[:, ft:ft + 1]), "S")
            ctx.wait("tensor", "S", uv)
            for k in range(KI):
                mm = ctx.op("tensor", lambda e, k=k: e.matmul(
                    psBC[0:32, 0:512], wxt[:, k, :], xib[:, k, 3:515],
                    start=(k == 0), stop=(k == KI - 1)), "T" if k == KI - 1 else None)
            ctx.wait("scalar", "T", mm)
            bv = ctx.op("scalar", lambda e: e.copy(bcs[0:32, :], psBC[0:32, 0:512]),
                        "S")
            ctx.wait("tensor", "S", bv)
            for n in range(NS):
                for (srow, dstb) in ((n, Bbcs), (16 + n, Cbcs)):
                    mm = ctx.op("tensor", lambda e, srow=srow: e.matmul(
                        psBC[:, 0:512], selc[:, srow, :], bcs[0:32, :],
                        start=True, stop=True), "T")
                    ctx.wait("scalar", "T", mm)
                    se = ctx.op("scalar", lambda e, n=n, dstb=dstb: e.copy(
                        dstb[:, n, :], psBC[:, 0:512]), "S")
                    ctx.wait("tensor", "S", se)
            ctx.wait("vector", "S", se)
            for ft in range(KI):
                sl = ft % 2
                if ft >= 2:
                    ctx.wait("gpsimd", "T", dtev[ft - 2])
                wtok = ctx.dma("gpsimd", wdtsl[:, sl, :, :],
                               wd["W_dtT"].rearrange("(k p) f -> p k f", p=128)
                               [:, :, ft * 128:(ft + 1) * 128], "W")
                ctx.wait("tensor", wtok)
                ps = psB if ft % 2 == 0 else psC
                for k in range(KI):
                    mm = ctx.op("tensor", lambda e, sl=sl, k=k, ps=ps: e.matmul(
                        ps[:, :], wdtsl[:, sl, k, :], xib[:, k, 3:515],
                        start=(k == 0), stop=(k == KI - 1)),
                        "T" if k == KI - 1 else None)
                if ft == 0:
                    dtev = {}
                dtev[ft] = mm
                ctx.wait("scalar", "T", mm)
                s1 = ctx.op("scalar", lambda e, ft=ft, ps=ps: e.activation(
                    spl[:, :], ps[:, :], AF.Abs, bias=bdt[:, ft:ft + 1]), "S")
                s1 = ctx.op("scalar", lambda e: e.activation(
                    spl[:, :], spl[:, :], AF.Exp, scale=-1.0), "S")
                s1 = ctx.op("scalar", lambda e: e.activation(
                    spl[:, :], spl[:, :], AF.Ln, bias=com["one_c"][:, 0:1]), "S")
                ctx.wait("vector", "S", s1)
                ctx.fop("vector", lambda e, ft=ft, ps=ps: e.tensor_scalar(
                    dtf[:, :], ps[:, :], bdt[:, ft:ft + 1], 0.0,
                    op0=ALU.add, op1=ALU.max))
                dv = ctx.fop("vector", lambda e: e.tensor_add(
                    dtf[:, :], dtf[:, :], spl[:, :]))
                ctx.wait("tensor", "V", dv)
                dv = ctx.op("vector", lambda e, ft=ft: e.tensor_mul(
                    dtu[:, :], dtf[:, :], xib[:, ft, 3:515]), "V")
                ctx.wait("scalar", "V", dv)
                scv = {}
                ymm = {}
                for n in range(NS):
                    p = n % 2
                    if n >= 2:
                        ctx.wait("scalar", "V", scv[n - 1])
                    sa = ctx.op("scalar", lambda e, ft=ft, n=n, p=p: e.activation(
                        dAt[:, p, :], dtf[:, :], AF.Exp,
                        scale=Ac[:, ft, n:n + 1]), "S")
                    ctx.wait("vector", "S", sa)
                    ctx.op("vector", lambda e, n=n: e.tensor_mul(
                        dBt[:, :], dtu[:, :], Bbcs[:, n, :]))
                    ci = ft * NS + n
                    sc_ = ctx.fop("vector", lambda e, ci=ci, p=p:
                                  e.tensor_tensor_scan(
                                      stt_[:, :], dAt[:, p, :], dBt[:, :],
                                      carry[:, ci:ci + 1],
                                      op0=ALU.mult, op1=ALU.add))
                    scv[n] = sc_
                    ctx.op("vector", lambda e, ci=ci: e.tensor_copy(
                        carry[:, ci:ci + 1], stt_[:, 511:512]), "V")
                    if n >= 2:
                        ctx.wait("vector", "T", ymm[n - 2])
                    mv = ctx.op("vector", lambda e, n=n, p=p: e.tensor_mul(
                        sCt[:, p, :], stt_[:, :], Cbcs[:, n, :]), "V")
                    ctx.wait("tensor", "V", mv)
                    ymm[n] = ctx.op("tensor", lambda e, ft=ft, n=n, p=p: e.matmul(
                        psY[:, :] if ft % 2 == 0 else psY2[:, :],
                        com["ident"][:, :], sCt[:, p, :],
                        start=(n == 0), stop=(n == NS - 1)), "T")
                yv = ymm[NS - 1]
                ctx.wait("vector", "T", yv)
                pY = psY if ft % 2 == 0 else psY2
                ctx.fop("vector", lambda e, ft=ft, pY=pY: e.scalar_tensor_tensor(
                    yt1[:, :], xib[:, ft, 3:515], dcol[:, ft:ft + 1], pY[:, :],
                    op0=ALU.mult, op1=ALU.add))
                ye = ctx.fop("vector", lambda e, ft=ft: e.tensor_mul(
                    yb[:, ft, :], yt1[:, :], zsb[:, ft, :]))
                ctx.wait("tensor", "V", ye)
            for oft in range(KD):
                pso = psB if oft % 2 == 0 else psC
                if oft >= 2:
                    ctx.wait("tensor", "V", oev[oft - 2])
                for k in range(KI):
                    mm = ctx.op("tensor", lambda e, oft=oft, k=k, pso=pso: e.matmul(
                        pso[:, :], wout[:, k, oft * 128:(oft + 1) * 128],
                        yb[:, k, :], start=(k == 0), stop=(k == KI - 1)),
                        "T" if k == KI - 1 else None)
                ctx.wait("vector", "T", mm)
                if oft == 0:
                    oev = {}
                oev[oft] = ctx.op("vector", lambda e, oft=oft, pso=pso: e.tensor_tensor(
                    xf[:, oft, :], xf[:, oft, :], pso[:, :], op=ALU.add), "V")
            ctx.join("sync", ["V"])
            st = ctx.dma("sync", xview(xout, b, t0c, CW), xf[:, :, :], "O")
    return None



def pass_l1ab(ctx, com, xin, xout, wd):
    NS = 16
    wslg = ctx.sb("wslg", [128, 2, KD, 128], BF16)
    w1b = ctx.sb("w1b", [128, 8], F32)
    v1b = ctx.sb("v1b", [128, 8], F32)
    w2b = ctx.sb("w2b", [128, 8], F32)
    ctx.dma("gpsimd", w1b[:, :], wd["w1_b_c"], "W")
    ctx.dma("gpsimd", v1b[:, :], wd["v1_b_c"], "W")
    ctx.dma("gpsimd", w2b[:, :], wd["w2_b_c"], "W")
    convw = ctx.sb("convw1", [128, 8, 3], F32)
    ctx.dma("gpsimd", convw[:, :, :], wd["conv_w_c"], "W")
    Araw = ctx.sb("Araw", [128, 8, 16], F32)
    ctx.dma("gpsimd", Araw[:, :, :], wd["A_c"], "W")
    Bc = ctx.sb("Bc", [128, 8, 16], F32)
    ctx.dma("gpsimd", Bc[:, :, :], wd["B_c"], "W")
    Cc = ctx.sb("Cc", [128, 8, 16], F32)
    ctx.dma("gpsimd", Cc[:, :, :], wd["C_c"], "W")
    Dc = ctx.sb("Dc", [128, 8], F32)
    ctx.dma("gpsimd", Dc[:, :], wd["D_c"], "W")
    for eng in ("tensor", "vector", "scalar"):
        ctx.join(eng, ["W"])
    Abar = ctx.sb("Abar", [128, 8, 16], F32)
    s0 = ctx.op("scalar", lambda e: e.activation(Abar[:, :, :], Araw[:, :, :],
                                                 AF.Sigmoid), "S")
    ctx.wait("vector", "S", s0)

    psA = ctx.ps("psA_gb", [128, 1024])
    psB = ctx.ps("psB_gb", [128, 512])
    psC = ctx.ps("psC_gb", [128, 512])
    psY = ctx.ps("psY_gb", [128, 2048])

    xf = ctx.sb("xf_gb", [128, KD, CW], F32)
    xln = ctx.sb("xln_gb", [128, KD, CW], BF16)
    x1b = ctx.sb("x1b", [128, 8, 2050], BF16)
    gout = x1b
    vb = ctx.sb("vb_gb", [128, 8, 2048], BF16)
    xcb = ctx.sb("xcb_gb", [128, 8, 2048], BF16)
    cacc = ctx.sb("cacc_gb", [128, 2048], F32)
    onest = ctx.sb("onest", [128, 2048], F32)
    abt = ctx.sb("abt", [128, 2048], F32)
    bt = ctx.sb("bt_gb", [128, 2048], F32)
    st2 = ctx.sb("st_gb", [128, 2048], F32)
    hCt = ctx.sb("hCt", [128, 2048], BF16)
    yt1 = ctx.sb("yt1_gb", [128, 2048], F32)

    oz = ctx.op("vector", lambda e: e.memset(onest[:, :], 1.0), "V")
    ctx.wait("vector", "V", oz)

    for b in range(NB):
        ctx.barrier()
        hz = ctx.op("vector", lambda e: e.memset(x1b[:, :, 0:1], 0.0), "V")
        hz = ctx.op("vector", lambda e: e.memset(x1b[:, :, 2049:2050], 0.0), "V")
        ctx.wait("vector", "V", hz)
        ctx.wait("scalar", "V", hz)
        for c in range(NC_PER_B):
            t0c = c * CW
            lnv, lne = emit_ln_chunk(ctx, com, xf, xln, psA[:, 0:512],
                                     psA[:, 512:1024],
                                     src_view=xview(xin, b, t0c, CW))
            ctx.wait("tensor", lne, lnv)
            ev = {}
            for (wname, bias, tag) in (("w1_wT", w1b, 0), ("v1_wT", v1b, 1)):
                for ft in range(KD):
                    ps = psB if (tag * KD + ft) % 2 == 0 else psC
                    key = tag * KD + ft
                    sl = key % 2
                    if key >= 2:
                        ctx.wait("tensor", "S", ev[key - 2])
                        ctx.wait("gpsimd", "T", tvv[key - 2])
                    wtok = ctx.dma("gpsimd", wslg[:, sl, :, :],
                                   wd[wname].rearrange("(k p) f -> p k f", p=128)
                                   [:, :, ft * 128:(ft + 1) * 128], "W")
                    ctx.wait("tensor", wtok)
                    for k in range(KD):
                        mm = ctx.op("tensor", lambda e, sl=sl, k=k, ps=ps:
                                    e.matmul(ps[:, :], wslg[:, sl, k, :],
                                             xln[:, k, :], start=(k == 0),
                                             stop=(k == KD - 1)),
                                    "T" if k == KD - 1 else None)
                    if key == 0:
                        tvv = {}
                    tvv[key] = mm
                    ctx.wait("scalar", "T", mm)
                    if tag == 0:
                        ev[key] = ctx.op("scalar", lambda e, ft=ft, t0c=t0c, ps=ps,
                                         bias=bias: e.activation(
                                             x1b[:, ft, 1 + t0c:1 + t0c + CW],
                                             ps[:, :], AF.Identity,
                                             bias=bias[:, ft:ft + 1]), "S")
                    else:
                        ev[key] = ctx.op("scalar", lambda e, ft=ft, t0c=t0c, ps=ps,
                                         bias=bias: e.activation(
                                             vb[:, ft, t0c:t0c + CW],
                                             ps[:, :], AF.Silu,
                                             bias=bias[:, ft:ft + 1]), "S")
        ctx.wait("vector", "S", ev[2 * KD - 1])
        for ft in range(KD):
            if ft > 0:
                ctx.wait("vector", "S", cv)
            ctx.fop("vector", lambda e, ft=ft: e.tensor_scalar(
                cacc[:, :], x1b[:, ft, 0:2048], convw[:, ft, 0:1], None,
                op0=ALU.mult))
            for j in range(1, 3):
                ctx.fop("vector", lambda e, ft=ft, j=j: e.scalar_tensor_tensor(
                    cacc[:, :], x1b[:, ft, j:2048 + j], convw[:, ft, j:j + 1],
                    cacc[:, :], op0=ALU.mult, op1=ALU.add))
            ctx.wait("scalar", "V", ctx.ctr["V"])
            cv = ctx.op("scalar", lambda e, ft=ft: e.activation(
                xcb[:, ft, :], cacc[:, :], AF.Silu), "S")
        ctx.wait("vector", "S", cv)
        for cb in range(KD):
            for s_ in range(NS):
                av = ctx.fop("vector", lambda e, cb=cb, s_=s_: e.tensor_scalar(
                    abt[:, :], onest[:, :], Abar[:, cb, s_:s_ + 1], None,
                    op0=ALU.mult))
                ctx.wait("scalar", "V", av)
                sb_ = ctx.op("scalar", lambda e, cb=cb, s_=s_: e.activation(
                    bt[:, :], xcb[:, cb, :], AF.Identity,
                    scale=Bc[:, cb, s_:s_ + 1]), "S")
                ctx.wait("vector", "S", sb_)
                sc_ = ctx.fop("vector", lambda e: e.tensor_tensor_scan(
                    st2[:, :], abt[:, :], bt[:, :], 0.0,
                    op0=ALU.mult, op1=ALU.add))
                ctx.wait("scalar", "V", sc_)
                hv = ctx.op("scalar", lambda e, cb=cb, s_=s_: e.activation(
                    hCt[:, :], st2[:, :], AF.Identity,
                    scale=Cc[:, cb, s_:s_ + 1]), "S")
                ctx.wait("tensor", "S", hv)
                for q in range(4):
                    mm = ctx.op("tensor", lambda e, q=q, s_=s_: e.matmul(
                        psY[:, q * 512:(q + 1) * 512], com["ident"][:, :],
                        hCt[:, q * 512:(q + 1) * 512],
                        start=(s_ == 0), stop=(s_ == NS - 1)),
                        "T" if (s_ == NS - 1 and q == 3) else None)
                if s_ < NS - 1:
                    ctx.wait("vector", "T", ctx.ctr["T"])
            ctx.wait("vector", "T", mm)
            ctx.fop("vector", lambda e, cb=cb: e.scalar_tensor_tensor(
                yt1[:, :], xcb[:, cb, :], Dc[:, cb:cb + 1], psY[:, 0:2048],
                op0=ALU.mult, op1=ALU.add))
            gv = ctx.fop("vector", lambda e, cb=cb: e.tensor_mul(
                gout[:, cb, 0:2048], yt1[:, :], vb[:, cb, :]))
            ctx.wait("tensor", "V", gv)
        for c in range(NC_PER_B):
            t0c = c * CW
            ctx.join("sync", ["V", "O"])
            a_done = ctx.dma("sync", xf[:, :, :], xview(xin, b, t0c, CW), "A")
            ctx.wait("vector", "A", a_done)
            for oft in range(KD):
                ps = psB if oft % 2 == 0 else psC
                sl = oft % 2
                if oft >= 2:
                    ctx.wait("tensor", "V", oev[oft - 2])
                    ctx.wait("gpsimd", "V", oev[oft - 2])
                wtok = ctx.dma("gpsimd", wslg[:, sl, :, :],
                               wd["w2_wT"].rearrange("(k p) f -> p k f", p=128)
                               [:, :, oft * 128:(oft + 1) * 128], "W")
                ctx.wait("tensor", wtok)
                for k in range(KD):
                    mm = ctx.op("tensor", lambda e, sl=sl, k=k, ps=ps, t0c=t0c:
                                e.matmul(ps[:, :], wslg[:, sl, k, :],
                                         gout[:, k, t0c:t0c + CW],
                                         start=(k == 0), stop=(k == KD - 1)),
                                "T" if k == KD - 1 else None)
                ctx.wait("vector", "T", mm)
                if oft == 0:
                    oev = {}
                oev[oft] = ctx.fop("vector", lambda e, oft=oft, ps=ps:
                                   e.scalar_tensor_tensor(
                                       xf[:, oft, :], ps[:, :],
                                       w2b[:, oft:oft + 1], xf[:, oft, :],
                                       op0=ALU.add, op1=ALU.add))
            ctx.join("sync", ["V"])
            st = ctx.dma("sync", xview(xout, b, t0c, CW), xf[:, :, :], "O")
    return None




def _l0ffn_consts(ctx):
    DH, WIN, N = 64, 512, 2048
    half = np.arange(0, DH, 2, dtype=np.float64)
    inv = 1.0 / 10000.0 ** (half / DH)
    t = np.arange(N, dtype=np.float64)
    fr = t[:, None] * inv[None, :]
    fr = np.concatenate([fr, fr], axis=1)
    base = (half + 0.4 * DH) / (1.4 * DH)
    power = (t - N // 2) / float(WIN // 2)
    sc = base[None, :] ** power[:, None]
    sc = np.concatenate([sc, sc], axis=1)
    cq = (np.cos(fr) * sc * 8.0).T
    sq = (np.sin(fr) * sc * 8.0).T
    ck = (np.cos(fr) / sc).T
    sk = (np.sin(fr) / sc).T
    def dup(a):
        return np.concatenate([a, a], axis=0).astype(bf16)
    c = {}
    c["cq"] = ctx.const("ro_cq", dup(cq))
    c["sq"] = ctx.const("ro_sq", dup(sq))
    c["ck"] = ctx.const("ro_ck", dup(ck))
    c["sk"] = ctx.const("ro_sk", dup(sk))
    rot = np.zeros((128, 128), dtype=bf16)
    for blk in range(2):
        o = blk * 64
        for m in range(32):
            rot[o + m + 32, o + m] = -1
            rot[o + m, o + m + 32] = 1
    c["rot"] = ctx.const("ro_rot", rot)
    NEG = -1e30
    mA = np.zeros((4, 128, 512), dtype=np.float32)
    mB = np.zeros((4, 128, 512), dtype=np.float32)
    for qt in range(4):
        for qi in range(128):
            mA[qt, qi, :] = NEG
            lo = 128 * qt + qi + 1
            if lo < 512:
                mA[qt, qi, lo:] = 0.0
            mB[qt, qi, :] = NEG
            mB[qt, qi, :128 * qt + qi + 1] = 0.0
    mT = np.zeros((8, 128, 512), dtype=np.float32)
    for rel in range(8):
        for ki in range(128):
            d0 = 128 * (rel - 4) + ki
            mT[rel, ki, :] = NEG
            lo = max(0, d0)
            hi = min(512, d0 + 512)
            if lo < hi:
                mT[rel, ki, lo:hi] = 0.0
    c["mA"] = ctx.const("ffn_mA", np.ascontiguousarray(
        mA.transpose(1, 0, 2)).astype(bf16))
    c["mB"] = ctx.const("ffn_mB", np.ascontiguousarray(
        mB.transpose(1, 0, 2)).astype(bf16))
    c["mT"] = ctx.const("ffn_mT", np.ascontiguousarray(
        mT.transpose(1, 0, 2)).astype(bf16))
    return c


FFN_DEBUG = None


def pass_l0ffn(ctx, com, xin, xout, wd):
    nc = ctx.nc
    H, DH, WIN = 8, 64, 512
    cst = _l0ffn_consts(ctx)
    rotc = ctx.sb("rotc", [128, 128], BF16)
    ctx.dma("gpsimd", rotc[:, :], cst["rot"], "W")
    mAb = ctx.sb("mAb", [128, 4, 512], BF16)
    mBb = ctx.sb("mBb", [128, 4, 512], BF16)
    mTb = ctx.sb("mTb", [128, 8, 512], BF16)
    ctx.dma("gpsimd", mAb[:, :, :], cst["mA"], "W")
    ctx.dma("gpsimd", mBb[:, :, :], cst["mB"], "W")
    ctx.dma("gpsimd", mTb[:, :, :], cst["mT"], "W")

    qsc = ctx.sb("qsc", [128, 2], F32)
    ksc = ctx.sb("ksc", [128, 2], F32)
    ctx.dma("gpsimd", qsc[:, :], wd["qs_c"], "W")
    ctx.dma("gpsimd", ksc[:, :], wd["ks_c"], "W")
    lgc = ctx.sb("lgc", [128, 2, 8], F32)
    lbc = ctx.sb("lbc", [128, 2, 8], F32)
    ctx.dma("gpsimd", lgc[:, :, :], wd["ln_g_c"], "W")
    ctx.dma("gpsimd", lbc[:, :, :], wd["ln_b_c"], "W")
    b1c = ctx.sb("b1c", [128, 2, 44], F32)
    ctx.dma("gpsimd", b1c[:, :, :], wd["b1p_c"], "W")
    b2c = ctx.sb("b2c", [128, 2, 8], F32)
    ctx.dma("gpsimd", b2c[:, :, :], wd["b2_c"], "W")
    wqslab = ctx.sb("wqslab", [128, 2, KD, 128], BF16)
    wvslab = ctx.sb("wvslab", [128, KD, 512], BF16)
    wo = ctx.sb("wo_ffn", [128, 4, 1024], BF16)
    for eng in ("tensor", "vector", "scalar"):
        ctx.join(eng, ["W"])

    tcq = ctx.sb("tcq", [128, T], BF16)
    tsq = ctx.sb("tsq", [128, T], BF16)
    tck = ctx.sb("tck", [128, T], BF16)
    tsk = ctx.sb("tsk", [128, T], BF16)
    ctx.dma("gpsimd", tcq[:, :], cst["cq"], "W")
    ctx.dma("gpsimd", tsq[:, :], cst["sq"], "W")
    ctx.dma("gpsimd", tck[:, :], cst["ck"], "W")
    ctx.dma("gpsimd", tsk[:, :], cst["sk"], "W")

    psA = ctx.ps("psA_f", [128, 1024])
    psB = ctx.ps("psB_f", [128, 512])
    psC = ctx.ps("psC_f", [128, 512])
    psC2 = ctx.ps("psC2_f", [128, 512])
    psD = ctx.ps("psD_f", [128, 512])
    psE = ctx.ps("psE_f", [128, 512])
    psE2 = ctx.ps("psE2_f", [128, 512])

    sA = nc.dram_tensor(f"sA_{ctx.uid()}", [NB, D, T], F32, kind="Internal").ap()
    sB = nc.dram_tensor(f"sB_{ctx.uid()}", [NB, D, T], F32, kind="Internal").ap()

    xf = ctx.sb("xf_f", [128, KD, CW], F32)
    xcb = ctx.sb("xcb_f", [128, KD, 513], BF16)
    qpre = ctx.sb("qpre", [128, 4, T], BF16)
    hlnb = qpre[:, 0:2, :].rearrange("p a (c d) -> p (a c) d", c=4, d=512)
    kpre = ctx.sb("kpre", [128, 4, T], BF16)
    vtm = ctx.sb("vtm_f", [128, 16, H, 65], BF16)
    sqb = ctx.sb("sqb_f", [128, 512], BF16)
    qhat = ctx.sb("qhat", [128, 512], BF16)
    tmpA = ctx.sb("tmpA_f", [128, 512], BF16)
    rrq = ctx.sb("rrq_f", [1, 512], BF16)
    Etb = ctx.sb("Etb_f", [128, 2, 512], BF16)
    negm = ctx.sb("negm_f", [128, 2, 4], F32)
    mx2 = ctx.sb("mx2_f", [128, 2], F32)
    negmc = ctx.sb("negmc_f", [128, 2], BF16)
    negrow = ctx.sb("negrow_f", [1, 2, 512], BF16)
    rrowb = ctx.sb("rrowb_f", [65, 512], BF16)
    attnh = ctx.sb("attnh_f", [64, 512], BF16)
    attv = ctx.sb("attv_f", [64, 512], BF16)
    aout = ctx.sb("aout_f", [128, 4, 512], BF16)
    sq2 = ctx.sb("sq2_f", [128, 2, 512], BF16)
    w1sl = ctx.sb("w1sl", [128, 2, KD, 256], BF16)
    w2sl = ctx.sb("w2sl", [128, 2, 22, 128], BF16)
    gact = ctx.sb("gact", [128, 512], BF16)
    tmpB = gact
    lnt2b = ctx.sb("lnt2b", [128, 512], F32)
    hidden = ctx.sb("hid_f", [128, 22, 512], BF16)
    eps24 = ctx.fconst(1e-24)
    ctx.join("scalar", ["W"])

    pc = ctx.dma("sync", xout, xin, "O")
    for b in range(NB):
        for c in range(NC_PER_B):
            t0c = c * CW
            lnv, lne = emit_ln_chunk(ctx, com, xf, hlnb, psA[:, 0:512],
                                     psA[:, 512:1024],
                                     src_view=xview(xin, b, t0c, CW),
                                     out_f32=xf)
            ctx.join("sync", ["V"])
            ctx.dma("sync", xview(sA, b, t0c, CW), xf[:, :, :], "O")

    for layer in range(2):
        if FFN_DEBUG == "LN0" and layer == 0:
            ctx.join("sync", ["O"])
            ctx.dma("sync", xout, sA, "O")
            return None
        if FFN_DEBUG == "B0" and layer == 1:
            ctx.join("sync", ["O"])
            ctx.dma("sync", xout, sA, "O")
            return None
        ctx.barrier()
        ctx.join("gpsimd", ["T", "V", "S"])
        wv2 = ctx.dma("gpsimd", wo[:, :, :],
                      wd["wo"].rearrange("l (k p) f -> l p k f", p=128)[layer],
                      "W")
        ctx.wait("tensor", wv2)
        for b in range(NB):
            ctx.barrier()
            od = ctx.op("vector", lambda e: e.memset(vtm[:, :, :, 64:65], 1.0), "V")
            ctx.wait("tensor", "V", od)
            for c in range(NC_PER_B):
                t0c = c * CW
                ctx.join("gpsimd", ["T", "S", "V"])
                if c == 0:
                    hz = ctx.op("vector", lambda e: e.memset(xcb[:, :, 0:1], 0.0), "V")
                    ctx.wait("tensor", "V", hz)
                    ctx.wait("scalar", "V", hz)
                    xt = ctx.dma("gpsimd", xcb[:, :, 1:513],
                                 xview(sA, b, 0, CW), "A")
                else:
                    xt = ctx.dma("gpsimd", xcb[:, :, 0:513],
                                 xview(sA, b, t0c - 1, CW + 1), "A")
                ctx.wait("tensor", xt)
                sev = {}
                qkmm = {}
                for ft in range(8):
                    ps = psB if ft % 2 == 0 else psC
                    sl = ft % 2
                    if ft >= 2:
                        ctx.wait("tensor", "S", sev[ft - 2])
                        ctx.wait("gpsimd", "T", qkmm[ft - 2])
                    wtok = ctx.dma("gpsimd", wqslab[:, sl, :, :],
                                   wd["wqkv"].rearrange("l (k p) f -> l p k f", p=128)
                                   [layer, :, :, ft * 128:(ft + 1) * 128], "W")
                    ctx.wait("tensor", wtok)
                    for k in range(KD):
                        off = 1 if k < 4 else 0
                        mm = ctx.op("tensor", lambda e, sl=sl, k=k, ps=ps, off=off:
                                    e.matmul(ps[:, :], wqslab[:, sl, k, :],
                                             xcb[:, k, off:off + 512],
                                             start=(k == 0), stop=(k == KD - 1)),
                                    "T" if k == KD - 1 else None)
                    qkmm[ft] = mm
                    ctx.wait("scalar", "T", mm)
                    dstb = qpre if ft < 4 else kpre
                    jt = ft % 4
                    sev[ft] = ctx.op("scalar", lambda e, dstb=dstb, jt=jt, ps=ps,
                                     t0c=t0c: e.copy(
                                         dstb[:, jt, t0c:t0c + CW], ps[:, :]), "S")
                    sq_ev = ctx.op("scalar", lambda e, ps=ps: e.activation(
                        sqb[:, :], ps[:, :], AF.Square), "S")
                    (tc_, ts_) = (tcq, tsq) if ft < 4 else (tck, tsk)
                    gsc_ = qsc if ft < 4 else ksc
                    ctx.wait("tensor", "S", sq_ev)
                    for hh in range(2):
                        p0 = hh * 64
                        smm = ctx.op("tensor", lambda e, p0=p0: e.matmul(
                            psA[0:1, 0:512], com["ones_col"][p0:p0 + 64, 0:1],
                            sqb[p0:p0 + 64, :], start=True, stop=True), "T")
                        ctx.wait("scalar", "T", smm)
                        s1 = ctx.op("scalar", lambda e: e.activation(
                            com["ln_tmp"][0:1, :], psA[0:1, 0:512], AF.Ln,
                            bias=eps24[0:1, 0:1]), "S")
                        s1 = ctx.op("scalar", lambda e: e.activation(
                            rrq[0:1, :], com["ln_tmp"][0:1, :], AF.Exp, scale=-0.5), "S")
                        ctx.wait("tensor", "S", s1)
                        bmm = ctx.op("tensor", lambda e: e.matmul(
                            psE[0:64, :], com["ones_row"][0:1, 0:64],
                            rrq[0:1, :], start=True, stop=True), "T")
                        ctx.wait("vector", "T", bmm)
                        qv = ctx.fop("vector", lambda e, dstb=dstb, jt=jt, p0=p0,
                                     t0c=t0c, gsc_=gsc_, layer=layer: e.scalar_tensor_tensor(
                                         qhat[p0:p0 + 64, :],
                                         dstb[p0:p0 + 64, jt, t0c:t0c + CW],
                                         gsc_[p0:p0 + 64, layer:layer + 1],
                                         psE[0:64, :], op0=ALU.mult,
                                         op1=ALU.mult), lp=True)
                        ctx.wait("tensor", "V", qv)
                    rmm = ctx.op("tensor", lambda e: e.matmul(
                        psE2[:, :], rotc[:, :], qhat[:, :],
                        start=True, stop=True), "T")
                    ctx.wait("vector", "T", rmm)
                    ctx.op("vector", lambda e, tc_=tc_, t0c=t0c: e.tensor_tensor(
                        tmpA[:, :], qhat[:, :], tc_[:, t0c:t0c + CW],
                        op=ALU.mult), lp=True)
                    ctx.op("vector", lambda e, ts_=ts_, t0c=t0c: e.tensor_tensor(
                        tmpB[:, :], psE2[:, :], ts_[:, t0c:t0c + CW],
                        op=ALU.mult), lp=True)
                    rv = ctx.op("vector", lambda e, dstb=dstb, jt=jt, t0c=t0c:
                                e.tensor_tensor(dstb[:, jt, t0c:t0c + CW],
                                                tmpA[:, :], tmpB[:, :],
                                                op=ALU.add), "V", lp=True)
                    ctx.wait("tensor", "V", rv)
                    ctx.wait("scalar", "V", rv)
                ctx.join("gpsimd", ["T"])
                wtok = ctx.dma("gpsimd", wvslab[:, :, :],
                               wd["wqkv"].rearrange("l (k p) f -> l p k f", p=128)
                               [layer, :, :, 1024:1536], "W")
                ctx.wait("tensor", wtok)
                v_ev = None
                for tt in range(4):
                    gtt = c * 4 + tt
                    if v_ev is not None:
                        ctx.wait("tensor", "V", v_ev)
                    for k in range(KD):
                        off = 1 if k < 4 else 0
                        mm = ctx.op("tensor", lambda e, tt=tt, k=k, off=off:
                                    e.matmul(psD[:, :],
                                             xcb[:, k, off + tt * 128:off + (tt + 1) * 128],
                                             wvslab[:, k, :],
                                             start=(k == 0), stop=(k == KD - 1)),
                                    "T" if k == KD - 1 else None)
                    ctx.wait("vector", "T", mm)
                    v_ev = ctx.op("vector", lambda e, gtt=gtt: e.tensor_copy(
                        vtm[:, gtt, :, 0:64],
                        psD[:, :].rearrange("p (h d) -> p h d", d=64)), "V",
                        lp=True)
            ctx.barrier()
            for w in range(4):
                t0c = w * CW
                se_tok = {}

                def P1(h):
                    jt, p0 = h // 2, (h % 2) * 64
                    par = h % 2
                    psm = psB if par == 0 else psE
                    red_done = None
                    for qt in range(4):
                        qoff = t0c + qt * 128
                        if red_done is not None:
                            ctx.wait("tensor", "V", red_done)
                        if w > 0:
                            ctx.op("tensor", lambda e, qt=qt: e.matmul(
                                psA[:, 0:512], com["ident"][:, :], mAb[:, qt, :],
                                start=True, stop=False))
                            ctx.op("tensor", lambda e, jt=jt, p0=p0, qoff=qoff,
                                   t0c=t0c: e.matmul(
                                       psA[:, 0:512],
                                       qpre[p0:p0 + 64, jt, qoff:qoff + 128],
                                       kpre[p0:p0 + 64, jt, t0c - 512:t0c],
                                       start=False, stop=True))
                        ctx.op("tensor", lambda e, qt=qt: e.matmul(
                            psA[:, 512:1024], com["ident"][:, :], mBb[:, qt, :],
                            start=True, stop=False))
                        mm = ctx.op("tensor", lambda e, jt=jt, p0=p0, qoff=qoff,
                                    t0c=t0c: e.matmul(
                                        psA[:, 512:1024],
                                        qpre[p0:p0 + 64, jt, qoff:qoff + 128],
                                        kpre[p0:p0 + 64, jt, t0c:t0c + 512],
                                        start=False, stop=True), "T")
                        ctx.wait("vector", "T", mm)
                        rng_ = (0, 1024) if w > 0 else (512, 1024)
                        ctx.fop("vector", lambda e, qt=qt, rng_=rng_, par=par:
                                e.reduce_max(negm[:, par, qt:qt + 1],
                                             psA[:, rng_[0]:rng_[1]],
                                             axis=AX.X, negate=True))
                        vv = ctx.op("vector", lambda e, qt=qt, par=par:
                                    e.tensor_copy(negmc[:, par:par + 1],
                                                  negm[:, par, qt:qt + 1]), "V")
                        ctx.wait("tensor", "V", vv)
                        tt_ = ctx.op("tensor", lambda e, qt=qt, par=par, psm=psm:
                                     e.matmul(
                                         psm[0:1, qt * 128:(qt + 1) * 128],
                                         negmc[:, par:par + 1],
                                         com["ident"][:, :], start=True,
                                         stop=True), "T")
                        red_done = vv
                    ctx.wait("scalar", "T", tt_)
                    se_tok[h] = ctx.op("scalar", lambda e, par=par, psm=psm:
                                       e.copy(negrow[0:1, par, :],
                                              psm[0:1, 0:512]), "S")

                def P2(h):
                    jt, p0 = h // 2, (h % 2) * 64
                    par = h % 2
                    ctx.wait("tensor", "S", se_tok[h])
                    kts = list(range(max(0, 4 * (w - 1)), 4 * w + 4))
                    ev, av = {}, {}
                    for i, kt in enumerate(kts):
                        ps2 = psC if i % 2 == 0 else psC2
                        if i >= 2:
                            ctx.wait("tensor", "S", ev[i - 2])
                        if i >= 1:
                            ctx.wait("tensor", "S", ev[i - 1])
                            av[i - 1] = ctx.op("tensor", lambda e, h=h,
                                               pk=kts[i - 1], i=i: e.matmul(
                                                   psD[0:65, :], vtm[:, pk, h, :],
                                                   Etb[:, (i - 1) % 2, :],
                                                   start=(i - 1 == 0),
                                                   stop=False), "T")
                        rel = kt - 4 * (w - 1)
                        ctx.op("tensor", lambda e, ps2=ps2, par=par: e.matmul(
                            ps2[:, :], com["ones_row"][0:1, :],
                            negrow[0:1, par, :], start=True, stop=False))
                        ctx.op("tensor", lambda e, ps2=ps2, rel=rel: e.matmul(
                            ps2[:, :], com["ident"][:, :], mTb[:, rel, :],
                            start=False, stop=False))
                        mm = ctx.op("tensor", lambda e, jt=jt, p0=p0, kt=kt,
                                    ps2=ps2, t0c=t0c: e.matmul(
                                        ps2[:, :],
                                        kpre[p0:p0 + 64, jt, kt * 128:(kt + 1) * 128],
                                        qpre[p0:p0 + 64, jt, t0c:t0c + CW],
                                        start=False, stop=True), "T")
                        ctx.wait("scalar", "T", mm)
                        ev[i] = ctx.op("scalar", lambda e, i=i, ps2=ps2:
                                       e.activation(Etb[:, i % 2, :], ps2[:, :],
                                                    AF.Exp), "S")
                    nk = len(kts)
                    ctx.wait("tensor", "S", ev[nk - 1])
                    av[nk - 1] = ctx.op("tensor", lambda e, h=h, pk=kts[nk - 1],
                                        nk=nk: e.matmul(
                                            psD[0:65, :], vtm[:, pk, h, :],
                                            Etb[:, (nk - 1) % 2, :],
                                            start=(nk == 1), stop=True), "T")
                    ctx.wait("vector", "T", av[nk - 1])
                    rv = ctx.op("vector", lambda e: e.reciprocal(
                        rrowb[64:65, :], psD[64:65, :]), "V", lp=True)
                    ctx.wait("tensor", "V", rv)
                    tb = ctx.op("tensor", lambda e: e.matmul(
                        psC[0:64, :], com["ones_row"][64:65, 0:64],
                        rrowb[64:65, :], start=True, stop=True), "T")
                    ctx.wait("scalar", "T", tb)
                    sv = ctx.op("scalar", lambda e: e.copy(attnh[0:64, :],
                                                           psD[0:64, :]), "S")
                    ctx.wait("vector", "S", sv)
                    vv = ctx.op("vector", lambda e: e.tensor_tensor(
                        attv[0:64, :], attnh[0:64, :], psC[0:64, :],
                        op=ALU.mult), "V")
                    ctx.wait("tensor", "V", vv)
                    sh = com["shift0"] if h % 2 == 0 else com["shift1"]
                    pe2 = ctx.op("tensor", lambda e, sh=sh, h=h: e.matmul(
                        psE2[:, :], sh[0:64, :], attv[0:64, :],
                        start=(h % 2 == 0), stop=(h % 2 == 1)), "T")
                    if h % 2 == 1:
                        ctx.wait("scalar", "T", pe2)
                        sv2 = ctx.op("scalar", lambda e, jt=jt: e.copy(
                            aout[:, jt, :], psE2[:, :]), "S")
                        ctx.wait("tensor", "S", sv2)
                    ctx.wait("scalar", "V", ctx.ctr["V"])

                P1(0)
                for h in range(H):
                    if h + 1 < H:
                        P1(h + 1)
                    P2(h)
                ctx.join("sync", ["V", "O"])
                a_done = ctx.dma("sync", xf[:, :, :], xview(sA, b, t0c, CW), "A")
                ctx.wait("vector", "A", a_done)
                ctx.join("tensor", ["S"])
                for oft in range(KD):
                    pso = psB if oft % 2 == 0 else psC
                    if oft >= 2:
                        ctx.wait("tensor", "V", oev[oft - 2])
                    for k in range(4):
                        mm = ctx.op("tensor", lambda e, oft=oft, k=k, pso=pso:
                                    e.matmul(pso[:, :],
                                             wo[:, k, oft * 128:(oft + 1) * 128],
                                             aout[:, k, :], start=(k == 0),
                                             stop=(k == 3)),
                                    "T" if k == 3 else None)
                    ctx.wait("vector", "T", mm)
                    if oft == 0:
                        oev = {}
                    oev[oft] = ctx.op("vector", lambda e, oft=oft, pso=pso:
                                      e.tensor_tensor(xf[:, oft, :],
                                                      xf[:, oft, :], pso[:, :],
                                                      op=ALU.add), "V")
                ctx.join("sync", ["V"])
                ctx.dma("sync", xview(sB, b, t0c, CW), xf[:, :, :], "O")
        if FFN_DEBUG == "A0":
            ctx.join("sync", ["O"])
            ctx.dma("sync", xout, sB, "O")
            return None
        ctx.barrier()
        for b in range(NB):
            for c in range(NC_PER_B):
                t0c = c * CW
                ctx.join("gpsimd", ["T", "S", "V"])
                if c == 0:
                    hz = ctx.op("vector", lambda e: e.memset(xcb[:, :, 0:1], 0.0), "V")
                    ctx.wait("tensor", "V", hz)
                    ctx.wait("scalar", "V", hz)
                    xt = ctx.dma("gpsimd", xcb[:, :, 1:513],
                                 xview(sB, b, 0, CW), "A")
                else:
                    xt = ctx.dma("gpsimd", xcb[:, :, 0:513],
                                 xview(sB, b, t0c - 1, CW + 1), "A")
                ctx.join("sync", ["V", "O"])
                xm = ctx.dma("sync", xf[:, :, :], xview(sB, b, t0c, CW), "A")
                ctx.wait("scalar", xt)
                ctx.wait("tensor", xt)
                ctx.join("tensor", ["V"])
                sqmm = {}
                for k in range(KD):
                    off = 1 if k < 4 else 0
                    if k >= 2:
                        ctx.wait("scalar", "T", sqmm[k - 2])
                    sq_ev = ctx.op("scalar", lambda e, k=k, off=off:
                                   e.activation(sq2[:, k % 2, :],
                                                xcb[:, k, off:off + 512],
                                                AF.Square), "S")
                    ctx.wait("tensor", "S", sq_ev)
                    ctx.op("tensor", lambda e, k=k, off=off: e.matmul(
                        psA[0:1, 0:512], com["ones_col"][:, 0:1],
                        xcb[:, k, off:off + 512],
                        start=(k == 0), stop=(k == KD - 1)))
                    t_done = ctx.op("tensor", lambda e, k=k: e.matmul(
                        psA[0:1, 512:1024],
                        com["ones_col"][:, 0:1], sq2[:, k % 2, :],
                        start=(k == 0), stop=(k == KD - 1)), "T")
                    sqmm[k] = t_done
                ctx.wait("vector", "T", t_done)
                nm, var = com["ln_nm"], com["ln_var"]
                msq, rstd = com["ln_t2"], com["ln_rstd"]
                ctx.op("vector", lambda e: e.tensor_scalar(
                    nm[:, 0:512], psA[0:1, 0:512], -1.0 / D, None, op0=ALU.mult))
                ctx.op("vector", lambda e: e.tensor_mul(msq[:, 0:512],
                                                        nm[:, 0:512], nm[:, 0:512]))
                v_done = ctx.op("vector", lambda e: e.scalar_tensor_tensor(
                    var[:, 0:512], psA[0:1, 512:1024], 1.0 / D, msq[:, 0:512],
                    op0=ALU.mult, op1=ALU.subtract), "V")
                ctx.wait("scalar", "V", v_done)
                epsc = ctx.fconst(EPS)
                ctx.op("scalar", lambda e: e.activation(
                    com["ln_t2"][:, 0:512], var[:, 0:512], AF.Ln,
                    bias=epsc[0:1, 0:1]))
                s2_ = ctx.op("scalar", lambda e: e.activation(
                    rstd[:, 0:512], com["ln_t2"][:, 0:512], AF.Exp, scale=-0.5), "S")
                ctx.wait("tensor", "S", s2_)
                ctx.join("tensor", ["V"])
                ctx.op("tensor", lambda e: e.matmul(
                    psA[:, 0:512], com["ones_row_f"][0:1, :], nm[0:1, 0:512],
                    start=True, stop=True))
                t2_ = ctx.op("tensor", lambda e: e.matmul(
                    psA[:, 512:1024], com["ones_row_f"][0:1, :],
                    rstd[0:1, 0:512], start=True, stop=True), "T")
                ctx.wait("vector", "T", t2_)
                hv = None
                for k in range(KD):
                    off = 1 if k < 4 else 0
                    if hv is not None:
                        ctx.wait("vector", "S", hv)
                    ctx.fop("vector", lambda e, k=k, off=off: e.tensor_tensor(
                        com["ln_tmp"][:, :], xcb[:, k, off:off + 512], psA[:, 0:512],
                        op=ALU.add))
                    ctx.fop("vector", lambda e, k=k, layer=layer: e.scalar_tensor_tensor(
                        lnt2b[:, :], com["ln_tmp"][:, :], lgc[:, layer, k:k + 1],
                        psA[:, 512:1024], op0=ALU.mult, op1=ALU.mult))
                    ctx.join("scalar", ["V"])
                    hv = ctx.op("scalar", lambda e, k=k, layer=layer: e.activation(
                        hlnb[:, k, :], lnt2b[:, :], AF.Identity,
                        bias=lbc[:, layer, k:k + 1]), "S")
                ctx.wait("tensor", "S", hv)
                hev = {}
                hq = None
                for tpair in range(22):
                    sl = tpair % 2
                    if tpair >= 2:
                        ctx.wait("gpsimd", "T", hev[tpair - 2])
                    wtok = ctx.dma("gpsimd", w1sl[:, sl, :, :],
                                   wd["w1p"].rearrange("l (k p) f -> l p k f", p=128)
                                   [layer, :, :, tpair * 256:(tpair + 1) * 256], "W")
                    ctx.wait("tensor", wtok)
                    if hq is not None:
                        ctx.wait("tensor", "V", hq)
                        ctx.wait("scalar", "V", hq)
                    for k in range(KD):
                        ctx.op("tensor", lambda e, sl=sl, k=k: e.matmul(
                            psB[:, :], w1sl[:, sl, k, 0:128], hlnb[:, k, :],
                            start=(k == 0), stop=(k == KD - 1)))
                    for k in range(KD):
                        mm = ctx.op("tensor", lambda e, sl=sl, k=k: e.matmul(
                            psC[:, :], w1sl[:, sl, k, 128:256], hlnb[:, k, :],
                            start=(k == 0), stop=(k == KD - 1)),
                            "T" if k == KD - 1 else None)
                    hev[tpair] = mm
                    ctx.wait("scalar", "T", mm)
                    ga = ctx.op("scalar", lambda e, tpair=tpair, layer=layer: e.activation(
                        gact[:, :], psC[:, :], AF.Gelu,
                        bias=b1c[:, layer, 2 * tpair + 1:2 * tpair + 2]), "S")
                    ctx.wait("vector", "S", ga)
                    hq = ctx.fop("vector", lambda e, tpair=tpair, layer=layer: e.scalar_tensor_tensor(
                        hidden[:, tpair, :], psB[:, :],
                        b1c[:, layer, 2 * tpair:2 * tpair + 1], gact[:, :],
                        op0=ALU.add, op1=ALU.mult))
                    ctx.wait("tensor", "V", hq)
                ctx.join("tensor", ["V"])
                for oft in range(KD):
                    slo = oft % 2
                    pso = psB if oft % 2 == 0 else psC
                    if oft >= 2:
                        ctx.wait("tensor", "V", oev2[oft - 2])
                        ctx.wait("gpsimd", "T", w2mm[oft - 2])
                    wtok = ctx.dma("gpsimd", w2sl[:, slo, :, :],
                                   wd["w2p"].rearrange("l (k p) f -> l p k f", p=128)
                                   [layer, :, :, oft * 128:(oft + 1) * 128], "W")
                    ctx.wait("tensor", wtok)
                    for k in range(22):
                        mm = ctx.op("tensor", lambda e, slo=slo, k=k, pso=pso:
                                    e.matmul(pso[:, :], w2sl[:, slo, k, :],
                                             hidden[:, k, :], start=(k == 0),
                                             stop=(k == 21)),
                                    "T" if k == 21 else None)
                    ctx.wait("vector", "T", mm)
                    if oft == 0:
                        oev2 = {}
                        w2mm = {}
                    w2mm[oft] = mm
                    oev2[oft] = ctx.fop("vector", lambda e, oft=oft, pso=pso, layer=layer:
                                        e.scalar_tensor_tensor(
                                            xf[:, oft, :], pso[:, :],
                                            b2c[:, layer, oft:oft + 1],
                                            xf[:, oft, :],
                                            op0=ALU.add, op1=ALU.add))
                if layer == 1:
                    ctx.join("gpsimd", ["V"])
                    ctx.wait("gpsimd", pc)
                    stream = "Og"
                    i = ctx.dma_ctr.get(stream, 0)
                    ctx.dma_ctr[stream] = i + 1
                    sub = f"{stream}{i % DMA_SLOTS}"
                    ctx.wait("gpsimd", sub, ctx.ctr.get(sub, 0))
                    ctx.op("gpsimd", lambda e, b=b, t0c=t0c: e.dma_start(
                        xview(xout, b, t0c, CW), xf[:, :, :],
                        accum_op=ALU.add), sub, 16)
                else:
                    ctx.join("sync", ["V"])
                    ctx.dma("sync", xview(sA, b, t0c, CW), xf[:, :, :], "O")
    return None


def _prep_inputs(inp):
    d = {}
    x = np.ascontiguousarray(np.transpose(inp["L0_attn_a_x"], (0, 2, 1)))
    d["x"] = x
    perm = ([h * 128 + j for h in range(16) for j in range(64)]
            + [h * 128 + 64 + j for h in range(16) for j in range(64)])
    d["l0aa.Wqk"] = np.ascontiguousarray(inp["L0_attn_a_Wqk"][:, perm])
    d["l0aa.Wv"] = inp["L0_attn_a_Wv"]
    d["l0aa.bqk_c"] = np.ascontiguousarray(
        inp["L0_attn_a_bqk"][perm].reshape(16, 128).T)
    d["l0aa.bv_r"] = inp["L0_attn_a_bv"].reshape(1, 1024)
    for k in ["W_dq", "W_uq", "W_dkv", "W_uk", "W_uv", "W_o"]:
        d[f"l1aa.{k}"] = inp[f"L1_attn_a_{k}"]
    d["l1moe.w12T"] = np.ascontiguousarray(
        inp["L1_ffn_w12"].transpose(0, 2, 1))
    d["l1moe.w3T"] = np.ascontiguousarray(
        inp["L1_ffn_w3"].transpose(0, 2, 1))
    d["l1moe.gate_wT"] = np.ascontiguousarray(inp["L1_ffn_gate_w"].T)
    d["l0ab.W_inT"] = np.ascontiguousarray(inp["L0_attn_b_W_in"].T)
    d["l0ab.W_xT"] = np.ascontiguousarray(inp["L0_attn_b_W_x"].T)
    d["l0ab.W_dtT"] = np.ascontiguousarray(inp["L0_attn_b_W_dt"].T)
    d["l0ab.W_outT"] = np.ascontiguousarray(inp["L0_attn_b_W_out"].T)
    d["l0ab.conv_w_c"] = np.ascontiguousarray(
        inp["L0_attn_b_conv_w"][:, 0, :].reshape(16, 128, 4).transpose(1, 0, 2))
    d["l0ab.conv_b_c"] = np.ascontiguousarray(
        inp["L0_attn_b_conv_b"].reshape(16, 128).T)
    d["l0ab.b_dt_c"] = np.ascontiguousarray(
        inp["L0_attn_b_b_dt"].reshape(16, 128).T)
    d["l0ab.A_log_c"] = np.ascontiguousarray(
        inp["L0_attn_b_A_log"].reshape(16, 128, 16).transpose(1, 0, 2))
    d["l0ab.D_c"] = np.ascontiguousarray(
        inp["L0_attn_b_D"].reshape(16, 128).T)
    d["l1ab.w1_wT"] = np.ascontiguousarray(inp["L1_attn_b_w1_w"].T)
    d["l1ab.v1_wT"] = np.ascontiguousarray(inp["L1_attn_b_v1_w"].T)
    d["l1ab.w2_wT"] = np.ascontiguousarray(inp["L1_attn_b_w2_w"].T)
    d["l1ab.w1_b_c"] = np.ascontiguousarray(
        inp["L1_attn_b_w1_b"].reshape(8, 128).T)
    d["l1ab.v1_b_c"] = np.ascontiguousarray(
        inp["L1_attn_b_v1_b"].reshape(8, 128).T)
    d["l1ab.w2_b_c"] = np.ascontiguousarray(
        inp["L1_attn_b_w2_b"].reshape(8, 128).T)
    d["l1ab.conv_w_c"] = np.ascontiguousarray(
        inp["L1_attn_b_conv_w"][:, 0, :].reshape(8, 128, 3).transpose(1, 0, 2))
    d["l1ab.A_c"] = np.ascontiguousarray(
        inp["L1_attn_b_A"].reshape(8, 128, 16).transpose(1, 0, 2))
    d["l1ab.B_c"] = np.ascontiguousarray(
        inp["L1_attn_b_B_ssm"].reshape(8, 128, 16).transpose(1, 0, 2))
    d["l1ab.C_c"] = np.ascontiguousarray(
        inp["L1_attn_b_C_ssm"].reshape(8, 128, 16).transpose(1, 0, 2))
    d["l1ab.D_c"] = np.ascontiguousarray(
        inp["L1_attn_b_D_ssm"].reshape(8, 128).T)
    d["l0ffn.wqkv"] = inp["L0_ffn_wqkv"]
    d["l0ffn.wo"] = inp["L0_ffn_wo"]
    qs = inp["L0_ffn_q_scale"]
    ks = inp["L0_ffn_k_scale"]
    d["l0ffn.qs_c"] = np.ascontiguousarray(
        np.concatenate([qs.T, qs.T], axis=0))
    d["l0ffn.ks_c"] = np.ascontiguousarray(
        np.concatenate([ks.T, ks.T], axis=0))
    d["l0ffn.ln_g_c"] = np.ascontiguousarray(
        inp["L0_ffn_ln_g"].reshape(2, 8, 128).transpose(2, 0, 1))
    d["l0ffn.ln_b_c"] = np.ascontiguousarray(
        inp["L0_ffn_ln_b"].reshape(2, 8, 128).transpose(2, 0, 1))
    w1 = inp["L0_ffn_w1"]
    b1 = inp["L0_ffn_b1"]
    HID = 2730
    w1p = np.zeros((2, 1024, 5632), dtype=np.float32)
    b1p = np.zeros((128, 2, 44), dtype=np.float32)
    for t in range(22):
        lo, hi = 128 * t, min(128 * (t + 1), HID)
        n = hi - lo
        w1p[:, :, 256 * t:256 * t + n] = w1[:, :, lo:hi]
        w1p[:, :, 256 * t + 128:256 * t + 128 + n] = w1[:, :, HID + lo:HID + hi]
        b1p[:n, :, 2 * t] = b1[:, lo:hi].T
        b1p[:n, :, 2 * t + 1] = b1[:, HID + lo:HID + hi].T
    d["l0ffn.w1p"] = w1p
    d["l0ffn.b1p_c"] = np.ascontiguousarray(b1p)
    d["l0ffn.w2p"] = np.ascontiguousarray(
        np.pad(inp["L0_ffn_w2"], ((0, 0), (0, 86), (0, 0))))
    d["l0ffn.b2_c"] = np.ascontiguousarray(
        inp["L0_ffn_b2"].reshape(2, 8, 128).transpose(2, 0, 1))
    return d


PASS_BUILDERS = {}


def _register():
    PASS_BUILDERS["l0aa"] = (pass_l0aa,
                             ["Wqk", "Wv", "bqk_c", "bv_r"])
    PASS_BUILDERS["l1aa"] = (pass_l1aa,
                             ["W_dq", "W_uq", "W_dkv", "W_uk", "W_uv", "W_o"])
    PASS_BUILDERS["l1moe"] = (pass_l1moe, ["w12T", "w3T", "gate_wT"])
    PASS_BUILDERS["l0ab"] = (pass_l0ab,
                             ["W_inT", "W_xT", "W_dtT", "W_outT", "conv_w_c",
                              "conv_b_c", "b_dt_c", "A_log_c", "D_c"])
    PASS_BUILDERS["l0ffn"] = (pass_l0ffn,
                              ["wqkv", "wo", "qs_c", "ks_c", "ln_g_c", "ln_b_c",
                               "w1p", "b1p_c", "w2p", "b2_c"])
    PASS_BUILDERS["l1ab"] = (pass_l1ab,
                             ["w1_wT", "v1_wT", "w2_wT", "w1_b_c", "v1_b_c",
                              "w2_b_c", "conv_w_c", "A_c", "B_c", "C_c", "D_c"])


_register()


def build_single_pass(pass_name, d):
    nc = bass.Bass("TRN2", target_bir_lowering=False, debug=False)
    xin = nc.dram_tensor("x_in", [NB, D, T], F32, kind="ExternalInput").ap()
    xout = nc.dram_tensor("x_out", [NB, D, T], F32, kind="ExternalOutput").ap()
    ctx = Ctx(nc)
    com = load_common(ctx)
    fn, wkeys = PASS_BUILDERS[pass_name]
    wd = {}
    for k in wkeys:
        arr = d[f"{pass_name}.{k}"]
        wd[k] = nc.dram_tensor(f"{pass_name}_{k}", list(arr.shape), F32,
                               kind="ExternalInput").ap()
    fn(ctx, com, xin, xout, wd)
    for s in list(ctx.dma_ctr):
        for i in range(DMA_SLOTS):
            ctx.wait("sync", f"{s}{i}", ctx.ctr.get(f"{s}{i}", 0))
    ctx.finalize()
    return nc


def _in_map_for(pass_name, d, xcur):
    m = {"x_in": xcur}
    pref = pass_name + "."
    for k, v in d.items():
        if k.startswith(pref):
            m[pass_name + "_" + k[len(pref):]] = np.ascontiguousarray(
                v, dtype=np.float32)
    return m


PASS_ORDER = ["l0aa", "l0ffn", "l0ab", "l0ffn", "l1aa", "l1moe", "l1ab",
              "l1moe"]


def build_fused(d):
    import contextlib
    nc = bass.Bass("TRN2", target_bir_lowering=False, debug=False)
    xin = nc.dram_tensor("x_in", [NB, D, T], F32, kind="ExternalInput").ap()
    xout = nc.dram_tensor("x_out", [NB, D, T], F32, kind="ExternalOutput").ap()
    s0 = nc.dram_tensor("chain0", [NB, D, T], F32, kind="Internal").ap()
    s1 = nc.dram_tensor("chain1", [NB, D, T], F32, kind="Internal").ap()
    ctx = Ctx(nc)
    com = load_common(ctx)
    wds = {}
    for pname in set(PASS_ORDER):
        fn, wkeys = PASS_BUILDERS[pname]
        wd = {}
        for k in wkeys:
            arr = d[f"{pname}.{k}"]
            wd[k] = nc.dram_tensor(f"{pname}_{k}", list(arr.shape), F32,
                                   kind="ExternalInput").ap()
        wds[pname] = wd
    n = len(PASS_ORDER)
    for i, pname in enumerate(PASS_ORDER):
        src_ap = xin if i == 0 else (s0 if i % 2 == 1 else s1)
        dst_ap = xout if i == n - 1 else (s0 if i % 2 == 0 else s1)
        fn, _ = PASS_BUILDERS[pname]
        ctx._es = contextlib.ExitStack()
        with ctx._es:
            fn(ctx, com, src_ap, dst_ap, wds[pname])
            ctx.full_barrier()
        ctx._es = None
    for s in list(ctx.dma_ctr):
        for j in range(DMA_SLOTS):
            ctx.wait("sync", f"{s}{j}", ctx.ctr.get(f"{s}{j}", 0))
    ctx.finalize()
    return nc


def _in_map_fused(d, xcur):
    m = {"x_in": xcur}
    for pname in set(PASS_ORDER):
        _, wkeys = PASS_BUILDERS[pname]
        for k in wkeys:
            m[f"{pname}_{k}"] = np.ascontiguousarray(d[f"{pname}.{k}"],
                                                     dtype=np.float32)
    return m


def kernel(**inputs):
    d = _prep_inputs(inputs)
    x = d["x"]
    shards = [np.ascontiguousarray(x[2 * i:2 * i + 2]) for i in range(8)]
    nc = build_fused(d)
    in_maps = [_in_map_fused(d, s) for s in shards]
    res = run_bass_kernel_spmd(nc, in_maps, core_ids=list(range(8)))
    shards = [np.asarray(r["x_out"]) for r in res.results]
    y = np.concatenate(shards, axis=0)
    return np.ascontiguousarray(np.transpose(y, (0, 2, 1)))
